# version 15
# speedup vs baseline: 1.2828x; 1.0088x over previous
import contextlib
import numpy as np
import ml_dtypes
import concourse.bass as bass
import concourse.mybir as mybir
from concourse.alu_op_type import AluOpType as OP
from concourse.bass_utils import run_bass_kernel_spmd

F32 = mybir.dt.float32
BF16 = mybir.dt.bfloat16
AF = mybir.ActivationFunctionType
BF = ml_dtypes.bfloat16

L = 2048
D = 1024
DI = 2048
DS = 128
NH = 32
HP = 64
CD = DI + 2 * DS
NCT = 18
DIP = 4384
CH = 128
SLAB = 256
NSLAB = L // SLAB
HEXP = 2048
NEXP = 8
FH = 4096
NEGBIG = -30000.0


class Sem:
    __slots__ = ("h", "n")

    def __init__(self, h):
        self.h = h
        self.n = 0


class Prog:
    def __init__(self):
        self.q = {k: [] for k in ("tensor", "vector", "scalar", "gpsimd", "sync")}

    def add(self, eng, fn):
        self.q[eng].append(fn)

    def flush(self, block):
        for eng, fns in self.q.items():
            if not fns:
                continue

            def run(h, fns=tuple(fns)):
                for fn in fns:
                    fn(h)
            getattr(block, eng)(run)


class KB:

    def __init__(self, nc, ctx, P, di, dout, tap_d):
        self.nc, self.ctx, self.P, self.di = nc, ctx, P, di
        self.dout, self.tap_d = dout, tap_d
        sb = lambda n, s, d: ctx.enter_context(nc.sbuf_tensor(n, s, d))
        self.sb = sb
        self.XN = sb("XN", [128, 8 * L], BF16)
        self.SM = sb("SM", [128, 8 * L], BF16)
        self.cid16 = sb("cid16", [128, 128], BF16)
        self.cidf = sb("cidf", [128, 128], F32)
        self.cl1 = sb("cl1", [128, 128], BF16)
        self.csub4 = sb("csub4", [128, 512], BF16)
        self.co1f = sb("co1f", [1, 128], F32)
        self.co1b = sb("co1b", [1, 128], BF16)
        self.cor512 = sb("cor512", [1, 512], F32)
        self.cocol = sb("cocol", [128, 1], BF16)
        self.ccvw = sb("ccvw", [128, NCT * 4], F32)
        self.ccvb = sb("ccvb", [128, NCT], F32)
        self.cdtb = sb("cdtb", [1, NH], F32)
        self.calog = sb("calog", [NH, 1], F32)
        self.cdvec = sb("cdvec", [1, NH], F32)
        self.cnormw = sb("cnormw", [128, 16], F32)
        self.clnw = sb("clnw", [128, 8], F32)
        self.clnb = sb("clnb", [128, 8], F32)
        self.cbprj = sb("cbprj", [128, 8], F32)
        self.cacol = sb("cacol", [NH, 1], F32)
        self.cdbc = sb("cdbc", [128, NH], BF16)
        self.bscr = sb("bscr", [1, 8], F32)
        self.spc = sb("spc", [128, 4], F32)
        self.ceps = sb("ceps", [1, 1], F32)
        self.Q = [ctx.enter_context(nc.psum_tensor("q%d" % i, [128, 1024], F32))
                  for i in range(4)]
        self.dG = Sem(ctx.enter_context(nc.semaphore("dG")))
        self.dX = [Sem(ctx.enter_context(nc.semaphore("dX%d" % i)))
                   for i in range(4)]
        self.gC = Sem(ctx.enter_context(nc.semaphore("gC")))
        self.dS = Sem(ctx.enter_context(nc.semaphore("dS")))
        self.sT = Sem(ctx.enter_context(nc.semaphore("sT")))
        self.sV = Sem(ctx.enter_context(nc.semaphore("sV")))
        self.sS = Sem(ctx.enter_context(nc.semaphore("sS")))

    def t(self, fn):
        self.P.add("tensor", fn)

    def v(self, fn):
        self.P.add("vector", fn)

    def s(self, fn):
        self.P.add("scalar", fn)

    def g(self, fn):
        self.P.add("gpsimd", fn)

    def y(self, fn):
        self.P.add("sync", fn)

    def barrier(self):
        sT, sV, sS = self.sT, self.sV, self.sS
        t0, v0, s0 = sT.n, sV.n, sS.n

        def bt(h, v0=v0, s0=s0):
            h.wait_ge(sV.h, v0)
            h.wait_ge(sS.h, s0)
            h.nop().then_inc(sT.h, 1)

        def bv(h, t0=t0, s0=s0, b=self.bscr):
            h.wait_ge(sT.h, t0)
            h.wait_ge(sS.h, s0)
            h.memset(b[0:1, 0:2], 0.0).then_inc(sV.h, 1)

        def bs(h, t0=t0, v0=v0, b=self.bscr):
            h.wait_ge(sT.h, t0)
            h.wait_ge(sV.h, v0)
            h.memzero(b[0:1, 2:4]).then_inc(sS.h, 1)
        self.t(bt)
        self.v(bv)
        self.s(bs)
        sT.n += 1
        sV.n += 1
        sS.n += 1
        tb, vb, ssn = sT.n, sV.n, sS.n

        def bg(h, tb=tb, vb=vb, ssn=ssn):
            h.wait_ge(sT.h, tb)
            h.wait_ge(sV.h, vb)
            h.wait_ge(sS.h, ssn)
        self.g(bg)
        self.y(bg)

    def tap_raw(self, name, src_ap):
        if name not in self.tap_d:
            return
        dG = self.dG
        v0, s0, t0 = self.sV.n, self.sS.n, self.sT.n
        dst = self.tap_d[name]

        def go(h, v0=v0, s0=s0, t0=t0, dst=dst, src=src_ap, sV=self.sV,
               sS=self.sS, sT=self.sT, dG=dG):
            h.wait_ge(sV.h, v0)
            h.wait_ge(sS.h, s0)
            h.wait_ge(sT.h, t0)
            h.dma_start(out=dst, in_=src).then_inc(dG.h, 16)
        self.g(go)
        dG.n += 16
        fin = dG.n

        def wt(h, fin=fin, dG=dG):
            h.wait_ge(dG.h, fin)
        self.g(wt)
        self.t(wt)
        self.v(wt)
        self.s(wt)

    def tap_fm(self, name, src):
        if name not in self.tap_d:
            return
        dG = self.dG
        v0, s0 = self.sV.n, self.sS.n
        dst = self.tap_d[name]

        def go(h, v0=v0, s0=s0, dst=dst, src=src, sV=self.sV, sS=self.sS, dG=dG):
            h.wait_ge(sV.h, v0)
            h.wait_ge(sS.h, s0)
            for k in range(8):
                h.dma_start(out=dst[k * 128:(k + 1) * 128, :],
                            in_=src[:, k * L:(k + 1) * L]).then_inc(dG.h, 16)
        self.g(go)
        dG.n += 128
        fin = dG.n

        def wt(h, fin=fin, dG=dG):
            h.wait_ge(dG.h, fin)
        self.g(wt)



def emit_consts(kb):
    di, dS, sT, sV, sS = kb.di, kb.dS, kb.sT, kb.sV, kb.sS
    cl = [("id16", kb.cid16), ("idf", kb.cidf), ("l1tri", kb.cl1),
          ("sub4", kb.csub4), ("ones1f", kb.co1f), ("ones1b", kb.co1b),
          ("onesr512", kb.cor512), ("onescol", kb.cocol), ("convw", kb.ccvw),
          ("convb", kb.ccvb), ("dtb", kb.cdtb), ("alog", kb.calog),
          ("dvec", kb.cdvec), ("normw", kb.cnormw), ("lnw0", kb.clnw),
          ("lnb0", kb.clnb), ("bprj", kb.cbprj)]

    def p0(h, cl=tuple(cl)):
        for nm, dst in cl:
            h.dma_start(out=dst[:], in_=di[nm]).then_inc(dS.h, 16)
    kb.y(p0)

    dS.n += 16 * len(cl)
    ck_cl = dS.n

    def p0s(h, c=ck_cl, kb=kb):
        h.wait_ge(dS.h, c)
        h.activation(kb.cacol[:], kb.calog[:], AF.Exp).then_inc(sS.h, 1)
    kb.s(p0s)
    sS.n += 1
    ck_ac = sS.n

    def p0v(h, c=ck_ac, kb=kb):
        h.wait_ge(sS.h, c)
        h.tensor_scalar_mul(kb.cacol[:], kb.cacol[:], -1.0).then_inc(sV.h, 1)
    kb.v(p0v)
    sV.n += 1

    def p0t(h, c=ck_cl, kb=kb):
        h.wait_ge(dS.h, c)
        h.matmul(kb.Q[0][:, 0:NH], kb.co1f[:], kb.cdvec[:],
                 start=True, stop=True).then_inc(sT.h, 1)
    kb.t(p0t)
    sT.n += 1
    ck_db = sT.n

    def p0s2(h, c=ck_db, kb=kb):
        h.wait_ge(sT.h, c)
        h.copy(kb.cdbc[:], kb.Q[0][:, 0:NH]).then_inc(sS.h, 1)
    kb.s(p0s2)
    sS.n += 1
    kb.barrier()


_LN_N = [0]


def emit_ln_fm(kb, lnc, dst, src_dram=None, src_sbuf=None):
    nc, dG, dS, sT, sV, sS = kb.nc, kb.dG, kb.dS, kb.sT, kb.sV, kb.sS
    Q = kb.Q
    _LN_N[0] += 1
    sfx = str(_LN_N[0])
    sbl = lambda n, s, d: lnc.enter_context(nc.sbuf_tensor(n + sfx, s, d))
    xsq = sbl("ln_xsq", [128, L], BF16)
    srow = sbl("ln_srow", [1, 3 * L], F32)
    mrb = sbl("ln_mrb", [128, 2 * L], BF16)
    xld = sbl("ln_xld", [128, 2 * L], BF16) if src_dram is not None else None

    def tile_src(k):
        if src_dram is not None:
            return xld[:, (k % 2) * L:(k % 2) * L + L]
        return src_sbuf[:, k * L:(k + 1) * L]

    def emit_load(k, gate_sem, gate_val):
        if src_dram is None:
            return None
        dx = kb.dX[k % 2]

        def ld(h, k=k, gs=gate_sem, gv=gate_val, dx=dx):
            if gs is not None:
                h.wait_ge(gs.h, gv)
            h.dma_start(out=xld[:, (k % 2) * L:(k % 2) * L + L],
                        in_=src_dram[k * 128:(k + 1) * 128, :]).then_inc(dx.h, 16)
        kb.g(ld)
        dx.n += 16
        return (dx, dx.n)

    cons_ck = []
    for k in range(8):
        gate = (sT, cons_ck[k - 2]) if (src_dram is not None and k >= 2) else (None, 0)
        dld = emit_load(k, gate[0], gate[1])
        src = tile_src(k)

        prevmm = cons_ck[k - 1] if k >= 1 else 0

        def s1(h, src=src, dld=dld, pm=prevmm):
            if dld is not None:
                h.wait_ge(dld[0].h, dld[1])
            if pm:
                h.wait_ge(sT.h, pm)
            h.activation(xsq[:], src, AF.Square).then_inc(sS.h, 1)
        kb.s(s1)
        sS.n += 1
        ck_sq = sS.n

        def t1(h, k=k, src=src, dld=dld, c=ck_sq):
            if dld is not None:
                h.wait_ge(dld[0].h, dld[1])
            h.wait_ge(sS.h, c)
            for tc in range(4):
                qa = Q[tc // 2]
                qb = Q[2 + tc // 2]
                off = (tc % 2) * 512
                h.matmul(qa[0:1, off:off + 512], kb.cocol[:],
                         src[:, tc * 512:(tc + 1) * 512],
                         start=(k == 0), stop=(k == 7))
                mm = h.matmul(qb[0:1, off:off + 512], kb.cocol[:],
                              xsq[:, tc * 512:(tc + 1) * 512],
                              start=(k == 0), stop=(k == 7))
            mm.then_inc(sT.h, 1)
        kb.t(t1)
        sT.n += 1
        cons_ck.append(sT.n)
    ck_st = sT.n

    def v1(h, c=ck_st):
        h.wait_ge(sT.h, c)
        h.tensor_scalar_mul(srow[0:1, 0:1024], Q[0][0:1, :], 1.0 / D)
        h.tensor_scalar_mul(srow[0:1, 1024:L], Q[1][0:1, :], 1.0 / D)
        h.tensor_scalar_mul(srow[0:1, L:L + 1024], Q[2][0:1, :], 1.0 / D)
        h.tensor_scalar_mul(srow[0:1, L + 1024:2 * L], Q[3][0:1, :], 1.0 / D)
        h.tensor_tensor(srow[0:1, 2 * L:3 * L], srow[0:1, 0:L], srow[0:1, 0:L],
                        OP.mult)
        h.tensor_tensor(srow[0:1, L:2 * L], srow[0:1, L:2 * L],
                        srow[0:1, 2 * L:3 * L], OP.subtract)
        h.tensor_scalar_add(srow[0:1, L:2 * L], srow[0:1, L:2 * L],
                            1e-5).then_inc(sV.h, 1)
    kb.v(v1)
    sV.n += 1
    ck_v = sV.n

    def s2(h, c=ck_v):
        h.wait_ge(sV.h, c)
        h.activation(srow[0:1, 2 * L:3 * L], srow[0:1, L:2 * L], AF.Sqrt).then_inc(sS.h, 1)
    kb.s(s2)
    sS.n += 1
    ck_sd = sS.n

    def v2(h, c=ck_sd):
        h.wait_ge(sS.h, c)
        h.reciprocal(srow[0:1, L:2 * L], srow[0:1, 2 * L:3 * L]).then_inc(sV.h, 1)
    kb.v(v2)
    sV.n += 1
    ck_r = sV.n

    def t2(h, c=ck_r):
        h.wait_ge(sV.h, c)
        for tc in range(4):
            qm = Q[tc // 2]
            qr = Q[2 + tc // 2]
            off = (tc % 2) * 512
            h.matmul(qm[:, off:off + 512], kb.co1f[:],
                     srow[0:1, tc * 512:(tc + 1) * 512], start=True, stop=True)
            mm = h.matmul(qr[:, off:off + 512], kb.co1f[:],
                          srow[0:1, L + tc * 512:L + (tc + 1) * 512], start=True,
                          stop=True)
        mm.then_inc(sT.h, 1)
    kb.t(t2)
    sT.n += 1
    ck_bc = sT.n

    def s3(h, c=ck_bc):
        h.wait_ge(sT.h, c)
        h.copy(mrb[:, 0:1024], Q[0][:, :])
        h.copy(mrb[:, 1024:L], Q[1][:, :])
        h.copy(mrb[:, L:L + 1024], Q[2][:, :])
        h.copy(mrb[:, L + 1024:2 * L], Q[3][:, :]).then_inc(sS.h, 1)
    kb.s(s3)
    sS.n += 1
    ck_m = sS.n

    app_ck = []
    for k in range(8):
        gate = (sV, app_ck[k - 2]) if (src_dram is not None and k >= 2) else \
               ((sS, ck_m) if src_dram is not None else (None, 0))
        dld = emit_load(k, gate[0], gate[1])
        src = tile_src(k)

        def v3(h, k=k, src=src, dld=dld, c=ck_m):
            h.wait_ge(sS.h, c)
            if dld is not None:
                h.wait_ge(dld[0].h, dld[1])
            h.tensor_tensor(xsq[:], src, mrb[:, 0:L], OP.subtract)
            h.tensor_tensor(dst[:, k * L:(k + 1) * L], xsq[:],
                            mrb[:, L:2 * L], OP.mult).then_inc(sV.h, 1)
        kb.v(v3)
        sV.n += 1
        app_ck.append(sV.n)


def emit_mamba(kb, mc):
    nc, di = kb.nc, kb.di
    dG, dS, sT, sV, sS = kb.dG, kb.dS, kb.sT, kb.sV, kb.sS
    Q0, Q1, Q2, Q3 = kb.Q
    XN, SM = kb.XN, kb.SM
    sbl = lambda n, s, d: mc.enter_context(nc.sbuf_tensor(n, s, d))
    WIN = sbl("WIN", [128, 8 * DIP], BF16)
    ZB = sbl("ZB", [128, DI], BF16)
    XBC = sbl("XBC", [128, NCT * (SLAB + 3)], BF16)
    CVO = sbl("CVO", [128, NCT * SLAB], BF16)
    XT = sbl("XT", [128, DI], BF16)
    BTK = sbl("BTK", [128, 128], BF16)
    SCR = sbl("SCR", [128, 512], BF16)
    YT = sbl("YT", [128, DI], BF16)
    YNT = sbl("YNT", [128, 16 * 512], BF16)
    H16 = sbl("H16", [128, DI], BF16)
    E8 = sbl("E8", [128, 1024], BF16)
    SPM = sbl("SPM", [128, 128], BF16)
    DTF = sbl("DTF", [NH, SLAB], F32)
    DTA = sbl("DTA", [NH, CH], F32)
    ZRO = sbl("ZRO", [NH, CH], F32)
    CSF = sbl("CSF", [NH, CH], F32)
    CFL = sbl("CFL", [1, 8 * CH], F32)
    TKS = sbl("TKS", [128, 8 * NH], F32)
    V16 = sbl("V16", [128, NH], BF16)
    SPRB = sbl("SPRB", [1, 512], F32)
    SQT = sbl("SQT", [128, 512], BF16)
    SQ2 = sbl("SQ2", [128, 512], BF16)
    ROWA = sbl("ROWA", [128, 1], F32)
    ROWB = sbl("ROWB", [128, 4], F32)
    Q1b = Q1[:, :].bitcast(BF16)
    Q2b = Q2[:, :].bitcast(BF16)
    cs_tok = TKS[:, 0 * NH:1 * NH]
    dt_tok = TKS[:, 1 * NH:2 * NH]
    logdt = TKS[:, 2 * NH:3 * NH]
    biasc = TKS[:, 3 * NH:4 * NH]
    rcs_t = TKS[:, 4 * NH:5 * NH]
    expcs = TKS[:, 5 * NH:6 * NH]
    gamt = TKS[:, 6 * NH:7 * NH]
    spare = TKS[:, 7 * NH:8 * NH]

    v0, s0 = sV.n, sS.n

    def pre_g(h, v0=v0, s0=s0):
        h.wait_ge(sV.h, v0)
        h.wait_ge(sS.h, s0)
        h.memset(ZRO[:], 0.0)
        h.memset(H16[:], 0.0)
        for e in range(NCT):
            h.memset(XBC[:, e * (SLAB + 3):e * (SLAB + 3) + 3], 0.0)
        h.memset(XBC[:, 0:1], 0.0).then_inc(kb.gC.h, 1)
        for k in range(8):
            h.dma_start(out=WIN[:, k * DIP:(k + 1) * DIP],
                        in_=di["win"][k * 128:(k + 1) * 128, :]).then_inc(dG.h, 16)
    kb.g(pre_g)
    dG.n += 16 * 8
    kb.gC.n += 1
    ck_win = dG.n
    ck_mset = kb.gC.n

    q0_ck = [(sS, sS.n), (sS, sS.n)]
    st = dict(xbc_evac=[0] * NCT, tail=sV.n, dtf=sV.n, cvo=[0] * NCT,
              lastFl=0, lastBcast=0, lastSpm=0, scrGate=(sT, 0))

    def emitA(slab):
        t0 = slab * SLAB
        for e in range(NCT):
            slot = e % 2
            gate = q0_ck[slot]
            ckw = ck_win
            first = (slab == 0 and e < 2)
            pv = st["tail"]

            def ta(h, e=e, slot=slot, gate=gate, ckw=ckw, t0=t0, first=first):
                if first:
                    h.wait_ge(dG.h, ckw)
                h.wait_ge(gate[0].h, gate[1])
                for k in range(8):
                    mm = h.matmul(Q0[:, slot * 512:slot * 512 + SLAB],
                                  WIN[:, k * DIP + DI + e * 128:k * DIP + DI + (e + 1) * 128],
                                  XN[:, k * L + t0:k * L + t0 + SLAB],
                                  start=(k == 0), stop=(k == 7))
                mm.then_inc(sT.h, 1)
            kb.t(ta)
            sT.n += 1
            ckt = sT.n

            def sa(h, e=e, slot=slot, ckt=ckt, pv=pv):
                h.wait_ge(sT.h, ckt)
                h.wait_ge(sV.h, pv)
                h.copy(XBC[:, e * (SLAB + 3) + 3:(e + 1) * (SLAB + 3)],
                       Q0[:, slot * 512:slot * 512 + SLAB]).then_inc(sS.h, 1)
            kb.s(sa)
            sS.n += 1
            q0_ck[slot] = (sS, sS.n)
            st["xbc_evac"][e] = sS.n

    emitA(0)
    for sc2 in range(4):
        for sl in range(2):
            slab = 2 * sc2 + sl
            t0 = slab * SLAB
            def tdt(h, t0=t0, pv=sV.n, ps=sS.n):
                h.wait_ge(sV.h, pv)
                h.wait_ge(sS.h, ps)
                for k in range(8):
                    h.matmul(Q1[0:NH, 224:224 + SLAB],
                             WIN[:, k * DIP + DI + CD:k * DIP + DIP],
                             XN[:, k * L + t0:k * L + t0 + SLAB],
                             start=(k == 0), stop=False)
                h.matmul(Q1[0:NH, 224:224 + SLAB], kb.cdtb[:],
                         kb.cor512[0:1, 0:SLAB], start=False,
                         stop=True).then_inc(sT.h, 1)
            kb.t(tdt)
            sT.n += 1
            ck_dtr = sT.n

            def vsp(h, c=ck_dtr):
                h.wait_ge(sT.h, c)
                h.tensor_scalar(DTF[:, :], Q1[0:NH, 224:224 + SLAB], 0.0, None,
                                OP.max).then_inc(sV.h, 1)
            kb.v(vsp)
            sV.n += 1
            ck_sp1 = sV.n

            def ssp(h, c=ck_sp1):
                h.wait_ge(sV.h, c)
                scr = SQT[:, :].bitcast(F32)
                h.activation(scr[0:NH, 0:SLAB], Q1[0:NH, 224:224 + SLAB], AF.Abs)
                h.activation(scr[NH:2 * NH, 0:SLAB], scr[0:NH, 0:SLAB], AF.Exp,
                             bias=0.0, scale=-1.0)
                h.activation(scr[0:NH, 0:SLAB], scr[NH:2 * NH, 0:SLAB], AF.Ln,
                             bias=1.0, scale=1.0).then_inc(sS.h, 1)
            kb.s(ssp)
            sS.n += 1
            ck_sp2 = sS.n

            def vsp2(h, c=ck_sp2):
                h.wait_ge(sS.h, c)
                scr = SQT[:, :].bitcast(F32)
                h.tensor_tensor(DTF[:, :], DTF[:, :], scr[0:NH, 0:SLAB],
                                OP.add).then_inc(sV.h, 1)
            kb.v(vsp2)
            sV.n += 1
            st["dtf"] = sV.n

            prev_silu = 0
            for e in range(NCT):
                ck_e = st["xbc_evac"][e]

                def vc(h, e=e, c=ck_e, pcv=prev_silu, cm=ck_mset):
                    h.wait_ge(sS.h, c)
                    h.wait_ge(kb.gC.h, cm)
                    if pcv:
                        h.wait_ge(sS.h, pcv)
                    if pcv:
                        h.wait_ge(sS.h, pcv)
                    o = e * (SLAB + 3)
                    h.tensor_scalar(SQ2[:, 0:SLAB], XBC[:, o:o + SLAB],
                                    kb.ccvw[:, e * 4:e * 4 + 1], None, OP.mult)
                    for kk in range(1, 4):
                        mm = h.scalar_tensor_tensor(SQ2[:, 0:SLAB],
                                                    XBC[:, o + kk:o + kk + SLAB],
                                                    kb.ccvw[:, e * 4 + kk:e * 4 + kk + 1],
                                                    SQ2[:, 0:SLAB], OP.mult, OP.add)
                    mm.then_inc(sV.h, 1)
                kb.v(vc)
                sV.n += 1
                ck_cv = sV.n

                def sc_(h, e=e, c=ck_cv):
                    h.wait_ge(sV.h, c)
                    h.activation(CVO[:, e * SLAB:(e + 1) * SLAB], SQ2[:, 0:SLAB],
                                 AF.Silu, bias=kb.ccvb[:, e:e + 1],
                                 scale=1.0).then_inc(sS.h, 1)
                kb.s(sc_)
                sS.n += 1
                st["cvo"][e] = sS.n
                prev_silu = sS.n
            ck_cvo_all = sS.n

            def vt(h, c=ck_cvo_all):
                h.wait_ge(sS.h, c)
                pch = NCT * (SLAB + 3)
                h.tensor_copy(bass.AP(XBC, 0, [[pch, 128], [SLAB + 3, NCT], [1, 3]]),
                              bass.AP(XBC, SLAB, [[pch, 128], [SLAB + 3, NCT], [1, 3]])
                              ).then_inc(sV.h, 1)
            kb.v(vt)
            sV.n += 1
            st["tail"] = sV.n

            if sl == 0:
                emitA(slab + 1)

            if slab == NSLAB - 1:
                kb.tap_raw("cvo", CVO[:])
                kb.tap_raw("dtf", DTF[:])

            for cc in range(SLAB // CH):
                ch = slab * 2 + cc
                chl = ch % 4
                ct0 = t0 + cc * CH
                cvo_off = cc * CH

                zb_gate = st.get("zbGate", 0)
                for zf in range(4):
                    slot = zf % 2
                    gate = q0_ck[slot]

                    def tz(h, zf=zf, slot=slot, gate=gate, ct0=ct0):
                        h.wait_ge(gate[0].h, gate[1])
                        for k in range(8):
                            mm = h.matmul(Q0[:, slot * 512:(slot + 1) * 512],
                                          XN[:, k * L + ct0:k * L + ct0 + CH],
                                          WIN[:, k * DIP + zf * 512:k * DIP + (zf + 1) * 512],
                                          start=(k == 0), stop=(k == 7))
                        mm.then_inc(sT.h, 1)
                    kb.t(tz)
                    sT.n += 1
                    ckz = sT.n

                    def sz(h, zf=zf, slot=slot, c=ckz, zg=zb_gate):
                        h.wait_ge(sT.h, c)
                        if zg:
                            h.wait_ge(sV.h, zg)
                        h.activation(ZB[:, zf * 512:(zf + 1) * 512],
                                     Q0[:, slot * 512:(slot + 1) * 512],
                                     AF.Silu).then_inc(sS.h, 1)
                    kb.s(sz)
                    sS.n += 1
                    q0_ck[slot] = (sS, sS.n)
                ck_zb = sS.n

                def vda(h, cc=cc, cdtf=st["dtf"], cfl=st["lastFl"], cm=ck_mset):
                    h.wait_ge(kb.gC.h, cm)
                    h.wait_ge(sV.h, cdtf)
                    if cfl:
                        h.wait_ge(dS.h, cfl)
                    h.tensor_scalar(DTA[:], DTF[:, cc * CH:cc * CH + CH],
                                    kb.cacol[:], None, OP.mult)
                    h.tensor_tensor_scan(CSF[:], DTA[:], ZRO[:], 0.0, OP.add,
                                         OP.add).then_inc(sV.h, 1)
                kb.v(vda)
                sV.n += 1
                ck_csf = sV.n

                def tcs(h, cc=cc, c=ck_csf, csp=ck_sp2, pv=sV.n, ps=sS.n):
                    h.wait_ge(sV.h, c)
                    h.wait_ge(sS.h, csp)
                    h.wait_ge(sV.h, pv)
                    h.wait_ge(sS.h, ps)
                    h.transpose(Q1[:, 128:160], CSF[:], kb.cidf[0:NH, 0:NH])
                    h.transpose(Q1[:, 160:192], DTF[:, cc * CH:cc * CH + CH],
                                kb.cidf[0:NH, 0:NH]).then_inc(sT.h, 1)
                kb.t(tcs)
                sT.n += 1
                ck_trc = sT.n

                def stok(h, c=ck_trc):
                    h.wait_ge(sT.h, c)
                    h.copy(cs_tok, Q1[:, 128:160])
                    h.copy(dt_tok, Q1[:, 160:192])
                    h.activation(logdt, Q1[:, 160:192], AF.Ln).then_inc(sS.h, 1)
                kb.s(stok)
                sS.n += 1
                ck_tok = sS.n

                def vbias(h, c=ck_tok):
                    h.wait_ge(sS.h, c)
                    h.tensor_tensor(biasc, logdt, cs_tok,
                                    OP.subtract).then_inc(sV.h, 1)
                kb.v(vbias)
                sV.n += 1
                ck_bias = sV.n

                def tbs(h, cvo_off=cvo_off, c=ck_cvo_all, ct=ck_tok,
                        pv=sV.n, ps=sS.n):
                    h.wait_ge(sS.h, c)
                    h.wait_ge(sS.h, ct)
                    h.wait_ge(sV.h, pv)
                    h.wait_ge(sS.h, ps)
                    h.transpose(Q1b[:, 1024:1152],
                                CVO[:, 16 * SLAB + cvo_off:16 * SLAB + cvo_off + CH],
                                kb.cid16[:, :])
                    h.matmul(Q1[:, 0:128],
                             CVO[:, 16 * SLAB + cvo_off:16 * SLAB + cvo_off + CH],
                             CVO[:, 17 * SLAB + cvo_off:17 * SLAB + cvo_off + CH],
                             start=True, stop=True).then_inc(sT.h, 1)
                kb.t(tbs)
                sT.n += 1
                ck_bs = sT.n

                def sbt(h, c=ck_bs, cp=st["scrGate"][1]):
                    h.wait_ge(sT.h, c)
                    if cp:
                        h.wait_ge(sT.h, cp)
                    h.copy(BTK[:], Q1b[:, 1024:1152]).then_inc(sS.h, 1)
                kb.s(sbt)
                sS.n += 1
                ck_btk = sS.n

                def vspm(h, c=ck_bs):
                    h.wait_ge(sT.h, c)
                    h.tensor_tensor(SPM[:], Q1[:, 0:128], kb.cl1[:],
                                    OP.mult).then_inc(sV.h, 1)
                kb.v(vspm)
                sV.n += 1
                ck_spm = sV.n
                st["lastSpm"] = ck_spm

                def txs(h, cvo_off=cvo_off, cR=st.get("lastR", 0), cE=st.get("lastE", 0)):
                    h.wait_ge(sS.h, cE)
                    for e in range(16):
                        mm = h.transpose(Q2b[:, e * 128:(e + 1) * 128],
                                         CVO[:, e * SLAB + cvo_off:e * SLAB + cvo_off + CH],
                                         kb.cid16[:, :])
                    mm.then_inc(sT.h, 1)
                kb.t(txs)
                sT.n += 1
                ck_txs = sT.n

                def sxt(h, c=ck_txs):
                    h.wait_ge(sT.h, c)
                    for j in range(4):
                        mm = h.copy(XT[:, j * 512:(j + 1) * 512],
                                    Q2b[:, j * 512:(j + 1) * 512])
                    mm.then_inc(sS.h, 1)
                kb.s(sxt)
                sS.n += 1
                ck_xt = sS.n

                yevac_ck = st.get("lastY", 0)
                hupd_ck = st.get("lastH", 0)
                e8_gate = st.get("lastIntra", 0)
                r_ck = [ck_xt, ck_xt]
                fl_gate = st["lastBcast"]
                ck_fl = 0

                for gg in range(4):
                    def yfl(h, gg=gg, c=ck_csf, cb=fl_gate):
                        h.wait_ge(sV.h, c)
                        if cb:
                            h.wait_ge(sT.h, cb)
                        h.dma_start(out=bass.AP(CFL, 0, [[8 * CH, 1], [CH, 8], [1, CH]]),
                                    in_=CSF[gg * 8:(gg + 1) * 8, :]).then_inc(dS.h, 16)
                    kb.y(yfl)
                    dS.n += 16
                    ck_fl = dS.n

                    def tgb(h, gg=gg, c=ck_fl, pv=sV.n, ps=sS.n):
                        h.wait_ge(dS.h, c)
                        h.wait_ge(sV.h, pv)
                        h.wait_ge(sS.h, ps)
                        gr = bass.AP(CFL, CH - 1, [[8 * CH, 1], [CH, 8]])
                        h.matmul(Q1[:, 192 + 8 * gg:200 + 8 * gg], kb.co1f[:],
                                 gr, start=True, stop=True).then_inc(sT.h, 1)
                    kb.t(tgb)
                    sT.n += 1
                    ck_gb = sT.n

                    def vh1(h, gg=gg, c=ck_gb, ct=ck_tok, cb=ck_bias):
                        h.wait_ge(sT.h, c)
                        h.wait_ge(sS.h, ct)
                        h.wait_ge(sV.h, cb)
                        sl = slice(gg * 8, gg * 8 + 8)
                        h.tensor_tensor(rcs_t[:, sl], Q1[:, 192 + 8 * gg:200 + 8 * gg],
                                        cs_tok[:, sl], OP.subtract).then_inc(sV.h, 1)
                    kb.v(vh1)
                    sV.n += 1
                    ck_rcs = sV.n

                    def sh1(h, gg=gg, c=ck_rcs):
                        h.wait_ge(sV.h, c)
                        sl = slice(gg * 8, gg * 8 + 8)
                        h.activation(spare[:, sl], rcs_t[:, sl], AF.Exp)
                        h.activation(expcs[:, sl], cs_tok[:, sl], AF.Exp)
                        h.activation(gamt[:, sl], Q1[:, 192 + 8 * gg:200 + 8 * gg],
                                     AF.Exp).then_inc(sS.h, 1)
                    kb.s(sh1)
                    sS.n += 1
                    ck_sh = sS.n

                    def vh2(h, gg=gg, c=ck_sh):
                        h.wait_ge(sS.h, c)
                        sl = slice(gg * 8, gg * 8 + 8)
                        h.tensor_tensor(V16[:, sl], spare[:, sl], dt_tok[:, sl],
                                        OP.mult).then_inc(sV.h, 1)
                    kb.v(vh2)
                    sV.n += 1
                    grp_sm = sV.n

                    for q4 in range(2):
                        rslot = (2 * gg + q4) % 2
                        rgate = r_ck[rslot]

                        def tr(h, gg=gg, q4=q4, rslot=rslot, rgate=rgate,
                               cfl=ck_fl):
                            h.wait_ge(dS.h, cfl)
                            h.wait_ge(sS.h, rgate)
                            rhs = bass.AP(CFL, (q4 * 4) * CH,
                                          [[8 * CH, 1], [CH, 4], [1, CH]])
                            h.matmul(Q2[:, rslot * 512:(rslot + 1) * 512],
                                     kb.co1f[:], rhs, start=True, stop=False)
                            h.matmul(Q2[:, rslot * 512:(rslot + 1) * 512],
                                     kb.cl1[:], kb.csub4[:], start=False,
                                     stop=True).then_inc(sT.h, 1)
                        kb.t(tr)
                        sT.n += 1
                        ck_r = sT.n

                        def se(h, gg=gg, q4=q4, rslot=rslot, c=ck_r,
                               ceg=e8_gate, cb=ck_bias):
                            h.wait_ge(sT.h, c)
                            h.wait_ge(sT.h, ceg)
                            h.wait_ge(sV.h, cb)
                            for j in range(4):
                                hg = gg * 8 + q4 * 4 + j
                                mm = h.activation(
                                    E8[:, (q4 * 4 + j) * 128:(q4 * 4 + j + 1) * 128],
                                    Q2[:, rslot * 512 + j * 128:rslot * 512 + (j + 1) * 128],
                                    AF.Exp, bias=biasc[:, hg:hg + 1], scale=1.0)
                            mm.then_inc(sS.h, 1)
                        kb.s(se)
                        sS.n += 1
                        r_ck[rslot] = sS.n
                    ck_e8 = sS.n
                    fl_gate = sT.n

                    def vm(h, c=ck_e8, cs=ck_spm):
                        h.wait_ge(sS.h, c)
                        h.wait_ge(sV.h, cs)
                        h.tensor_tensor(
                            E8[:, :].rearrange("p (r f) -> p r f", r=8),
                            E8[:, :].rearrange("p (r f) -> p r f", r=8),
                            SPM[:, :].unsqueeze(1).broadcast_to([128, 8, 128]),
                            OP.mult).then_inc(sV.h, 1)
                    kb.v(vm)
                    sV.n += 1
                    ck_m8 = sV.n

                    def vdx(h, gg=gg, cxt=ck_xt, cscr=st["scrGate"][1]):
                        h.wait_ge(sS.h, cxt)
                        if cscr:
                            h.wait_ge(sT.h, cscr)
                        h.tensor_tensor(
                            SCR[:, :].rearrange("p (r f) -> p r f", r=8),
                            XT[:, gg * 512:(gg + 1) * 512].rearrange("p (r f) -> p r f", r=8),
                            kb.cdbc[:, gg * 8:(gg + 1) * 8].unsqueeze(2).broadcast_to([128, 8, 64]),
                            OP.mult).then_inc(sV.h, 1)
                    kb.v(vdx)
                    sV.n += 1
                    ck_dx = sV.n

                    def tin(h, gg=gg, cm=ck_m8, cdx=ck_dx,
                            cy=yevac_ck, chu=hupd_ck, cvo_off=cvo_off,
                            cms=ck_mset):
                        h.wait_ge(kb.gC.h, cms)
                        h.wait_ge(sV.h, cm)
                        h.wait_ge(sV.h, cy)
                        for j in range(8):
                            h.matmul(Q3[:, j * 64:(j + 1) * 64],
                                     E8[:, j * 128:(j + 1) * 128],
                                     XT[:, (gg * 8 + j) * 64:(gg * 8 + j + 1) * 64],
                                     start=(j == 0), stop=False)
                        h.wait_ge(sV.h, cdx)
                        h.matmul(Q3[:, 0:512], kb.cid16[:, :], SCR[:, 0:512],
                                 start=False, stop=True)
                        h.wait_ge(sV.h, chu)
                        h.matmul(Q3[:, 512:1024],
                                 CVO[:, 17 * SLAB + cvo_off:17 * SLAB + cvo_off + CH],
                                 H16[:, gg * 512:(gg + 1) * 512], start=True,
                                 stop=True).then_inc(sT.h, 1)
                    kb.t(tin)
                    sT.n += 1
                    st["lastIntra"] = sT.n
                    e8_gate = sT.n
                    ck_tin = sT.n

                    def vxh(h, gg=gg, c=ck_tin, cv=grp_sm):
                        h.wait_ge(sT.h, c)
                        h.wait_ge(sV.h, cv)
                        h.tensor_tensor(
                            SCR[:, :].rearrange("p (r f) -> p r f", r=8),
                            XT[:, gg * 512:(gg + 1) * 512].rearrange("p (r f) -> p r f", r=8),
                            V16[:, gg * 8:(gg + 1) * 8].unsqueeze(2).broadcast_to([128, 8, 64]),
                            OP.mult).then_inc(sV.h, 1)
                    kb.v(vxh)
                    sV.n += 1
                    ck_xh = sV.n

                    def tp8(h, c=ck_xh, cbt=ck_btk, cq0=q0_ck[0]):
                        h.wait_ge(sV.h, c)
                        h.wait_ge(sS.h, cbt)
                        h.wait_ge(cq0[0].h, cq0[1])
                        h.matmul(Q0[:, 0:512], BTK[:], SCR[:, 0:512],
                                 start=True, stop=True).then_inc(sT.h, 1)
                    kb.t(tp8)
                    sT.n += 1
                    ck_p8 = sT.n
                    st["scrGate"] = (sT, ck_p8)

                    def vy(h, gg=gg, c=ck_tin):
                        h.wait_ge(sT.h, c)
                        h.tensor_tensor(
                            SQT[:, :].rearrange("p (r f) -> p r f", r=8),
                            Q3[:, 512:1024].rearrange("p (r f) -> p r f", r=8),
                            expcs[:, gg * 8:(gg + 1) * 8].unsqueeze(2).broadcast_to([128, 8, 64]),
                            OP.mult)
                        h.tensor_tensor(YT[:, gg * 512:(gg + 1) * 512],
                                        SQT[:, 0:512], Q3[:, 0:512],
                                        OP.add).then_inc(sV.h, 1)
                    kb.v(vy)
                    sV.n += 1
                    yevac_ck = sV.n
                    st["lastY"] = sV.n

                    def vhu(h, gg=gg, c=ck_p8):
                        h.wait_ge(sT.h, c)
                        h.tensor_tensor(
                            SQT[:, :].rearrange("p (r f) -> p r f", r=8),
                            H16[:, gg * 512:(gg + 1) * 512].rearrange("p (r f) -> p r f", r=8),
                            gamt[:, gg * 8:(gg + 1) * 8].unsqueeze(2).broadcast_to([128, 8, 64]),
                            OP.mult)
                        h.tensor_tensor(H16[:, gg * 512:(gg + 1) * 512],
                                        SQT[:, 0:512], Q0[:, 0:512],
                                        OP.add).then_inc(sV.h, 1)
                    kb.v(vhu)
                    sV.n += 1
                    hupd_ck = sV.n
                    st["lastH"] = sV.n
                    q0_ck[0] = (sV, sV.n)
                st["lastE"] = max(r_ck)
                st["lastFl"] = ck_fl
                st["lastBcast"] = fl_gate

                if ch == 15:
                    kb.tap_raw("zb", ZB[:])
                    kb.tap_raw("ytraw", YT[:])
                    kb.tap_raw("csf", CSF[:])
                    kb.tap_raw("cstok", TKS[:])
                    kb.tap_raw("v16", V16[:])
                    kb.tap_raw("spm", SPM[:])
                    kb.tap_raw("xt", XT[:])
                    kb.tap_raw("e8", E8[:])

                ck_groups_v = sV.n
                ck_groups_e = max(r_ck)

                def vz1(h, czb=ck_zb):
                    h.wait_ge(sS.h, czb)
                    h.tensor_tensor(YT[:], YT[:], ZB[:], OP.mult).then_inc(sV.h, 1)
                kb.v(vz1)
                sV.n += 1
                ck_yz = sV.n
                st["zbGate"] = ck_yz

                def sq1(h, c=ck_yz, ct=st["scrGate"][1]):
                    h.wait_ge(sV.h, c)
                    h.wait_ge(sT.h, ct)
                    h.activation(XT[:], YT[:], AF.Square,
                                 accum_out=ROWA[:]).then_inc(sS.h, 1)
                kb.s(sq1)
                sS.n += 1
                ck_ra = sS.n

                def vr1(h, c=ck_ra):
                    h.wait_ge(sS.h, c)
                    h.tensor_scalar(ROWA[:], ROWA[:], 1.0 / DI, 1e-5, OP.mult,
                                    OP.add).then_inc(sV.h, 1)
                kb.v(vr1)
                sV.n += 1
                ck_rb = sV.n

                def sr1(h, c=ck_rb):
                    h.wait_ge(sV.h, c)
                    h.activation(ROWB[:, 0:1], ROWA[:], AF.Sqrt).then_inc(sS.h, 1)
                kb.s(sr1)
                sS.n += 1
                ck_rc = sS.n

                def vr2(h, c=ck_rc):
                    h.wait_ge(sS.h, c)
                    h.reciprocal(ROWA[:], ROWB[:, 0:1])
                    h.memset(kb.spc[:, 0:2], 0.0)
                    h.memset(kb.spc[:, 2:4], 0.0)
                    h.tensor_scalar(YT[:], YT[:], ROWA[:], None, OP.mult).then_inc(sV.h, 1)
                kb.v(vr2)
                sV.n += 1
                ck_yn = sV.n

                def tyt(h, c=ck_yn, ce=ck_groups_e):
                    h.wait_ge(sV.h, c)
                    h.wait_ge(sS.h, ce)
                    for e in range(16):
                        mm = h.transpose(Q2b[:, e * 128:(e + 1) * 128],
                                         YT[:, e * 128:(e + 1) * 128],
                                         kb.cid16[:, :])
                    mm.then_inc(sT.h, 1)
                kb.t(tyt)
                sT.n += 1
                ck_tyt = sT.n

                def syn(h, c=ck_tyt, chl=chl):
                    h.wait_ge(sT.h, c)
                    for j in range(4):
                        dst = bass.AP(YNT, (4 * j) * 512 + chl * 128,
                                      [[16 * 512, 128], [512, 4], [1, 128]])
                        src = Q2b[:, j * 512:(j + 1) * 512].rearrange(
                            "p (a b) -> p a b", a=4)
                        mm = h.copy(dst, src)
                    mm.then_inc(sS.h, 1)
                kb.s(syn)
                sS.n += 1
                st["ynDone"] = sS.n
                st["lastE"] = max(st["lastE"], st["ynDone"])

        if sc2 == 3:
            kb.tap_raw("h16", H16[:])
            kb.tap_raw("ynt", YNT[:])

        if sc2 < 3:
            emitA(2 * sc2 + 2)

        WOB = ZB
        ck_pre_v, ck_pre_s = sV.n, sS.n
        wo_cons = [(sV, ck_pre_v)] * 16
        for k in range(16):
            gate = wo_cons[k - 2] if k >= 2 else (sV, ck_pre_v)

            dx = kb.dX[k % 2]

            def gw(h, k=k, sc2=sc2, gate=gate, dx=dx):
                if gate[1]:
                    h.wait_ge(gate[0].h, gate[1])
                h.dma_start(out=WOB[:, (k % 2) * 1024:(k % 2) * 1024 + 1024],
                            in_=di["wout"][k * 128:(k + 1) * 128, :]).then_inc(dx.h, 16)
            kb.g(gw)
            dx.n += 16
            ck_l = (dx, dx.n)

            def vw(h, k=k, c=ck_l):
                h.wait_ge(c[0].h, c[1])
                h.tensor_scalar(WOB[:, (k % 2) * 1024:(k % 2) * 1024 + 1024],
                                WOB[:, (k % 2) * 1024:(k % 2) * 1024 + 1024],
                                kb.cnormw[:, k:k + 1], None, OP.mult).then_inc(sV.h, 1)
            kb.v(vw)
            sV.n += 1
            ck_s = sV.n

            def tw(h, k=k, c=ck_s, cv=ck_pre_v, cs=ck_pre_s):
                h.wait_ge(sV.h, c)
                if k == 0:
                    h.wait_ge(sV.h, cv)
                    h.wait_ge(sS.h, cs)
                for dd in range(8):
                    mm = h.matmul(kb.Q[dd // 2][:, (dd % 2) * 512:(dd % 2) * 512 + 512],
                                  WOB[:, (k % 2) * 1024 + dd * 128:(k % 2) * 1024 + (dd + 1) * 128],
                                  YNT[:, k * 512:(k + 1) * 512],
                                  start=(k == 0), stop=(k == 15))
                mm.then_inc(sT.h, 1)
            kb.t(tw)
            sT.n += 1
            wo_cons[k] = (sT, sT.n)
        ck_op = sT.n

        for dd in range(8):
            def vo(h, dd=dd, sc2=sc2, c=ck_op):
                h.wait_ge(sT.h, c)
                h.tensor_tensor(SM[:, dd * L + sc2 * 512:dd * L + sc2 * 512 + 512],
                                kb.Q[dd // 2][:, (dd % 2) * 512:(dd % 2) * 512 + 512],
                                XN[:, dd * L + sc2 * 512:dd * L + sc2 * 512 + 512],
                                OP.add).then_inc(sV.h, 1)
            kb.v(vo)
            sV.n += 1
        ck_sm = sV.n

        prev_mm = 0
        for dd in range(8):
            def sql(h, dd=dd, sc2=sc2, c=ck_sm, pm=prev_mm):
                h.wait_ge(sV.h, c)
                if pm:
                    h.wait_ge(sT.h, pm)
                h.activation(SQT[:], SM[:, dd * L + sc2 * 512:dd * L + sc2 * 512 + 512],
                             AF.Square).then_inc(sS.h, 1)
            kb.s(sql)
            sS.n += 1
            ck_q = sS.n

            def tst(h, dd=dd, sc2=sc2, c=ck_q, cv=ck_sm):
                h.wait_ge(sS.h, c)
                h.wait_ge(sV.h, cv)
                h.matmul(Q0[0:1, 0:512], kb.cocol[:],
                         SM[:, dd * L + sc2 * 512:dd * L + sc2 * 512 + 512],
                         start=(dd == 0), stop=(dd == 7))
                h.matmul(Q0[0:1, 512:1024], kb.cocol[:], SQT[:],
                         start=(dd == 0), stop=(dd == 7)).then_inc(sT.h, 1)
            kb.t(tst)
            sT.n += 1
            prev_mm = sT.n
        ck_lnst = sT.n

        def vln(h, c=ck_lnst):
            h.wait_ge(sT.h, c)
            h.tensor_scalar_mul(CFL[0:1, 0:512], Q0[0:1, 0:512], 1.0 / D)
            h.tensor_scalar_mul(CFL[0:1, 512:1024], Q0[0:1, 512:1024], 1.0 / D)
            h.tensor_tensor(SPRB[:], CFL[0:1, 0:512], CFL[0:1, 0:512], OP.mult)
            h.memset(kb.ceps[:], 1e-5)
            h.memset(kb.spc[:, 2:4], 0.0)
            h.tensor_tensor(CFL[0:1, 512:1024], CFL[0:1, 512:1024], SPRB[:],
                            OP.subtract).then_inc(sV.h, 1)
        kb.v(vln)
        sV.n += 1
        ck_lv = sV.n

        def sln(h, c=ck_lv):
            h.wait_ge(sV.h, c)
            h.activation(SPRB[:], CFL[0:1, 512:1024], AF.Sqrt,
                         bias=kb.ceps[0:1, 0:1], scale=1.0).then_inc(sS.h, 1)
        kb.s(sln)
        sS.n += 1
        ck_ls = sS.n

        def vln2(h, c=ck_ls):
            h.wait_ge(sS.h, c)
            h.reciprocal(SPRB[:], SPRB[:]).then_inc(sV.h, 1)
        kb.v(vln2)
        sV.n += 1
        ck_lr = sV.n

        def tbc(h, c=ck_lr):
            h.wait_ge(sV.h, c)
            h.matmul(Q1[:, 0:512], kb.co1f[:], CFL[0:1, 0:512], start=True,
                     stop=True)
            h.matmul(Q1[:, 512:1024], kb.co1f[:], SPRB[:], start=True,
                     stop=True).then_inc(sT.h, 1)
        kb.t(tbc)
        sT.n += 1
        ck_lb = sT.n

        for dd in range(8):
            def vap(h, dd=dd, sc2=sc2, c=ck_lb):
                h.wait_ge(sT.h, c)
                h.tensor_tensor(SQT[:], SM[:, dd * L + sc2 * 512:dd * L + sc2 * 512 + 512],
                                Q1[:, 0:512], OP.subtract)
                h.tensor_tensor(SQ2[:], SQT[:], Q1[:, 512:1024], OP.mult)
                h.tensor_scalar(SM[:, dd * L + sc2 * 512:dd * L + sc2 * 512 + 512],
                                SQ2[:], kb.clnw[:, dd:dd + 1],
                                kb.clnb[:, dd:dd + 1], OP.mult,
                                OP.add).then_inc(sV.h, 1)
            kb.v(vap)
            sV.n += 1


def emit_moe(kb, mcx):
    nc, di = kb.nc, kb.di
    dG, dS, sT, sV, sS = kb.dG, kb.dS, kb.sT, kb.sV, kb.sS
    Q0, Q1, Q2, Q3 = kb.Q
    XN, SM = kb.XN, kb.SM
    sbl = lambda n, s, d: mcx.enter_context(nc.sbuf_tensor(n, s, d))
    GAC = sbl("GAC", [128, 8 * L], BF16)
    HS = sbl("HS", [128, 16 * 1024], BF16)
    W1B = sbl("W1B", [128, 4 * 1024], BF16)
    W2B = sbl("W2B", [128, 3 * 2048], BF16)
    RWB = sbl("RWB", [128, 64], BF16)
    LGF = sbl("LGF", [8, L], F32)
    LGT = sbl("LGT", [128, 128], F32)
    GFM = sbl("GFM", [8, L], BF16)
    GFL = sbl("GFL", [1, 8 * L], BF16)
    GBC = sbl("GBC", [128, L], BF16)
    TPS = sbl("TPS_m", [128, 16], F32)
    TPR = sbl("TPR_m", [128, 16], F32)
    GE8 = sbl("GE8_m", [128, 16], F32)
    EXM = sbl("EXM_m", [128, 16], F32)
    SQM = sbl("SQM", [128, L], BF16)
    XLD = GBC
    Q2b = Q2[:, :].bitcast(BF16)

    v0, s0 = sV.n, sS.n

    def g0(h, v0=v0, s0=s0):
        h.wait_ge(sV.h, v0)
        h.wait_ge(sS.h, s0)
        h.memset(GAC[:], 0.0).then_inc(kb.gC.h, 1)
        src = bass.AP(di["routw"].tensor, 0, [[8, 128], [1024, 8], [1, 8]])
        h.dma_start(out=RWB[:], in_=src).then_inc(dG.h, 16)
    kb.g(g0)
    dG.n += 16
    kb.gC.n += 1
    ck_rw = dG.n
    ck_gmset = kb.gC.n

    q0g = [(sS, 0), (sS, 0)]
    for tc in range(4):
        gate = q0g[tc % 2]

        def t0(h, tc=tc, c=ck_rw, gate=gate):
            h.wait_ge(dG.h, c)
            if gate[1]:
                h.wait_ge(gate[0].h, gate[1])
            for k in range(8):
                mm = h.matmul(Q0[0:8, (tc % 2) * 512:(tc % 2) * 512 + 512],
                              RWB[:, k * 8:(k + 1) * 8],
                              XN[:, k * L + tc * 512:k * L + tc * 512 + 512],
                              start=(k == 0), stop=(k == 7))
            mm.then_inc(sT.h, 1)
        kb.t(t0)
        sT.n += 1
        ck_mm = sT.n

        def s0e(h, tc=tc, c=ck_mm):
            h.wait_ge(sT.h, c)
            h.copy(LGF[:, tc * 512:(tc + 1) * 512],
                   Q0[0:8, (tc % 2) * 512:(tc % 2) * 512 + 512]).then_inc(sS.h, 1)
        kb.s(s0e)
        sS.n += 1
        q0g[tc % 2] = (sS, sS.n)
    ck_lgf = sS.n

    def t1(h, c=ck_lgf):
        h.wait_ge(sS.h, c)
        for tt in range(16):
            mm = h.transpose(Q1[:, tt * 8:(tt + 1) * 8],
                             LGF[0:8, tt * 128:(tt + 1) * 128], kb.cidf[0:8, 0:8])
        mm.then_inc(sT.h, 1)
    kb.t(t1)
    sT.n += 1
    ck_tr = sT.n

    def s1(h, c=ck_tr):
        h.wait_ge(sT.h, c)
        h.copy(LGT[:], Q1[:, 0:128]).then_inc(sS.h, 1)
    kb.s(s1)
    sS.n += 1
    ck_lgt = sS.n

    gt_cons = [(sS, 0), (sS, 0)]
    for pp in range(8):
        ta, tb = 2 * pp, 2 * pp + 1
        la = LGT[:, ta * 8:(ta + 1) * 8]
        lb = LGT[:, tb * 8:(tb + 1) * 8]
        A = lambda j: TPR[:, j:j + 1]
        B = lambda j: TPR[:, 8 + j:8 + j + 1]

        def vtp(h, la=la, lb=lb, c=ck_lgt):
            h.wait_ge(sS.h, c)
            h.tensor_reduce(TPR[:, 0:1], la, mybir.AxisListType.X, OP.max)
            h.tensor_reduce(TPR[:, 8:9], lb, mybir.AxisListType.X, OP.max)
            h.tensor_tensor(TPS[:, 0:8], la,
                            TPR[:, 0:1].broadcast_to([128, 8]), OP.is_ge)
            h.tensor_tensor(TPS[:, 8:16], lb,
                            TPR[:, 8:9].broadcast_to([128, 8]), OP.is_ge)
            h.tensor_scalar_mul(TPS[:, 0:8], TPS[:, 0:8], -1e30)
            h.tensor_scalar_mul(TPS[:, 8:16], TPS[:, 8:16], -1e30)
            h.tensor_tensor(TPS[:, 0:8], TPS[:, 0:8], la, OP.add)
            h.tensor_tensor(TPS[:, 8:16], TPS[:, 8:16], lb, OP.add)
            h.tensor_reduce(TPR[:, 1:2], TPS[:, 0:8], mybir.AxisListType.X, OP.max)
            h.tensor_reduce(TPR[:, 9:10], TPS[:, 8:16], mybir.AxisListType.X, OP.max)
            h.tensor_tensor(TPR[:, 2:3], TPR[:, 1:2], TPR[:, 0:1], OP.subtract)
            h.tensor_tensor(TPR[:, 10:11], TPR[:, 9:10], TPR[:, 8:9], OP.subtract)
            h.tensor_scalar_mul(TPR[:, 3:4], TPR[:, 0:1], -1.0)
            h.tensor_scalar_mul(TPR[:, 11:12], TPR[:, 8:9],
                                -1.0).then_inc(sV.h, 1)
        kb.v(vtp)
        sV.n += 1
        ck_v1 = sV.n

        def stp(h, la=la, lb=lb, c=ck_v1):
            h.wait_ge(sV.h, c)
            h.activation(TPR[:, 4:5], TPR[:, 2:3], AF.Exp)
            h.activation(TPR[:, 12:13], TPR[:, 10:11], AF.Exp)
            h.activation(EXM[:, 0:8], la, AF.Exp, bias=TPR[:, 3:4], scale=1.0)
            h.activation(EXM[:, 8:16], lb, AF.Exp, bias=TPR[:, 11:12],
                         scale=1.0).then_inc(sS.h, 1)
        kb.s(stp)
        sS.n += 1
        ck_s1 = sS.n

        def vtp2(h, la=la, lb=lb, ta=ta, tb=tb, c=ck_s1):
            h.wait_ge(sS.h, c)
            h.tensor_scalar_add(TPR[:, 5:6], TPR[:, 4:5], 1.0)
            h.tensor_scalar_add(TPR[:, 13:14], TPR[:, 12:13], 1.0)
            h.reciprocal(TPR[:, 6:7], TPR[:, 5:6])
            h.reciprocal(TPR[:, 14:15], TPR[:, 13:14])
            h.tensor_tensor(GE8[:, 0:8], la,
                            TPR[:, 1:2].broadcast_to([128, 8]), OP.is_ge)
            h.tensor_tensor(GE8[:, 8:16], lb,
                            TPR[:, 9:10].broadcast_to([128, 8]), OP.is_ge)
            h.tensor_tensor(TPS[:, 0:8], EXM[:, 0:8], GE8[:, 0:8], OP.mult)
            h.tensor_tensor(TPS[:, 8:16], EXM[:, 8:16], GE8[:, 8:16], OP.mult)
            h.tensor_tensor(TPS[:, 0:8], TPS[:, 0:8],
                            TPR[:, 6:7].broadcast_to([128, 8]), OP.mult)
            h.tensor_tensor(TPS[:, 8:16], TPS[:, 8:16],
                            TPR[:, 14:15].broadcast_to([128, 8]), OP.mult)
            h.tensor_copy(SQM[:, ta * 8:ta * 8 + 8], TPS[:, 0:8])
            h.tensor_copy(SQM[:, tb * 8:tb * 8 + 8],
                          TPS[:, 8:16]).then_inc(sV.h, 1)
        kb.v(vtp2)
        sV.n += 1
        ck_v3 = sV.n

        for tt in (ta, tb):
            gt_gate = gt_cons[tt % 2]

            def t2(h, tt=tt, c=ck_v3, gate=gt_gate):
                h.wait_ge(sV.h, c)
                if gate[1]:
                    h.wait_ge(gate[0].h, gate[1])
                h.transpose(Q2b[0:8, (tt % 2) * 1024:(tt % 2) * 1024 + 128],
                            SQM[:, tt * 8:tt * 8 + 8],
                            kb.cid16[:, :]).then_inc(sT.h, 1)
            kb.t(t2)
            sT.n += 1
            ck_t2 = sT.n

            def s2(h, tt=tt, c=ck_t2):
                h.wait_ge(sT.h, c)
                h.copy(GFM[0:8, tt * 128:(tt + 1) * 128],
                       Q2b[0:8, (tt % 2) * 1024:(tt % 2) * 1024 + 128]).then_inc(sS.h, 1)
            kb.s(s2)
            sS.n += 1
            gt_cons[tt % 2] = (sS, sS.n)
    ck_gfm = sS.n

    def yfl(h, c=ck_gfm):
        h.wait_ge(sS.h, c)
        h.dma_start(out=bass.AP(GFL, 0, [[8 * L, 1], [L, 8], [1, L]]),
                    in_=GFM[:]).then_inc(dS.h, 16)
    kb.y(yfl)
    dS.n += 16
    ck_gfl = dS.n

    hs_gate = (sV, 0)
    gbc_gate = (sV, 0)
    w1_cons = [(sV, 0)] * 16
    w2_mm = [(sV, 0)] * 8
    q0m = [(sS, ck_lgf), (sS, ck_lgf)]
    q1m = [(sV, 0), (sV, 0)]
    for e in range(NEXP):
        rnd_gate = gbc_gate
        for rnd in range(2):
            def tgb(h, e=e, rnd=rnd, c=ck_gfl, gb=rnd_gate):
                h.wait_ge(dS.h, c)
                if gb[1]:
                    h.wait_ge(gb[0].h, gb[1])
                for j in range(2):
                    tc = rnd * 2 + j
                    mm = h.matmul(Q3[:, j * 512:(j + 1) * 512], kb.co1b[:],
                                  GFL[0:1, e * L + tc * 512:e * L + tc * 512 + 512],
                                  start=True, stop=True)
                mm.then_inc(sT.h, 1)
            kb.t(tgb)
            sT.n += 1
            ck_gb = sT.n

            def sgb(h, rnd=rnd, c=ck_gb):
                h.wait_ge(sT.h, c)
                h.copy(GBC[:, rnd * 1024:(rnd + 1) * 1024],
                       Q3[:, :]).then_inc(sS.h, 1)
            kb.s(sgb)
            sS.n += 1
            rnd_gate = (sS, sS.n)
        ck_gbc = sS.n

        for half in range(2):
            ht0 = half * 1024
            for hh in range(16):
                gate = w1_cons[hh - 4] if hh >= 4 else w1_cons[12 + hh]

                dx = kb.dX[hh % 4]

                def gw1(h, e=e, hh=hh, gate=gate, dx=dx):
                    if gate[1]:
                        h.wait_ge(gate[0].h, gate[1])
                    src = bass.AP(di["w1"].tensor,
                                  (e * 16 + hh) * 128 * 1024,
                                  [[1024, 128], [1, 1024]])
                    h.dma_start(out=W1B[:, (hh % 4) * 1024:(hh % 4) * 1024 + 1024],
                                in_=src).then_inc(dx.h, 16)
                kb.g(gw1)
                dx.n += 16
                ck_w1 = (dx, dx.n)

                for tt2 in range(2):
                    def th(h, hh=hh, tt2=tt2, ht0=ht0, c=ck_w1, gate=q0m[tt2]):
                        h.wait_ge(c[0].h, c[1])
                        if gate[1]:
                            h.wait_ge(gate[0].h, gate[1])
                        for k in range(8):
                            mm = h.matmul(Q0[:, tt2 * 512:(tt2 + 1) * 512],
                                          W1B[:, (hh % 4) * 1024 + k * 128:(hh % 4) * 1024 + (k + 1) * 128],
                                          XN[:, k * L + ht0 + tt2 * 512:k * L + ht0 + tt2 * 512 + 512],
                                          start=(k == 0), stop=(k == 7))
                        mm.then_inc(sT.h, 1)
                    kb.t(th)
                    sT.n += 1
                    ck_h = sT.n

                    def sh(h, hh=hh, tt2=tt2, c=ck_h, hg=hs_gate):
                        h.wait_ge(sT.h, c)
                        if hg[1]:
                            h.wait_ge(hg[0].h, hg[1])
                        h.activation(HS[:, hh * 1024 + tt2 * 512:hh * 1024 + tt2 * 512 + 512],
                                     Q0[:, tt2 * 512:(tt2 + 1) * 512],
                                     AF.Relu).then_inc(sS.h, 1)
                    kb.s(sh)
                    sS.n += 1
                    q0m[tt2] = (sS, sS.n)
                w1_cons[hh] = (sT, sT.n)
                ck_hh = sS.n

                def vhs(h, hh=hh, c=ck_hh):
                    h.wait_ge(sS.h, c)
                    h.tensor_tensor(HS[:, hh * 1024:(hh + 1) * 1024],
                                    HS[:, hh * 1024:(hh + 1) * 1024],
                                    HS[:, hh * 1024:(hh + 1) * 1024],
                                    OP.mult).then_inc(sV.h, 1)
                kb.v(vhs)
                sV.n += 1
            ck_sq = sV.n

            for dd in range(8):
                gate = w2_mm[dd - 3] if dd >= 3 else w2_mm[[6, 7, 5][dd]]

                dx = kb.dX[dd % 4]

                def gw2(h, e=e, dd=dd, gate=gate, dx=dx):
                    if gate[1]:
                        h.wait_ge(gate[0].h, gate[1])
                    src = bass.AP(di["w2"].tensor,
                                  (e * 8 + dd) * 128 * 2048,
                                  [[2048, 128], [1, 2048]])
                    h.dma_start(out=W2B[:, (dd % 3) * 2048:(dd % 3) * 2048 + 2048],
                                in_=src).then_inc(dx.h, 16)
                kb.g(gw2)
                dx.n += 16
                ck_w2 = (dx, dx.n)

                for tt2 in range(2):
                    def ty(h, dd=dd, tt2=tt2, c=ck_w2, cq=ck_sq, gate=q1m[tt2]):
                        h.wait_ge(c[0].h, c[1])
                        h.wait_ge(sV.h, cq)
                        if gate[1]:
                            h.wait_ge(gate[0].h, gate[1])
                        for k in range(16):
                            mm = h.matmul(Q1[:, tt2 * 512:(tt2 + 1) * 512],
                                          W2B[:, (dd % 3) * 2048 + k * 128:(dd % 3) * 2048 + (k + 1) * 128],
                                          HS[:, k * 1024 + tt2 * 512:k * 1024 + tt2 * 512 + 512],
                                          start=(k == 0), stop=(k == 15))
                        mm.then_inc(sT.h, 1)
                    kb.t(ty)
                    sT.n += 1
                    ck_y = sT.n

                    def vy2(h, dd=dd, tt2=tt2, ht0=ht0, c=ck_y, cg=ck_gbc,
                            cm=ck_gmset):
                        h.wait_ge(kb.gC.h, cm)
                        h.wait_ge(sT.h, c)
                        h.wait_ge(sS.h, cg)
                        h.tensor_tensor(SQM[:, 0:512], Q1[:, tt2 * 512:(tt2 + 1) * 512],
                                        GBC[:, ht0 + tt2 * 512:ht0 + tt2 * 512 + 512],
                                        OP.mult)
                        tgt = GAC[:, dd * L + ht0 + tt2 * 512:dd * L + ht0 + tt2 * 512 + 512]
                        h.tensor_tensor(tgt, tgt, SQM[:, 0:512],
                                        OP.add).then_inc(sV.h, 1)
                    kb.v(vy2)
                    sV.n += 1
                    q1m[tt2] = (sV, sV.n)
                w2_mm[dd] = (sT, sT.n)
            hs_gate = (sT, sT.n)
        gbc_gate = (sV, sV.n)

    kb.tap_raw("lgt", LGT[:])
    kb.tap_raw("sqmg", SQM[:])
    kb.tap_raw("gfm", GFM[:])
    kb.tap_raw("gac", GAC[:])

    ck_moe_v = sV.n
    xld_gate = (sV, ck_moe_v)
    for dd in range(8):
        def gx(h, dd=dd, gate=xld_gate):
            if gate[1]:
                h.wait_ge(gate[0].h, gate[1])
            h.dma_start(out=XLD[:], in_=di["x0"][dd * 128:(dd + 1) * 128, :]).then_inc(dG.h, 16)
        kb.g(gx)
        dG.n += 16
        ck_x = dG.n

        def scb(h, dd=dd, c=ck_moe_v, pv=xld_gate):
            h.wait_ge(sV.h, c)
            if pv[1]:
                h.wait_ge(pv[0].h, pv[1])
            h.activation(SQM[:], GAC[:, dd * L:(dd + 1) * L],
                         AF.Silu).then_inc(sS.h, 1)
        kb.s(scb)
        sS.n += 1
        ck_g = sS.n

        def vcb(h, dd=dd, c=ck_g, cx=ck_x):
            h.wait_ge(sS.h, c)
            h.wait_ge(dG.h, cx)
            h.tensor_tensor(SQM[:], SM[:, dd * L:(dd + 1) * L], SQM[:], OP.mult)
            h.tensor_tensor(SM[:, dd * L:(dd + 1) * L], XLD[:], SQM[:],
                            OP.add).then_inc(sV.h, 1)
        kb.v(vcb)
        sV.n += 1
        xld_gate = (sV, sV.n)
        ck_cb = sV.n

        def gpark(h, dd=dd, c=ck_cb):
            h.wait_ge(sV.h, c)
            h.dma_start(out=kb.dout[dd * 128:(dd + 1) * 128, :],
                        in_=SM[:, dd * L:(dd + 1) * L]).then_inc(kb.dX[3].h, 16)
        kb.g(gpark)
        kb.dX[3].n += 16
    kb.x1_park_ck = kb.dX[3].n


def emit_attn(kb, acx, S1):
    nc, di = kb.nc, kb.di
    dG, dS, sT, sV, sS = kb.dG, kb.dS, kb.sT, kb.sV, kb.sS
    Q0, Q1, Q2, Q3 = kb.Q
    XN = kb.XN
    sbl = lambda n, s, d: acx.enter_context(nc.sbuf_tensor(n, s, d))
    QP = sbl("QP", [128, 8 * L], BF16)
    KVT = sbl("KVT", [128, 2048], BF16)
    BD = sbl("BD", [128, 8 * 128], BF16)
    KSEL = sbl("KSEL", [128, 128], BF16)
    KSR = sbl("KSR", [1, 1024], F32)
    KST = sbl("KST", [128, 8], F32)
    NRM = sbl("NRM", [16, 512], F32)
    NRB = sbl("NRB", [16, 512], BF16)
    SC1 = sbl("SC1", [128, 512], BF16)
    SC2 = sbl("SC2", [128, 512], BF16)
    SC3 = sbl("SC3", [128, 512], BF16)
    crep = sbl("crep", [16, 1024], BF16)
    kb.crep = crep

    def gcr(h):
        h.dma_start(out=crep[:], in_=di["repsel"]).then_inc(dG.h, 16)
    kb.g(gcr)
    dG.n += 16

    with contextlib.ExitStack() as wcx:
        WQ = wcx.enter_context(nc.sbuf_tensor("WQ", [128, 8 * 3072], BF16))
        v0, s0 = sV.n, sS.n

        def gq(h, v0=v0, s0=s0):
            h.wait_ge(sV.h, v0)
            h.wait_ge(sS.h, s0)
            for k in range(8):
                h.dma_start(out=WQ[:, k * 3072:(k + 1) * 3072],
                            in_=di["wqkv"][k * 128:(k + 1) * 128, :]).then_inc(dG.h, 16)
        kb.g(gq)
        dG.n += 8 * 16
        ck_wq = dG.n

        qslot = [(sV, 0), (sV, 0)]
        for qq in range(8):
            for tc in range(4):
                slot = (qq * 4 + tc) % 2
                gate = qslot[slot]

                def tq(h, qq=qq, tc=tc, slot=slot, gate=gate, c=ck_wq):
                    h.wait_ge(dG.h, c)
                    if gate[1]:
                        h.wait_ge(gate[0].h, gate[1])
                    for k in range(8):
                        mm = h.matmul(Q0[:, slot * 512:(slot + 1) * 512],
                                      WQ[:, k * 3072 + qq * 128:k * 3072 + (qq + 1) * 128],
                                      XN[:, k * L + tc * 512:k * L + tc * 512 + 512],
                                      start=(k == 0), stop=(k == 7))
                    mm.then_inc(sT.h, 1)
                kb.t(tq)
                sT.n += 1
                ck_q = sT.n

                def vq(h, slot=slot, c=ck_q, ps=sS.n):
                    h.wait_ge(sT.h, c)
                    h.wait_ge(sS.h, ps)
                    h.tensor_scalar(SC1[:], Q0[:, slot * 512:(slot + 1) * 512],
                                    0.0, None, OP.min)
                    h.tensor_scalar(SC2[:], Q0[:, slot * 512:(slot + 1) * 512],
                                    0.0, None, OP.max).then_inc(sV.h, 1)
                kb.v(vq)
                sV.n += 1
                ck_qv = sV.n
                qslot[slot] = (sV, ck_qv)

                def sq(h, c=ck_qv):
                    h.wait_ge(sV.h, c)
                    h.activation(SC3[:], SC1[:], AF.Exp).then_inc(sS.h, 1)
                kb.s(sq)
                sS.n += 1
                ck_qs = sS.n

                def vq2(h, qq=qq, tc=tc, c=ck_qs):
                    h.wait_ge(sS.h, c)
                    h.tensor_tensor(QP[:, qq * L + tc * 512:qq * L + tc * 512 + 512],
                                    SC3[:], SC2[:], OP.add).then_inc(sV.h, 1)
                kb.v(vq2)
                sV.n += 1
        ck_qp = sV.n

        kvt_gate = (sT, 0)
        for tt in range(16):
            def tkv(h, tt=tt, c=ck_wq, cq=ck_qp, pv=sV.n, ps=sS.n):
                if tt == 0:
                    h.wait_ge(sV.h, cq)
                h.wait_ge(sV.h, pv)
                h.wait_ge(sS.h, ps)
                for k in range(8):
                    for j in range(2):
                        h.matmul(Q1[:, j * 512:(j + 1) * 512],
                                 XN[:, k * L + tt * 128:k * L + (tt + 1) * 128],
                                 WQ[:, k * 3072 + 1024 + j * 512:k * 3072 + 1024 + (j + 1) * 512],
                                 start=(k == 0), stop=(k == 7))
                        mm = h.matmul(Q2[:, j * 512:(j + 1) * 512],
                                      XN[:, k * L + tt * 128:k * L + (tt + 1) * 128],
                                      WQ[:, k * 3072 + 2048 + j * 512:k * 3072 + 2048 + (j + 1) * 512],
                                      start=(k == 0), stop=(k == 7))
                mm.then_inc(sT.h, 1)
            kb.t(tkv)
            sT.n += 1
            ck_kv = sT.n

            def vkf(h, c=ck_kv, kg=kvt_gate):
                h.wait_ge(sT.h, c)
                if kg[1]:
                    h.wait_ge(kg[0].h, kg[1])
                h.tensor_scalar(SC1[:], Q1[:, 0:512], 0.0, None, OP.min)
                h.tensor_scalar(KVT[:, 0:512], Q1[:, 0:512], 0.0, None, OP.max)
                h.tensor_copy(SC2[:, 0:512], SC1[:])
                h.tensor_scalar(SC1[:], Q1[:, 512:1024], 0.0, None, OP.min)
                h.tensor_scalar(KVT[:, 512:1024], Q1[:, 512:1024], 0.0, None,
                                OP.max)
                h.tensor_copy(SC3[:, 0:512], SC1[:]).then_inc(sV.h, 1)
            kb.v(vkf)
            sV.n += 1
            ck_kf = sV.n

            def skf(h, c=ck_kf):
                h.wait_ge(sV.h, c)
                h.activation(SC2[:], SC2[:], AF.Exp)
                h.activation(SC3[:], SC3[:], AF.Exp)
                h.copy(KVT[:, 1024:1536], Q2[:, 0:512])
                h.copy(KVT[:, 1536:2048], Q2[:, 512:1024]).then_inc(sS.h, 1)
            kb.s(skf)
            sS.n += 1
            ck_ks = sS.n

            def vka(h, c=ck_ks):
                h.wait_ge(sS.h, c)
                h.tensor_tensor(KVT[:, 0:512], KVT[:, 0:512], SC2[:], OP.add)
                h.tensor_tensor(KVT[:, 512:1024], KVT[:, 512:1024], SC3[:],
                                OP.add).then_inc(sV.h, 1)
            kb.v(vka)
            sV.n += 1
            ck_ka = sV.n

            def tac(h, tt=tt, c=ck_ka):
                h.wait_ge(sV.h, c)
                for hh in range(16):
                    pr = (hh % 2) * 64
                    h.matmul(Q3[pr:pr + 64, (hh // 2) * 64:(hh // 2 + 1) * 64],
                             KVT[:, hh * 64:(hh + 1) * 64],
                             KVT[:, 1024 + hh * 64:1024 + (hh + 1) * 64],
                             start=(tt == 0 and hh == 0),
                             stop=(tt == 15 and hh == 15),
                             skip_group_check=True)
                h.matmul(Q0[0:1, 0:512], kb.cocol[:], KVT[:, 0:512],
                         start=(tt == 0), stop=(tt == 15))
                h.matmul(Q0[0:1, 512:1024], kb.cocol[:], KVT[:, 512:1024],
                         start=(tt == 0), stop=(tt == 15)).then_inc(sT.h, 1)
            kb.t(tac)
            sT.n += 1
            kvt_gate = (sT, sT.n)
        ck_acc = sT.n

        def vbd(h, c=ck_acc):
            h.wait_ge(sT.h, c)
            h.memset(BD[:], 0.0)
            h.memset(KSEL[:], 0.0).then_inc(sV.h, 1)
        kb.v(vbd)
        sV.n += 1
        ck_bd0 = sV.n

        def sbd(h, c=ck_bd0):
            h.wait_ge(sV.h, c)
            for hh in range(16):
                pr = (hh % 2) * 64
                h.copy(BD[pr:pr + 64, (hh // 2) * 128 + pr:(hh // 2) * 128 + pr + 64],
                       Q3[pr:pr + 64, (hh // 2) * 64:(hh // 2 + 1) * 64])
            h.copy(KSR[:], Q0[0:1, 0:1024]).then_inc(sS.h, 1)
        kb.s(sbd)
        sS.n += 1
        ck_bd = sS.n

        def tks(h, c=ck_bd):
            h.wait_ge(sS.h, c)
            for k in range(8):
                mm = h.transpose(Q1[:, k:k + 1], KSR[0:1, k * 128:(k + 1) * 128],
                                 kb.cidf[0:1, 0:1])
            mm.then_inc(sT.h, 1)
        kb.t(tks)
        sT.n += 1
        ck_kt = sT.n

        def skt(h, c=ck_kt):
            h.wait_ge(sT.h, c)
            h.copy(KST[:], Q1[:, 0:8]).then_inc(sS.h, 1)
        kb.s(skt)
        sS.n += 1
        ck_kst = sS.n

        def vks(h, c=ck_kst):
            h.wait_ge(sS.h, c)
            for k in range(8):
                h.tensor_copy(KSEL[0:64, k * 16 + 2 * k:k * 16 + 2 * k + 1],
                              KST[0:64, k:k + 1])
                mm = h.tensor_copy(KSEL[64:128, k * 16 + 2 * k + 1:k * 16 + 2 * k + 2],
                                   KST[64:128, k:k + 1])
            mm.then_inc(sV.h, 1)
        kb.v(vks)
        sV.n += 1
        ck_ksel = sV.n

    with contextlib.ExitStack() as pcx:
        AO = pcx.enter_context(nc.sbuf_tensor("AO", [128, 8 * L], BF16))
        WP = pcx.enter_context(nc.sbuf_tensor("WP", [128, 8 * 1024], BF16))

        def gp(h, c=ck_acc, cv=sV.n, cs=sS.n):
            h.wait_ge(sT.h, c)
            h.wait_ge(sV.h, cv)
            h.wait_ge(sS.h, cs)
            for k in range(8):
                h.dma_start(out=WP[:, k * 1024:(k + 1) * 1024],
                            in_=di["wprj"][k * 128:(k + 1) * 128, :]).then_inc(dG.h, 16)
        kb.g(gp)
        dG.n += 8 * 16
        ck_wp = dG.n

        aslot = [(sV, 0), (sV, 0)]
        rslot = [(sS, 0), (sS, 0)]
        prev_nr = 0
        prev_tat = 0
        for tc in range(4):
            def tnm(h, tc=tc, c=ck_ksel, prevnrm=rslot[0]):
                h.wait_ge(sV.h, c)
                if prevnrm[1]:
                    h.wait_ge(prevnrm[0].h, prevnrm[1])
                for k in range(8):
                    mm = h.matmul(Q0[0:16, 0:512], KSEL[:, k * 16:(k + 1) * 16],
                                  QP[:, k * L + tc * 512:k * L + tc * 512 + 512],
                                  start=(k == 0), stop=(k == 7))
                mm.then_inc(sT.h, 1)
            kb.t(tnm)
            sT.n += 1
            ck_nm = sT.n

            def snm(h, c=ck_nm, pv=prev_nr):
                h.wait_ge(sT.h, c)
                if pv:
                    h.wait_ge(sV.h, pv)
                h.activation(NRM[:], Q0[0:16, 0:512], AF.Copy, bias=1e-6,
                             scale=1.0).then_inc(sS.h, 1)
            kb.s(snm)
            sS.n += 1
            ck_ne = sS.n
            rslot[0] = (sS, ck_ne)

            def vnm(h, c=ck_ne, pt=prev_tat):
                h.wait_ge(sS.h, c)
                if pt:
                    h.wait_ge(sT.h, pt)
                h.reciprocal(NRM[:], NRM[:])
                h.tensor_copy(NRB[:], NRM[:]).then_inc(sV.h, 1)
            kb.v(vnm)
            sV.n += 1
            ck_nr = sV.n
            prev_nr = ck_nr

            for qq in range(8):
                slot = qq % 2
                gate = aslot[slot]

                def tat(h, qq=qq, tc=tc, slot=slot, gate=gate, c=ck_nr):
                    h.wait_ge(sV.h, c)
                    if gate[1]:
                        h.wait_ge(gate[0].h, gate[1])
                    h.matmul(Q1[:, slot * 512:(slot + 1) * 512],
                             BD[:, qq * 128:(qq + 1) * 128],
                             QP[:, qq * L + tc * 512:qq * L + tc * 512 + 512],
                             start=True, stop=True)
                    h.matmul(Q2[:, slot * 512:(slot + 1) * 512],
                             kb.crep[:, qq * 128:(qq + 1) * 128], NRB[:],
                             start=True, stop=True).then_inc(sT.h, 1)
                kb.t(tat)
                sT.n += 1
                ck_at = sT.n

                def sat(h, slot=slot, c=ck_at):
                    h.wait_ge(sT.h, c)
                    h.copy(SC1[:], Q2[:, slot * 512:(slot + 1) * 512]).then_inc(sS.h, 1)
                kb.s(sat)
                sS.n += 1
                ck_rb = sS.n

                def vat(h, qq=qq, tc=tc, slot=slot, c=ck_rb):
                    h.wait_ge(sS.h, c)
                    h.tensor_tensor(AO[:, qq * L + tc * 512:qq * L + tc * 512 + 512],
                                    Q1[:, slot * 512:(slot + 1) * 512], SC1[:],
                                    OP.mult).then_inc(sV.h, 1)
                kb.v(vat)
                sV.n += 1
                aslot[slot] = (sV, sV.n)
            prev_tat = sT.n
        ck_ao = sV.n

        pslot = [(sS, 0), (sS, 0)]
        for dd in range(8):
            for tc in range(4):
                slot = (dd * 4 + tc) % 2
                gate = pslot[slot]

                def tpj(h, dd=dd, tc=tc, slot=slot, gate=gate, c=ck_ao, cw=ck_wp):
                    h.wait_ge(dG.h, cw)
                    h.wait_ge(sV.h, c)
                    if gate[1]:
                        h.wait_ge(gate[0].h, gate[1])
                    for k in range(8):
                        mm = h.matmul(Q3[:, slot * 512:(slot + 1) * 512],
                                      WP[:, k * 1024 + dd * 128:k * 1024 + (dd + 1) * 128],
                                      AO[:, k * L + tc * 512:k * L + tc * 512 + 512],
                                      start=(k == 0), stop=(k == 7))
                    mm.then_inc(sT.h, 1)
                kb.t(tpj)
                sT.n += 1
                ck_pj = sT.n

                def spj(h, dd=dd, tc=tc, slot=slot, c=ck_pj,
                        cp=getattr(kb, "x1_park_ck", 0)):
                    h.wait_ge(sT.h, c)
                    if cp:
                        h.wait_ge(kb.dX[3].h, cp)
                    h.tensor_scalar(S1[:, dd * L + tc * 512:dd * L + tc * 512 + 512],
                                    Q3[:, slot * 512:(slot + 1) * 512],
                                    kb.cbprj[:, dd:dd + 1], None,
                                    OP.add).then_inc(sV.h, 1)
                kb.v(spj)
                sV.n += 1
                pslot[slot] = (sV, sV.n)


def emit_ffn(kb, fcx, S1):
    nc, di = kb.nc, kb.di
    dG, dS, sT, sV, sS = kb.dG, kb.dS, kb.sT, kb.sV, kb.sS
    Q0, Q1, Q2, Q3 = kb.Q
    XN = kb.XN
    I8 = mybir.dt.int8
    sbl = lambda n, s, d: fcx.enter_context(nc.sbuf_tensor(n, s, d))
    TFC = sbl("TFC", [128, 8 * FH], I8)
    TPR = sbl("TPR", [128, 32 * 1024], I8)
    SCF = sbl("SCF", [128, 32], F32)
    SCP = sbl("SCP", [128, 8], F32)
    UTS = sbl("UTS", [16, FH], BF16)
    FBT = sbl("FBT", [16, 16], BF16)
    VTB = sbl("VTB", [128, 128], BF16)
    GCL = sbl("GCL", [16, 1], F32)

    with contextlib.ExitStack() as pcx:
        psb = lambda n, s, d: pcx.enter_context(nc.sbuf_tensor(n, s, d))
        WFB = psb("WFB", [128, FH], BF16)
        WFA = psb("WFA", [128, FH], BF16)
        SROW = psb("SROW", [1, FH + 1024], F32)
        HBC = psb("HBC", [128, 512], BF16)
        NHB = psb("NHB", [128, 512], BF16)
        HB2 = psb("HB2", [128, 1024], BF16)
        NH2 = psb("NH2", [128, 1024], BF16)
        UTB = psb("UTB", [16, FH], BF16)
        RSC = psb("RSC", [1, FH], BF16)
        WFAi = WFA[:, :].bitcast(I8)

        v0, s0 = sV.n, sS.n

        def g0(h, v0=v0, s0=s0):
            h.wait_ge(sV.h, v0)
            h.wait_ge(sS.h, s0)
            src = bass.AP(di["vt"].tensor, 0, [[16, 128], [16 * 128, 8], [1, 16]])
            h.dma_start(out=VTB[:], in_=src).then_inc(dG.h, 16)
            h.dma_start(out=FBT[:], in_=di["fbt"]).then_inc(dG.h, 16)
            h.dma_start(out=UTB[:], in_=di["ut"]).then_inc(dG.h, 16)
        kb.g(g0)
        dG.n += 48
        ck_small = dG.n

        gsb = psb("GSB", [1, 1], F32)

        def g0b(h):
            h.dma_start(out=gsb[:], in_=di["gsc"]).then_inc(dG.h, 16)
        kb.g(g0b)
        dG.n += 16
        ck_gs = dG.n

        def tg(h, c=ck_gs):
            h.wait_ge(dG.h, c)
            h.matmul(Q0[0:16, 0:1], kb.co1f[0:1, 0:16], gsb[:], start=True,
                     stop=True).then_inc(sT.h, 1)
        kb.t(tg)
        sT.n += 1
        ck_tg = sT.n

        def sg(h, c=ck_tg):
            h.wait_ge(sT.h, c)
            h.copy(GCL[:], Q0[0:16, 0:1]).then_inc(sS.h, 1)
        kb.s(sg)
        sS.n += 1

        wfb_gate = (sV, 0)
        for hh in range(8):
            def gfc(h, hh=hh, g=wfb_gate):
                if g[1]:
                    h.wait_ge(g[0].h, g[1])
                src = bass.AP(di["wfc"].tensor, hh * 512,
                              [[FH, 128], [128 * FH, 8], [1, 512]])
                h.dma_start(out=WFB[:], in_=src).then_inc(dG.h, 16)
            kb.g(gfc)
            dG.n += 16
            ck_w = dG.n

            def vab(h, c=ck_w):
                h.wait_ge(dG.h, c)
                h.activation(WFA[:], WFB[:], AF.Abs).then_inc(sS.h, 1)
            kb.s(vab)
            sS.n += 1
            ck_ab = sS.n

            def tcs2(h, hh=hh, c=ck_ab, pv=sV.n):
                h.wait_ge(sS.h, c)
                h.wait_ge(sV.h, pv)
                for k in range(8):
                    h.matmul(Q1[0:1, 0:512], kb.cocol[:],
                             WFA[:, k * 512:(k + 1) * 512],
                             start=(k == 0), stop=(k == 7))
                for hs in range(4):
                    for k in range(8):
                        mm = h.matmul(Q1[:, 512 + hs:512 + hs + 1],
                                      WFA[:, k * 512 + hs * 128:k * 512 + hs * 128 + 128],
                                      kb.cocol[:], start=(k == 0 and hs == 0),
                                      stop=(k == 7 and hs == 3))
                mm.then_inc(sT.h, 1)
            kb.t(tcs2)
            sT.n += 1
            ck_cs = sT.n

            def vsc(h, hh=hh, c=ck_cs):
                h.wait_ge(sT.h, c)
                h.tensor_scalar(SROW[0:1, hh * 512:(hh + 1) * 512], Q1[0:1, 0:512],
                                1.0 / D, 1e-5, OP.mult, OP.max)
                h.tensor_scalar(SCF[:, hh * 4:(hh + 1) * 4], Q1[:, 512:516],
                                1.0 / D, 1e-5, OP.mult, OP.max).then_inc(sV.h, 1)
            kb.v(vsc)
            sV.n += 1
            ck_sc = sV.n

            def tbc2(h, hh=hh, c=ck_sc):
                h.wait_ge(sV.h, c)
                h.matmul(Q2[:, 0:512], kb.co1f[:],
                         SROW[0:1, hh * 512:(hh + 1) * 512], start=True,
                         stop=True).then_inc(sT.h, 1)
            kb.t(tbc2)
            sT.n += 1
            ck_b = sT.n

            def vhb(h, c=ck_b):
                h.wait_ge(sT.h, c)
                h.tensor_scalar(HBC[:], Q2[:, 0:512], 0.5, None, OP.mult)
                h.tensor_scalar(NHB[:], Q2[:, 0:512], -0.5, None, OP.mult).then_inc(sV.h, 1)
            kb.v(vhb)
            sV.n += 1
            ck_hb = sV.n

            def vtn(h, hh=hh, c=ck_hb):
                rep = HBC[:, :].unsqueeze(1).broadcast_to([128, 8, 512])
                nrep = NHB[:, :].unsqueeze(1).broadcast_to([128, 8, 512])
                w3 = WFB[:, :].rearrange("p (r f) -> p r f", r=8)
                tdst = bass.AP(TFC, hh * 512, [[8 * FH, 128], [FH, 8], [1, 512]])
                h.tensor_tensor(tdst, w3, rep, OP.is_ge)
                h.tensor_tensor(WFAi[:, 0:FH].rearrange("p (r f) -> p r f", r=8),
                                w3, nrep, OP.is_le)
                h.tensor_tensor(tdst, tdst,
                                WFAi[:, 0:FH].rearrange("p (r f) -> p r f", r=8),
                                OP.subtract).then_inc(sV.h, 1)
            kb.v(vtn)
            sV.n += 1
            wfb_gate = (sV, sV.n)
        ck_tfc = sV.n

        def vrs(h, c=ck_tfc):
            with nc.allow_low_precision(reason="intentional bf16 1/scale for u-path"):
                h.reciprocal(RSC[:], SROW[0:1, 0:FH]).then_inc(sV.h, 1)
        kb.v(vrs)
        sV.n += 1
        ck_rs = sV.n

        ub_gate = 0
        for j in range(8):
            def tub(h, j=j, c=ck_rs, g=ub_gate):
                h.wait_ge(sV.h, c)
                if g:
                    h.wait_ge(sV.h, g)
                h.matmul(Q2[0:16, 0:512], kb.co1b[0:1, 0:16],
                         RSC[0:1, j * 512:(j + 1) * 512], start=True,
                         stop=True).then_inc(sT.h, 1)
            kb.t(tub)
            sT.n += 1
            ck_u = sT.n

            def vub(h, j=j, c=ck_u, cs=ck_gs):
                h.wait_ge(sT.h, c)
                h.wait_ge(dG.h, cs)
                h.tensor_tensor(UTS[:, j * 512:(j + 1) * 512],
                                UTB[:, j * 512:(j + 1) * 512], Q2[0:16, 0:512],
                                OP.mult).then_inc(sV.h, 1)
            kb.v(vub)
            sV.n += 1
            ub_gate = sV.n
        ck_uts = sV.n

        def tpa_setup(h):
            pass
        for k in range(32):
            def gpa(h, k=k, g=wfb_gate):
                if g[1]:
                    h.wait_ge(g[0].h, g[1])
                src = bass.AP(di["wpr"].tensor, k * 128 * 1024,
                              [[1024, 128], [1, 1024]])
                h.dma_start(out=WFB[:, 0:1024], in_=src).then_inc(dG.h, 16)
            kb.g(gpa)
            dG.n += 16
            ck_w = dG.n

            def vab2(h, k=k, c=ck_w):
                h.wait_ge(dG.h, c)
                h.activation(WFA[:, 0:1024], WFB[:, 0:1024],
                             AF.Abs).then_inc(sS.h, 1)
            kb.s(vab2)
            sS.n += 1
            ck_ab = sS.n
            wfb_gate = (sS, ck_ab)

            def tca(h, k=k, c=ck_ab):
                h.wait_ge(sS.h, c)
                h.matmul(Q3[0:1, 0:512], kb.cocol[:], WFA[:, 0:512],
                         start=(k == 0), stop=(k == 31))
                h.matmul(Q3[0:1, 512:1024], kb.cocol[:], WFA[:, 512:1024],
                         start=(k == 0), stop=(k == 31))
                for dd in range(8):
                    mm = h.matmul(Q2[:, 512 + dd:512 + dd + 1],
                                  WFA[:, dd * 128:(dd + 1) * 128], kb.cocol[:],
                                  start=(k == 0 and dd == 0),
                                  stop=(k == 31 and dd == 7))
                mm.then_inc(sT.h, 1)
            kb.t(tca)
            sT.n += 1
        ck_pa = sT.n

        def vsp2(h, c=ck_pa):
            h.wait_ge(sT.h, c)
            h.tensor_scalar(SROW[0:1, FH:FH + 1024],
                            Q3[0:1, 0:1024], 1.0 / FH, 1e-5, OP.mult, OP.max)
            h.tensor_scalar(SCP[:], Q2[:, 512:520], 1.0 / FH, 1e-5, OP.mult,
                            OP.max).then_inc(sV.h, 1)
        kb.v(vsp2)
        sV.n += 1
        ck_sp = sV.n

        def tb2(h, c=ck_sp):
            h.wait_ge(sV.h, c)
            h.matmul(Q2[:, 0:512], kb.co1f[:], SROW[0:1, FH:FH + 512],
                     start=True, stop=True)
            h.matmul(Q2[:, 512:1024], kb.co1f[:], SROW[0:1, FH + 512:FH + 1024],
                     start=True, stop=True).then_inc(sT.h, 1)
        kb.t(tb2)
        sT.n += 1
        ck_b2 = sT.n

        def vh2(h, c=ck_b2):
            h.wait_ge(sT.h, c)
            h.tensor_scalar(HB2[:], Q2[:, :], 0.5, None, OP.mult)
            h.tensor_scalar(NH2[:], Q2[:, :], -0.5, None, OP.mult).then_inc(sV.h, 1)
        kb.v(vh2)
        sV.n += 1
        ck_h2 = sV.n

        for k in range(32):
            def gpb(h, k=k, g=wfb_gate):
                if g[1]:
                    h.wait_ge(g[0].h, g[1])
                src = bass.AP(di["wpr"].tensor, k * 128 * 1024,
                              [[1024, 128], [1, 1024]])
                h.dma_start(out=WFB[:, 0:1024], in_=src).then_inc(dG.h, 16)
            kb.g(gpb)
            dG.n += 16
            ck_w = dG.n

            def vtn2(h, k=k, c=ck_w, ch=ck_h2):
                h.wait_ge(dG.h, c)
                h.wait_ge(sV.h, ch)
                tdst = TPR[:, k * 1024:(k + 1) * 1024]
                h.tensor_tensor(tdst, WFB[:, 0:1024], HB2[:], OP.is_ge)
                h.tensor_tensor(WFAi[:, 0:1024], WFB[:, 0:1024], NH2[:], OP.is_le)
                h.tensor_tensor(tdst, tdst, WFAi[:, 0:1024],
                                OP.subtract).then_inc(sV.h, 1)
            kb.v(vtn2)
            sV.n += 1
            wfb_gate = (sV, sV.n)
        ck_tpr = sV.n
        kb.barrier()

    with contextlib.ExitStack() as qcx:
        qsb = lambda n, s, d: qcx.enter_context(nc.sbuf_tensor(n, s, d))
        RR = qsb("RR", [128, 32 * 512], BF16)
        WFE = qsb("WFE", [128, 2048], BF16)
        WPE = qsb("WPE", [128, 1024], BF16)
        T1S = qsb("T1S", [16, 512], BF16)
        T2G = qsb("T2G", [16, 512], BF16)
        G1T = qsb("G1T", [128, 512], BF16)
        X2O = qsb("X2O", [128, 512], F32)
        X1L = qsb("X1L", [128, 512], BF16)

        wfe_gate = (sT, 0)
        wpe_gate = (sT, 0)
        x2_gate = (dS, 0)
        x1_gate = (sV, 0)
        for q in range(4):
            qt0 = q * 512
            def tt1(h, qt0=qt0, pv=sV.n, ps=sS.n):
                h.wait_ge(sV.h, pv)
                h.wait_ge(sS.h, ps)
                for k in range(8):
                    mm = h.matmul(Q1[0:16, 0:512], VTB[:, k * 16:(k + 1) * 16],
                                  XN[:, k * L + qt0:k * L + qt0 + 512],
                                  start=(k == 0), stop=(k == 7))
                mm.then_inc(sT.h, 1)
            kb.t(tt1)
            sT.n += 1
            ck_t1 = sT.n

            def st1(h, c=ck_t1):
                h.wait_ge(sT.h, c)
                h.copy(T1S[:], Q1[0:16, 0:512]).then_inc(sS.h, 1)
            kb.s(st1)
            sS.n += 1
            ck_t1s = sS.n

            def tt2(h, c=ck_t1s):
                h.wait_ge(sS.h, c)
                h.matmul(Q1[0:16, 512:1024], FBT[:], T1S[:], start=True,
                         stop=True).then_inc(sT.h, 1)
            kb.t(tt2)
            sT.n += 1
            ck_t2 = sT.n

            def vt2(h, c=ck_t2):
                h.wait_ge(sT.h, c)
                h.tensor_scalar(T2G[:], Q1[0:16, 512:1024], GCL[:], None,
                                OP.mult).then_inc(sV.h, 1)
            kb.v(vt2)
            sV.n += 1
            ck_t2g = sV.n

            q0f = [(sS, 0), (sS, 0)]
            for hh in range(8):
                for hf in range(2):
                    def vcf(h, hh=hh, hf=hf, g=wfe_gate):
                        if g[1]:
                            h.wait_ge(g[0].h, g[1])
                        src = bass.AP(TFC, hh * 512 + hf * 256,
                                      [[8 * FH, 128], [FH, 8], [1, 256]])
                        h.tensor_copy(WFE[:, :].rearrange(
                            "p (r f) -> p r f", r=8), src).then_inc(sV.h, 1)
                    kb.v(vcf)
                    sV.n += 1
                    ck_cast = sV.n

                    for hs2 in range(2):
                        hs = hf * 2 + hs2
                        slot = hs % 2
                        gate = q0f[slot]

                        def tf(h, hh=hh, hs=hs, hs2=hs2, slot=slot, qt0=qt0,
                               gate=gate, c=ck_cast, ct=ck_t2g):
                            h.wait_ge(sV.h, c)
                            h.wait_ge(sV.h, ct)
                            if gate[1]:
                                h.wait_ge(gate[0].h, gate[1])
                            h.matmul(Q0[:, slot * 512:(slot + 1) * 512],
                                     UTS[:, hh * 512 + hs * 128:hh * 512 + (hs + 1) * 128],
                                     T2G[:], start=True, stop=False)
                            for k in range(8):
                                mm = h.matmul(Q0[:, slot * 512:(slot + 1) * 512],
                                              WFE[:, k * 256 + hs2 * 128:k * 256 + hs2 * 128 + 128],
                                              XN[:, k * L + qt0:k * L + qt0 + 512],
                                              start=False, stop=(k == 7))
                            mm.then_inc(sT.h, 1)
                        kb.t(tf)
                        sT.n += 1
                        ck_f = sT.n

                        def sf(h, hh=hh, hs=hs, slot=slot, c=ck_f):
                            h.wait_ge(sT.h, c)
                            j = hh * 4 + hs
                            h.activation(RR[:, j * 512:(j + 1) * 512],
                                         Q0[:, slot * 512:(slot + 1) * 512],
                                         AF.Relu,
                                         scale=SCF[:, j:j + 1]).then_inc(sS.h, 1)
                        kb.s(sf)
                        sS.n += 1
                        q0f[slot] = (sS, sS.n)
                    wfe_gate = (sT, sT.n)
            ck_fc = sS.n

            def vsq2(h, c=ck_fc):
                h.wait_ge(sS.h, c)
                for j in range(8):
                    mm = h.tensor_tensor(RR[:, j * 2048:(j + 1) * 2048],
                                         RR[:, j * 2048:(j + 1) * 2048],
                                         RR[:, j * 2048:(j + 1) * 2048], OP.mult)
                mm.then_inc(sV.h, 1)
            kb.v(vsq2)
            sV.n += 1
            ck_rr = sV.n

            for k in range(32):
                def vcp(h, k=k, g=wpe_gate):
                    if g[1]:
                        h.wait_ge(g[0].h, g[1])
                    h.tensor_copy(WPE[:], TPR[:, k * 1024:(k + 1) * 1024]).then_inc(sV.h, 1)
                kb.v(vcp)
                sV.n += 1
                ck_c = sV.n

                def tpj2(h, k=k, c=ck_c, cr=ck_rr, cq0=q0f[0], cq1=q0f[1]):
                    h.wait_ge(sV.h, c)
                    h.wait_ge(sV.h, cr)
                    if k == 0:
                        h.wait_ge(cq0[0].h, cq0[1])
                        h.wait_ge(cq1[0].h, cq1[1])
                    for dd in range(8):
                        mm = h.matmul(kb.Q[dd // 2][:, (dd % 2) * 512:(dd % 2) * 512 + 512],
                                      WPE[:, dd * 128:(dd + 1) * 128],
                                      RR[:, k * 512:(k + 1) * 512],
                                      start=(k == 0), stop=(k == 31))
                    mm.then_inc(sT.h, 1)
                kb.t(tpj2)
                sT.n += 1
                wpe_gate = (sT, sT.n)
            ck_pj = sT.n

            g1_gate = (sV, 0)
            for dd in range(8):
                def sg1(h, dd=dd, c=ck_pj, g=g1_gate):
                    h.wait_ge(sT.h, c)
                    if g[1]:
                        h.wait_ge(g[0].h, g[1])
                    h.activation(G1T[:], kb.Q[dd // 2][:, (dd % 2) * 512:(dd % 2) * 512 + 512],
                                 AF.Silu, scale=SCP[:, dd:dd + 1]).then_inc(sS.h, 1)
                kb.s(sg1)
                sS.n += 1
                ck_g1 = sS.n

                def gx1(h, dd=dd, q=q, g=x1_gate):
                    if g[1]:
                        h.wait_ge(g[0].h, g[1])
                    h.dma_start(out=X1L[:],
                                in_=kb.dout[dd * 128:(dd + 1) * 128,
                                            q * 512:(q + 1) * 512]).then_inc(dG.h, 16)
                kb.g(gx1)
                dG.n += 16
                ck_x1 = dG.n

                def vx2(h, dd=dd, q=q, c=ck_g1, cx=ck_x1, g=x2_gate):
                    h.wait_ge(sS.h, c)
                    h.wait_ge(dG.h, cx)
                    if g[1]:
                        h.wait_ge(g[0].h, g[1])
                    h.tensor_tensor(G1T[:], kb.SM[:, dd * L + q * 512:dd * L + q * 512 + 512],
                                    G1T[:], OP.mult)
                    h.tensor_tensor(X2O[:], X1L[:], G1T[:],
                                    OP.add).then_inc(sV.h, 1)
                kb.v(vx2)
                sV.n += 1
                ck_x2 = sV.n
                x1_gate = (sV, ck_x2)
                g1_gate = (sV, ck_x2)

                def yx2(h, dd=dd, q=q, c=ck_x2):
                    h.wait_ge(sV.h, c)
                    h.dma_start(out=kb.dout[dd * 128:(dd + 1) * 128,
                                            q * 512:(q + 1) * 512],
                                in_=X2O[:]).then_inc(dS.h, 16)
                kb.y(yx2)
                dS.n += 16
                x2_gate = (dS, dS.n)



def build_core_program(taps=()):
    nc = bass.Bass("TRN2", target_bir_lowering=False, debug=False,
                   detect_race_conditions=False)
    di = {}

    def dinp(name, shape, dt=F32):
        di[name] = nc.dram_tensor(name, list(shape), dt, kind="ExternalInput").ap()

    dinp("x0", (D, L))
    dinp("win", (D, DIP))
    dinp("wout", (DI, D))
    dinp("convw", (128, NCT * 4))
    dinp("convb", (128, NCT))
    dinp("dtb", (1, NH))
    dinp("alog", (NH, 1))
    dinp("dvec", (1, NH))
    dinp("normw", (128, 16))
    dinp("lnw0", (128, 8))
    dinp("lnb0", (128, 8))
    dinp("routw", (D, NEXP))
    dinp("w1", (NEXP * 16 * 128, 8 * 128))
    dinp("w2", (NEXP * 8 * 128, 16 * 128))
    dinp("wqkv", (D, 3 * D))
    dinp("wprj", (D, D))
    dinp("bprj", (128, 8))
    dinp("wfc", (D, FH))
    dinp("wpr", (FH, D))
    dinp("ut", (16, FH))
    dinp("vt", (D, 16))
    dinp("fbt", (16, 16))
    dinp("gsc", (1, 1))
    dinp("id16", (128, 128), BF16)
    dinp("idf", (128, 128), F32)
    dinp("l1tri", (128, 128), BF16)
    dinp("sub4", (128, 512), BF16)
    dinp("repsel", (16, 1024), BF16)
    dinp("ones1f", (1, 128), F32)
    dinp("ones1b", (1, 128), BF16)
    dinp("onesr512", (1, 512), F32)
    dinp("onescol", (128, 1), BF16)

    dout = nc.dram_tensor("out", [D, L], F32, kind="ExternalOutput").ap()
    tap_d = {}
    for tname, tshape, tdt in taps:
        tap_d[tname] = nc.dram_tensor("tap_" + tname, list(tshape), tdt,
                                      kind="ExternalOutput").ap()

    P = Prog()
    with contextlib.ExitStack() as ctx:
        kb = KB(nc, ctx, P, di, dout, tap_d)
        emit_consts(kb)

        with contextlib.ExitStack() as lnc:
            emit_ln_fm(kb, lnc, kb.XN, src_dram=di["x0"])
            kb.barrier()
        kb.tap_fm("xn", kb.XN)

        with contextlib.ExitStack() as mc:
            emit_mamba(kb, mc)
            kb.barrier()
        kb.tap_fm("sm", kb.SM)

        with contextlib.ExitStack() as mcx:
            emit_moe(kb, mcx)
            kb.barrier()
        kb.tap_fm("x1", kb.SM)

        with contextlib.ExitStack() as lnc2:
            emit_ln_fm(kb, lnc2, kb.XN, src_sbuf=kb.SM)
            kb.barrier()
        kb.tap_fm("xn1", kb.XN)

        with contextlib.ExitStack() as acx:
            emit_attn(kb, acx, kb.SM)
            kb.barrier()
        kb.tap_fm("s1", kb.SM)

        with contextlib.ExitStack() as fcx:
            emit_ffn(kb, fcx, kb.SM)

        fin = kb.dS.n

        def wfin(h, f=fin, dS=kb.dS):
            h.wait_ge(dS.h, f)
        kb.y(wfin)

        block = ctx.enter_context(nc.Block())
        P.flush(block)
    return nc


def _consts_host():
    i = np.arange(128)
    l1 = (i[:, None] <= i[None, :]).astype(BF)
    sub = np.zeros((128, 512), np.float32)
    for rep in range(4):
        for t in range(128):
            if t + 1 < 128:
                sub[t + 1, rep * 128 + t] = NEGBIG
    repsel = np.zeros((16, 1024), np.float32)
    for qq in range(8):
        for j in range(128):
            repsel[2 * qq + j // 64, qq * 128 + j] = 1.0
    return {
        "id16": np.eye(128, dtype=BF),
        "idf": np.eye(128, dtype=np.float32),
        "l1tri": l1,
        "sub4": sub.astype(BF),
        "repsel": repsel.astype(BF),
        "ones1f": np.ones((1, 128), np.float32),
        "ones1b": np.ones((1, 128), BF),
        "onesr512": np.ones((1, 512), np.float32),
        "onescol": np.ones((128, 1), BF),
    }


def _prep_weights(kw):
    f32 = np.float32
    c = lambda a: np.ascontiguousarray(a, dtype=f32)
    d = {}
    d["win"] = c(np.asarray(kw["L0_attn_in_proj_w"]).T)
    d["wout"] = c(np.asarray(kw["L0_attn_out_proj_w"]).T)
    cw = np.asarray(kw["L0_attn_conv_w"], f32)
    d["convw"] = c(cw.reshape(NCT, 128, 4).transpose(1, 0, 2).reshape(128, NCT * 4))
    cb = np.asarray(kw["L0_attn_conv_b"], f32)
    d["convb"] = c(cb.reshape(NCT, 128).T)
    d["dtb"] = c(np.asarray(kw["L0_attn_dt_bias"]).reshape(1, NH))
    d["alog"] = c(np.asarray(kw["L0_attn_A_log"]).reshape(NH, 1))
    d["dvec"] = c(np.asarray(kw["L0_attn_D"]).reshape(1, NH))
    d["normw"] = c(np.asarray(kw["L0_attn_norm_w"]).reshape(16, 128).T)
    d["lnw0"] = c(np.asarray(kw["L0_attn_ln_w"]).reshape(8, 128).T)
    d["lnb0"] = c(np.asarray(kw["L0_attn_ln_b"]).reshape(8, 128).T)
    d["routw"] = c(np.asarray(kw["L0_ffn_router_w"]).T)
    w1 = np.asarray(kw["L0_ffn_w1"], f32)
    d["w1"] = c(w1.reshape(8, 8, 128, 16, 128).transpose(0, 3, 2, 1, 4)
                .reshape(8 * 16 * 128, 8 * 128))
    w2 = np.asarray(kw["L0_ffn_w2"], f32)
    d["w2"] = c(w2.reshape(8, 16, 128, 8, 128).transpose(0, 3, 2, 1, 4)
                .reshape(8 * 8 * 128, 16 * 128))
    d["wqkv"] = c(np.asarray(kw["L1_attn_Wqkv"]).T)
    d["wprj"] = c(np.asarray(kw["L1_attn_Wproj"]).T)
    d["bprj"] = c(np.asarray(kw["L1_attn_bproj"]).reshape(8, 128).T)
    d["wfc"] = c(np.asarray(kw["L1_ffn_w_fc"]).T)
    d["wpr"] = c(np.asarray(kw["L1_ffn_w_proj"]).T)
    d["ut"] = c(np.asarray(kw["L1_ffn_u"]).T)
    d["vt"] = c(np.asarray(kw["L1_ffn_v"]).T)
    d["fbt"] = c(np.asarray(kw["L1_ffn_fast_b"]).T)
    d["gsc"] = c(np.asarray(kw["L1_ffn_gate"]).reshape(1, 1))
    d.update(_consts_host())
    return d


def kernel(**inputs):
    x = np.asarray(inputs["L0_attn_x"], np.float32)
    shared = _prep_weights(inputs)
    nc = build_core_program()
    in_maps = []
    for b in range(8):
        m = dict(shared)
        m["x0"] = np.ascontiguousarray(x[b].T)
        in_maps.append(m)
    res = run_bass_kernel_spmd(nc, in_maps, core_ids=list(range(8)))
    out = np.empty((8, L, D), np.float32)
    for b in range(8):
        out[b] = res.results[b]["out"].T
    return out


# version 16
# speedup vs baseline: 1.3072x; 1.0190x over previous
import contextlib
import numpy as np
import ml_dtypes
import concourse.bass as bass
import concourse.mybir as mybir
from concourse.alu_op_type import AluOpType as OP
from concourse.bass_utils import run_bass_kernel_spmd

F32 = mybir.dt.float32
BF16 = mybir.dt.bfloat16
AF = mybir.ActivationFunctionType
BF = ml_dtypes.bfloat16

L = 2048
D = 1024
DI = 2048
DS = 128
NH = 32
HP = 64
CD = DI + 2 * DS
NCT = 18
DIP = 4384
CH = 128
SLAB = 256
NSLAB = L // SLAB
HEXP = 2048
NEXP = 8
FH = 4096
NEGBIG = -30000.0


class Sem:
    __slots__ = ("h", "n")

    def __init__(self, h):
        self.h = h
        self.n = 0


class Prog:
    def __init__(self):
        self.q = {k: [] for k in ("tensor", "vector", "scalar", "gpsimd", "sync")}

    def add(self, eng, fn):
        self.q[eng].append(fn)

    def flush(self, block):
        for eng, fns in self.q.items():
            if not fns:
                continue

            def run(h, fns=tuple(fns)):
                for fn in fns:
                    fn(h)
            getattr(block, eng)(run)


class KB:

    def __init__(self, nc, ctx, P, di, dout, tap_d):
        self.nc, self.ctx, self.P, self.di = nc, ctx, P, di
        self.dout, self.tap_d = dout, tap_d
        sb = lambda n, s, d: ctx.enter_context(nc.sbuf_tensor(n, s, d))
        self.sb = sb
        self.XN = sb("XN", [128, 8 * L], BF16)
        self.SM = sb("SM", [128, 8 * L], BF16)
        self.cid16 = sb("cid16", [128, 128], BF16)
        self.cidf = sb("cidf", [128, 128], F32)
        self.cl1 = sb("cl1", [128, 128], BF16)
        self.csub4 = sb("csub4", [128, 512], BF16)
        self.co1f = sb("co1f", [1, 128], F32)
        self.co1b = sb("co1b", [1, 128], BF16)
        self.cor512 = sb("cor512", [1, 512], F32)
        self.cocol = sb("cocol", [128, 1], BF16)
        self.ccvw = sb("ccvw", [128, NCT * 4], F32)
        self.ccvb = sb("ccvb", [128, NCT], F32)
        self.cdtb = sb("cdtb", [1, NH], F32)
        self.calog = sb("calog", [NH, 1], F32)
        self.cdvec = sb("cdvec", [1, NH], F32)
        self.cnormw = sb("cnormw", [128, 16], F32)
        self.clnw = sb("clnw", [128, 8], F32)
        self.clnb = sb("clnb", [128, 8], F32)
        self.cbprj = sb("cbprj", [128, 8], F32)
        self.cacol = sb("cacol", [NH, 1], F32)
        self.cdbc = sb("cdbc", [128, NH], BF16)
        self.bscr = sb("bscr", [1, 8], F32)
        self.spc = sb("spc", [128, 4], F32)
        self.ceps = sb("ceps", [1, 1], F32)
        self.Q = [ctx.enter_context(nc.psum_tensor("q%d" % i, [128, 1024], F32))
                  for i in range(4)]
        self.dG = Sem(ctx.enter_context(nc.semaphore("dG")))
        self.dX = [Sem(ctx.enter_context(nc.semaphore("dX%d" % i)))
                   for i in range(4)]
        self.gC = Sem(ctx.enter_context(nc.semaphore("gC")))
        self.dS = Sem(ctx.enter_context(nc.semaphore("dS")))
        self.sT = Sem(ctx.enter_context(nc.semaphore("sT")))
        self.sV = Sem(ctx.enter_context(nc.semaphore("sV")))
        self.sS = Sem(ctx.enter_context(nc.semaphore("sS")))

    def t(self, fn):
        self.P.add("tensor", fn)

    def v(self, fn):
        self.P.add("vector", fn)

    def s(self, fn):
        self.P.add("scalar", fn)

    def g(self, fn):
        self.P.add("gpsimd", fn)

    def y(self, fn):
        self.P.add("sync", fn)

    def barrier(self):
        sT, sV, sS = self.sT, self.sV, self.sS
        t0, v0, s0 = sT.n, sV.n, sS.n

        def bt(h, v0=v0, s0=s0):
            h.wait_ge(sV.h, v0)
            h.wait_ge(sS.h, s0)
            h.nop().then_inc(sT.h, 1)

        def bv(h, t0=t0, s0=s0, b=self.bscr):
            h.wait_ge(sT.h, t0)
            h.wait_ge(sS.h, s0)
            h.memset(b[0:1, 0:2], 0.0).then_inc(sV.h, 1)

        def bs(h, t0=t0, v0=v0, b=self.bscr):
            h.wait_ge(sT.h, t0)
            h.wait_ge(sV.h, v0)
            h.memzero(b[0:1, 2:4]).then_inc(sS.h, 1)
        self.t(bt)
        self.v(bv)
        self.s(bs)
        sT.n += 1
        sV.n += 1
        sS.n += 1
        tb, vb, ssn = sT.n, sV.n, sS.n

        def bg(h, tb=tb, vb=vb, ssn=ssn):
            h.wait_ge(sT.h, tb)
            h.wait_ge(sV.h, vb)
            h.wait_ge(sS.h, ssn)
        self.g(bg)
        self.y(bg)

    def tap_raw(self, name, src_ap):
        if name not in self.tap_d:
            return
        dG = self.dG
        v0, s0, t0 = self.sV.n, self.sS.n, self.sT.n
        dst = self.tap_d[name]

        def go(h, v0=v0, s0=s0, t0=t0, dst=dst, src=src_ap, sV=self.sV,
               sS=self.sS, sT=self.sT, dG=dG):
            h.wait_ge(sV.h, v0)
            h.wait_ge(sS.h, s0)
            h.wait_ge(sT.h, t0)
            h.dma_start(out=dst, in_=src).then_inc(dG.h, 16)
        self.g(go)
        dG.n += 16
        fin = dG.n

        def wt(h, fin=fin, dG=dG):
            h.wait_ge(dG.h, fin)
        self.g(wt)
        self.t(wt)
        self.v(wt)
        self.s(wt)

    def tap_fm(self, name, src):
        if name not in self.tap_d:
            return
        dG = self.dG
        v0, s0 = self.sV.n, self.sS.n
        dst = self.tap_d[name]

        def go(h, v0=v0, s0=s0, dst=dst, src=src, sV=self.sV, sS=self.sS, dG=dG):
            h.wait_ge(sV.h, v0)
            h.wait_ge(sS.h, s0)
            for k in range(8):
                h.dma_start(out=dst[k * 128:(k + 1) * 128, :],
                            in_=src[:, k * L:(k + 1) * L]).then_inc(dG.h, 16)
        self.g(go)
        dG.n += 128
        fin = dG.n

        def wt(h, fin=fin, dG=dG):
            h.wait_ge(dG.h, fin)
        self.g(wt)



def emit_consts(kb):
    di, dS, sT, sV, sS = kb.di, kb.dS, kb.sT, kb.sV, kb.sS
    cl = [("id16", kb.cid16), ("idf", kb.cidf), ("l1tri", kb.cl1),
          ("sub4", kb.csub4), ("ones1f", kb.co1f), ("ones1b", kb.co1b),
          ("onesr512", kb.cor512), ("onescol", kb.cocol), ("convw", kb.ccvw),
          ("convb", kb.ccvb), ("dtb", kb.cdtb), ("alog", kb.calog),
          ("dvec", kb.cdvec), ("normw", kb.cnormw), ("lnw0", kb.clnw),
          ("lnb0", kb.clnb), ("bprj", kb.cbprj)]

    def p0(h, cl=tuple(cl)):
        for nm, dst in cl:
            h.dma_start(out=dst[:], in_=di[nm]).then_inc(dS.h, 16)
    kb.y(p0)

    dS.n += 16 * len(cl)
    ck_cl = dS.n

    def p0s(h, c=ck_cl, kb=kb):
        h.wait_ge(dS.h, c)
        h.activation(kb.cacol[:], kb.calog[:], AF.Exp).then_inc(sS.h, 1)
    kb.s(p0s)
    sS.n += 1
    ck_ac = sS.n

    def p0v(h, c=ck_ac, kb=kb):
        h.wait_ge(sS.h, c)
        h.tensor_scalar_mul(kb.cacol[:], kb.cacol[:], -1.0).then_inc(sV.h, 1)
    kb.v(p0v)
    sV.n += 1

    def p0t(h, c=ck_cl, kb=kb):
        h.wait_ge(dS.h, c)
        h.matmul(kb.Q[0][:, 0:NH], kb.co1f[:], kb.cdvec[:],
                 start=True, stop=True).then_inc(sT.h, 1)
    kb.t(p0t)
    sT.n += 1
    ck_db = sT.n

    def p0s2(h, c=ck_db, kb=kb):
        h.wait_ge(sT.h, c)
        h.copy(kb.cdbc[:], kb.Q[0][:, 0:NH]).then_inc(sS.h, 1)
    kb.s(p0s2)
    sS.n += 1
    kb.barrier()


_LN_N = [0]


def emit_ln_fm(kb, lnc, dst, src_dram=None, src_sbuf=None):
    nc, dG, dS, sT, sV, sS = kb.nc, kb.dG, kb.dS, kb.sT, kb.sV, kb.sS
    Q = kb.Q
    _LN_N[0] += 1
    sfx = str(_LN_N[0])
    sbl = lambda n, s, d: lnc.enter_context(nc.sbuf_tensor(n + sfx, s, d))
    xsq = sbl("ln_xsq", [128, L], BF16)
    srow = sbl("ln_srow", [1, 3 * L], F32)
    mrb = sbl("ln_mrb", [128, 2 * L], BF16)
    xld = sbl("ln_xld", [128, 2 * L], BF16) if src_dram is not None else None

    def tile_src(k):
        if src_dram is not None:
            return xld[:, (k % 2) * L:(k % 2) * L + L]
        return src_sbuf[:, k * L:(k + 1) * L]

    def emit_load(k, gate_sem, gate_val):
        if src_dram is None:
            return None
        dx = kb.dX[k % 2]

        def ld(h, k=k, gs=gate_sem, gv=gate_val, dx=dx):
            if gs is not None:
                h.wait_ge(gs.h, gv)
            h.dma_start(out=xld[:, (k % 2) * L:(k % 2) * L + L],
                        in_=src_dram[k * 128:(k + 1) * 128, :]).then_inc(dx.h, 16)
        kb.g(ld)
        dx.n += 16
        return (dx, dx.n)

    cons_ck = []
    for k in range(8):
        gate = (sT, cons_ck[k - 2]) if (src_dram is not None and k >= 2) else (None, 0)
        dld = emit_load(k, gate[0], gate[1])
        src = tile_src(k)

        prevmm = cons_ck[k - 1] if k >= 1 else 0

        def s1(h, src=src, dld=dld, pm=prevmm):
            if dld is not None:
                h.wait_ge(dld[0].h, dld[1])
            if pm:
                h.wait_ge(sT.h, pm)
            h.activation(xsq[:], src, AF.Square).then_inc(sS.h, 1)
        kb.s(s1)
        sS.n += 1
        ck_sq = sS.n

        def t1(h, k=k, src=src, dld=dld, c=ck_sq):
            if dld is not None:
                h.wait_ge(dld[0].h, dld[1])
            h.wait_ge(sS.h, c)
            for tc in range(4):
                qa = Q[tc // 2]
                qb = Q[2 + tc // 2]
                off = (tc % 2) * 512
                h.matmul(qa[0:1, off:off + 512], kb.cocol[:],
                         src[:, tc * 512:(tc + 1) * 512],
                         start=(k == 0), stop=(k == 7))
                mm = h.matmul(qb[0:1, off:off + 512], kb.cocol[:],
                              xsq[:, tc * 512:(tc + 1) * 512],
                              start=(k == 0), stop=(k == 7))
            mm.then_inc(sT.h, 1)
        kb.t(t1)
        sT.n += 1
        cons_ck.append(sT.n)
    ck_st = sT.n

    def v1(h, c=ck_st):
        h.wait_ge(sT.h, c)
        h.tensor_scalar_mul(srow[0:1, 0:1024], Q[0][0:1, :], 1.0 / D)
        h.tensor_scalar_mul(srow[0:1, 1024:L], Q[1][0:1, :], 1.0 / D)
        h.tensor_scalar_mul(srow[0:1, L:L + 1024], Q[2][0:1, :], 1.0 / D)
        h.tensor_scalar_mul(srow[0:1, L + 1024:2 * L], Q[3][0:1, :], 1.0 / D)
        h.tensor_tensor(srow[0:1, 2 * L:3 * L], srow[0:1, 0:L], srow[0:1, 0:L],
                        OP.mult)
        h.tensor_tensor(srow[0:1, L:2 * L], srow[0:1, L:2 * L],
                        srow[0:1, 2 * L:3 * L], OP.subtract)
        h.tensor_scalar_add(srow[0:1, L:2 * L], srow[0:1, L:2 * L],
                            1e-5).then_inc(sV.h, 1)
    kb.v(v1)
    sV.n += 1
    ck_v = sV.n

    def s2(h, c=ck_v):
        h.wait_ge(sV.h, c)
        h.activation(srow[0:1, 2 * L:3 * L], srow[0:1, L:2 * L], AF.Sqrt).then_inc(sS.h, 1)
    kb.s(s2)
    sS.n += 1
    ck_sd = sS.n

    def v2(h, c=ck_sd):
        h.wait_ge(sS.h, c)
        h.reciprocal(srow[0:1, L:2 * L], srow[0:1, 2 * L:3 * L]).then_inc(sV.h, 1)
    kb.v(v2)
    sV.n += 1
    ck_r = sV.n

    def t2(h, c=ck_r):
        h.wait_ge(sV.h, c)
        for tc in range(4):
            qm = Q[tc // 2]
            qr = Q[2 + tc // 2]
            off = (tc % 2) * 512
            h.matmul(qm[:, off:off + 512], kb.co1f[:],
                     srow[0:1, tc * 512:(tc + 1) * 512], start=True, stop=True)
            mm = h.matmul(qr[:, off:off + 512], kb.co1f[:],
                          srow[0:1, L + tc * 512:L + (tc + 1) * 512], start=True,
                          stop=True)
        mm.then_inc(sT.h, 1)
    kb.t(t2)
    sT.n += 1
    ck_bc = sT.n

    def s3(h, c=ck_bc):
        h.wait_ge(sT.h, c)
        h.copy(mrb[:, 0:1024], Q[0][:, :])
        h.copy(mrb[:, 1024:L], Q[1][:, :])
        h.copy(mrb[:, L:L + 1024], Q[2][:, :])
        h.copy(mrb[:, L + 1024:2 * L], Q[3][:, :]).then_inc(sS.h, 1)
    kb.s(s3)
    sS.n += 1
    ck_m = sS.n

    app_ck = []
    for k in range(8):
        gate = (sV, app_ck[k - 2]) if (src_dram is not None and k >= 2) else \
               ((sS, ck_m) if src_dram is not None else (None, 0))
        dld = emit_load(k, gate[0], gate[1])
        src = tile_src(k)

        def v3(h, k=k, src=src, dld=dld, c=ck_m):
            h.wait_ge(sS.h, c)
            if dld is not None:
                h.wait_ge(dld[0].h, dld[1])
            h.tensor_tensor(xsq[:], src, mrb[:, 0:L], OP.subtract)
            h.tensor_tensor(dst[:, k * L:(k + 1) * L], xsq[:],
                            mrb[:, L:2 * L], OP.mult).then_inc(sV.h, 1)
        kb.v(v3)
        sV.n += 1
        app_ck.append(sV.n)


def emit_mamba(kb, mc):
    nc, di = kb.nc, kb.di
    dG, dS, sT, sV, sS = kb.dG, kb.dS, kb.sT, kb.sV, kb.sS
    Q0, Q1, Q2, Q3 = kb.Q
    XN, SM = kb.XN, kb.SM
    sbl = lambda n, s, d: mc.enter_context(nc.sbuf_tensor(n, s, d))
    WIN = sbl("WIN", [128, 8 * DIP], BF16)
    ZB = sbl("ZB", [128, DI], BF16)
    XBC = sbl("XBC", [128, NCT * (SLAB + 3)], BF16)
    CVO = sbl("CVO", [128, NCT * SLAB], BF16)
    XT = sbl("XT", [128, DI], BF16)
    BTK = sbl("BTK", [128, 128], BF16)
    SCR = sbl("SCR", [128, 512], BF16)
    YT = sbl("YT", [128, DI], BF16)
    YNT = sbl("YNT", [128, 16 * 512], BF16)
    H16 = sbl("H16", [128, DI], BF16)
    E8 = sbl("E8", [128, 1024], BF16)
    SPM = sbl("SPM", [128, 128], BF16)
    DTF = sbl("DTF", [NH, SLAB], F32)
    DTA = sbl("DTA", [NH, CH], F32)
    ZRO = sbl("ZRO", [NH, CH], F32)
    CSF = sbl("CSF", [NH, CH], F32)
    CFL = sbl("CFL", [1, 8 * CH], F32)
    TKS = sbl("TKS", [128, 8 * NH], F32)
    V16 = sbl("V16", [128, NH], BF16)
    SPRB = sbl("SPRB", [1, 512], F32)
    SQT = sbl("SQT", [128, 512], BF16)
    SQ2 = sbl("SQ2", [128, 512], BF16)
    ROWA = sbl("ROWA", [128, 1], F32)
    ROWB = sbl("ROWB", [128, 4], F32)
    Q1b = Q1[:, :].bitcast(BF16)
    Q2b = Q2[:, :].bitcast(BF16)
    cs_tok = TKS[:, 0 * NH:1 * NH]
    dt_tok = TKS[:, 1 * NH:2 * NH]
    logdt = TKS[:, 2 * NH:3 * NH]
    biasc = TKS[:, 3 * NH:4 * NH]
    rcs_t = TKS[:, 4 * NH:5 * NH]
    expcs = TKS[:, 5 * NH:6 * NH]
    gamt = TKS[:, 6 * NH:7 * NH]
    spare = TKS[:, 7 * NH:8 * NH]

    v0, s0 = sV.n, sS.n

    def pre_g(h, v0=v0, s0=s0):
        h.wait_ge(sV.h, v0)
        h.wait_ge(sS.h, s0)
        h.memset(ZRO[:], 0.0)
        h.memset(H16[:], 0.0)
        for e in range(NCT):
            h.memset(XBC[:, e * (SLAB + 3):e * (SLAB + 3) + 3], 0.0)
        h.memset(XBC[:, 0:1], 0.0).then_inc(kb.gC.h, 1)
        for k in range(8):
            h.dma_start(out=WIN[:, k * DIP:(k + 1) * DIP],
                        in_=di["win"][k * 128:(k + 1) * 128, :]).then_inc(dG.h, 16)
    kb.g(pre_g)
    dG.n += 16 * 8
    kb.gC.n += 1
    ck_win = dG.n
    ck_mset = kb.gC.n

    q0_ck = [(sS, sS.n), (sS, sS.n)]
    st = dict(xbc_evac=[0] * NCT, tail=sV.n, dtf=sV.n, cvo=[0] * NCT,
              lastFl=0, lastBcast=0, lastSpm=0, scrGate=(sT, 0))

    def emitA(slab):
        t0 = slab * SLAB
        for e in range(NCT):
            slot = e % 2
            gate = q0_ck[slot]
            ckw = ck_win
            first = (slab == 0 and e < 2)
            pv = st["tail"]

            def ta(h, e=e, slot=slot, gate=gate, ckw=ckw, t0=t0, first=first):
                if first:
                    h.wait_ge(dG.h, ckw)
                h.wait_ge(gate[0].h, gate[1])
                for k in range(8):
                    mm = h.matmul(Q0[:, slot * 512:slot * 512 + SLAB],
                                  WIN[:, k * DIP + DI + e * 128:k * DIP + DI + (e + 1) * 128],
                                  XN[:, k * L + t0:k * L + t0 + SLAB],
                                  start=(k == 0), stop=(k == 7))
                mm.then_inc(sT.h, 1)
            kb.t(ta)
            sT.n += 1
            ckt = sT.n

            def sa(h, e=e, slot=slot, ckt=ckt, pv=pv):
                h.wait_ge(sT.h, ckt)
                h.wait_ge(sV.h, pv)
                h.copy(XBC[:, e * (SLAB + 3) + 3:(e + 1) * (SLAB + 3)],
                       Q0[:, slot * 512:slot * 512 + SLAB]).then_inc(sS.h, 1)
            kb.s(sa)
            sS.n += 1
            q0_ck[slot] = (sS, sS.n)
            st["xbc_evac"][e] = sS.n

    emitA(0)
    for sc2 in range(4):
        for sl in range(2):
            slab = 2 * sc2 + sl
            t0 = slab * SLAB
            def tdt(h, t0=t0, pv=sV.n, ps=sS.n):
                h.wait_ge(sV.h, pv)
                h.wait_ge(sS.h, ps)
                for k in range(8):
                    h.matmul(Q1[0:NH, 224:224 + SLAB],
                             WIN[:, k * DIP + DI + CD:k * DIP + DIP],
                             XN[:, k * L + t0:k * L + t0 + SLAB],
                             start=(k == 0), stop=False)
                h.matmul(Q1[0:NH, 224:224 + SLAB], kb.cdtb[:],
                         kb.cor512[0:1, 0:SLAB], start=False,
                         stop=True).then_inc(sT.h, 1)
            kb.t(tdt)
            sT.n += 1
            ck_dtr = sT.n

            def vsp(h, c=ck_dtr):
                h.wait_ge(sT.h, c)
                h.tensor_scalar(DTF[:, :], Q1[0:NH, 224:224 + SLAB], 0.0, None,
                                OP.max).then_inc(sV.h, 1)
            kb.v(vsp)
            sV.n += 1
            ck_sp1 = sV.n

            def ssp(h, c=ck_sp1):
                h.wait_ge(sV.h, c)
                scr = SQT[:, :].bitcast(F32)
                h.activation(scr[0:NH, 0:SLAB], Q1[0:NH, 224:224 + SLAB], AF.Abs)
                h.activation(scr[NH:2 * NH, 0:SLAB], scr[0:NH, 0:SLAB], AF.Exp,
                             bias=0.0, scale=-1.0)
                h.activation(scr[0:NH, 0:SLAB], scr[NH:2 * NH, 0:SLAB], AF.Ln,
                             bias=1.0, scale=1.0).then_inc(sS.h, 1)
            kb.s(ssp)
            sS.n += 1
            ck_sp2 = sS.n

            def vsp2(h, c=ck_sp2):
                h.wait_ge(sS.h, c)
                scr = SQT[:, :].bitcast(F32)
                h.tensor_tensor(DTF[:, :], DTF[:, :], scr[0:NH, 0:SLAB],
                                OP.add).then_inc(sV.h, 1)
            kb.v(vsp2)
            sV.n += 1
            st["dtf"] = sV.n

            prev_silu = 0
            for e in range(NCT):
                ck_e = st["xbc_evac"][e]

                def vc(h, e=e, c=ck_e, pcv=prev_silu, cm=ck_mset):
                    h.wait_ge(sS.h, c)
                    h.wait_ge(kb.gC.h, cm)
                    if pcv:
                        h.wait_ge(sS.h, pcv)
                    if pcv:
                        h.wait_ge(sS.h, pcv)
                    o = e * (SLAB + 3)
                    h.tensor_scalar(SQ2[:, 0:SLAB], XBC[:, o:o + SLAB],
                                    kb.ccvw[:, e * 4:e * 4 + 1], None, OP.mult)
                    for kk in range(1, 4):
                        mm = h.scalar_tensor_tensor(SQ2[:, 0:SLAB],
                                                    XBC[:, o + kk:o + kk + SLAB],
                                                    kb.ccvw[:, e * 4 + kk:e * 4 + kk + 1],
                                                    SQ2[:, 0:SLAB], OP.mult, OP.add)
                    mm.then_inc(sV.h, 1)
                kb.v(vc)
                sV.n += 1
                ck_cv = sV.n

                def sc_(h, e=e, c=ck_cv):
                    h.wait_ge(sV.h, c)
                    h.activation(CVO[:, e * SLAB:(e + 1) * SLAB], SQ2[:, 0:SLAB],
                                 AF.Silu, bias=kb.ccvb[:, e:e + 1],
                                 scale=1.0).then_inc(sS.h, 1)
                kb.s(sc_)
                sS.n += 1
                st["cvo"][e] = sS.n
                prev_silu = sS.n
            ck_cvo_all = sS.n

            def vt(h, c=ck_cvo_all):
                h.wait_ge(sS.h, c)
                pch = NCT * (SLAB + 3)
                h.tensor_copy(bass.AP(XBC, 0, [[pch, 128], [SLAB + 3, NCT], [1, 3]]),
                              bass.AP(XBC, SLAB, [[pch, 128], [SLAB + 3, NCT], [1, 3]])
                              ).then_inc(sV.h, 1)
            kb.v(vt)
            sV.n += 1
            st["tail"] = sV.n

            if sl == 0:
                emitA(slab + 1)

            if slab == NSLAB - 1:
                kb.tap_raw("cvo", CVO[:])
                kb.tap_raw("dtf", DTF[:])

            for cc in range(SLAB // CH):
                ch = slab * 2 + cc
                chl = ch % 4
                ct0 = t0 + cc * CH
                cvo_off = cc * CH

                zb_gate = st.get("zbGate", 0)
                for zf in range(4):
                    slot = zf % 2
                    gate = q0_ck[slot]

                    def tz(h, zf=zf, slot=slot, gate=gate, ct0=ct0):
                        h.wait_ge(gate[0].h, gate[1])
                        for k in range(8):
                            mm = h.matmul(Q0[:, slot * 512:(slot + 1) * 512],
                                          XN[:, k * L + ct0:k * L + ct0 + CH],
                                          WIN[:, k * DIP + zf * 512:k * DIP + (zf + 1) * 512],
                                          start=(k == 0), stop=(k == 7))
                        mm.then_inc(sT.h, 1)
                    kb.t(tz)
                    sT.n += 1
                    ckz = sT.n

                    def sz(h, zf=zf, slot=slot, c=ckz, zg=zb_gate):
                        h.wait_ge(sT.h, c)
                        if zg:
                            h.wait_ge(sV.h, zg)
                        h.activation(ZB[:, zf * 512:(zf + 1) * 512],
                                     Q0[:, slot * 512:(slot + 1) * 512],
                                     AF.Silu).then_inc(sS.h, 1)
                    kb.s(sz)
                    sS.n += 1
                    q0_ck[slot] = (sS, sS.n)
                ck_zb = sS.n

                def vda(h, cc=cc, cdtf=st["dtf"], cfl=st["lastFl"], cm=ck_mset):
                    h.wait_ge(kb.gC.h, cm)
                    h.wait_ge(sV.h, cdtf)
                    if cfl:
                        h.wait_ge(dS.h, cfl)
                    h.tensor_scalar(DTA[:], DTF[:, cc * CH:cc * CH + CH],
                                    kb.cacol[:], None, OP.mult)
                    h.tensor_tensor_scan(CSF[:], DTA[:], ZRO[:], 0.0, OP.add,
                                         OP.add).then_inc(sV.h, 1)
                kb.v(vda)
                sV.n += 1
                ck_csf = sV.n

                def tcs(h, cc=cc, c=ck_csf, csp=ck_sp2, pv=sV.n, ps=sS.n):
                    h.wait_ge(sV.h, c)
                    h.wait_ge(sS.h, csp)
                    h.wait_ge(sV.h, pv)
                    h.wait_ge(sS.h, ps)
                    h.transpose(Q1[:, 128:160], CSF[:], kb.cidf[0:NH, 0:NH])
                    h.transpose(Q1[:, 160:192], DTF[:, cc * CH:cc * CH + CH],
                                kb.cidf[0:NH, 0:NH]).then_inc(sT.h, 1)
                kb.t(tcs)
                sT.n += 1
                ck_trc = sT.n

                def stok(h, c=ck_trc):
                    h.wait_ge(sT.h, c)
                    h.copy(cs_tok, Q1[:, 128:160])
                    h.copy(dt_tok, Q1[:, 160:192])
                    h.activation(logdt, Q1[:, 160:192], AF.Ln).then_inc(sS.h, 1)
                kb.s(stok)
                sS.n += 1
                ck_tok = sS.n

                def vbias(h, c=ck_tok):
                    h.wait_ge(sS.h, c)
                    h.tensor_tensor(biasc, logdt, cs_tok,
                                    OP.subtract).then_inc(sV.h, 1)
                kb.v(vbias)
                sV.n += 1
                ck_bias = sV.n

                def tbs(h, cvo_off=cvo_off, c=ck_cvo_all, ct=ck_tok,
                        pv=sV.n, ps=sS.n):
                    h.wait_ge(sS.h, c)
                    h.wait_ge(sS.h, ct)
                    h.wait_ge(sV.h, pv)
                    h.wait_ge(sS.h, ps)
                    h.transpose(Q1b[:, 1024:1152],
                                CVO[:, 16 * SLAB + cvo_off:16 * SLAB + cvo_off + CH],
                                kb.cid16[:, :])
                    h.matmul(Q1[:, 0:128],
                             CVO[:, 16 * SLAB + cvo_off:16 * SLAB + cvo_off + CH],
                             CVO[:, 17 * SLAB + cvo_off:17 * SLAB + cvo_off + CH],
                             start=True, stop=True).then_inc(sT.h, 1)
                kb.t(tbs)
                sT.n += 1
                ck_bs = sT.n

                def sbt(h, c=ck_bs, cp=st["scrGate"][1]):
                    h.wait_ge(sT.h, c)
                    if cp:
                        h.wait_ge(sT.h, cp)
                    h.copy(BTK[:], Q1b[:, 1024:1152]).then_inc(sS.h, 1)
                kb.s(sbt)
                sS.n += 1
                ck_btk = sS.n

                def vspm(h, c=ck_bs):
                    h.wait_ge(sT.h, c)
                    h.tensor_tensor(SPM[:], Q1[:, 0:128], kb.cl1[:],
                                    OP.mult).then_inc(sV.h, 1)
                kb.v(vspm)
                sV.n += 1
                ck_spm = sV.n
                st["lastSpm"] = ck_spm

                def txs(h, cvo_off=cvo_off, cR=st.get("lastR", 0), cE=st.get("lastE", 0)):
                    h.wait_ge(sS.h, cE)
                    for e in range(16):
                        mm = h.transpose(Q2b[:, e * 128:(e + 1) * 128],
                                         CVO[:, e * SLAB + cvo_off:e * SLAB + cvo_off + CH],
                                         kb.cid16[:, :])
                    mm.then_inc(sT.h, 1)
                kb.t(txs)
                sT.n += 1
                ck_txs = sT.n

                def sxt(h, c=ck_txs):
                    h.wait_ge(sT.h, c)
                    for j in range(4):
                        mm = h.copy(XT[:, j * 512:(j + 1) * 512],
                                    Q2b[:, j * 512:(j + 1) * 512])
                    mm.then_inc(sS.h, 1)
                kb.s(sxt)
                sS.n += 1
                ck_xt = sS.n

                yevac_ck = st.get("lastY", 0)
                hupd_ck = st.get("lastH", 0)
                e8_gate = st.get("lastIntra", 0)
                r_ck = [ck_xt, ck_xt]
                fl_gate = st["lastBcast"]
                ck_fl = 0

                for gg in range(4):
                    def yfl(h, gg=gg, c=ck_csf, cb=fl_gate):
                        h.wait_ge(sV.h, c)
                        if cb:
                            h.wait_ge(sT.h, cb)
                        h.dma_start(out=bass.AP(CFL, 0, [[8 * CH, 1], [CH, 8], [1, CH]]),
                                    in_=CSF[gg * 8:(gg + 1) * 8, :]).then_inc(dS.h, 16)
                    kb.y(yfl)
                    dS.n += 16
                    ck_fl = dS.n

                    def tgb(h, gg=gg, c=ck_fl, pv=sV.n, ps=sS.n):
                        h.wait_ge(dS.h, c)
                        h.wait_ge(sV.h, pv)
                        h.wait_ge(sS.h, ps)
                        gr = bass.AP(CFL, CH - 1, [[8 * CH, 1], [CH, 8]])
                        h.matmul(Q1[:, 192 + 8 * gg:200 + 8 * gg], kb.co1f[:],
                                 gr, start=True, stop=True).then_inc(sT.h, 1)
                    kb.t(tgb)
                    sT.n += 1
                    ck_gb = sT.n

                    def vh1(h, gg=gg, c=ck_gb, ct=ck_tok, cb=ck_bias):
                        h.wait_ge(sT.h, c)
                        h.wait_ge(sS.h, ct)
                        h.wait_ge(sV.h, cb)
                        sl = slice(gg * 8, gg * 8 + 8)
                        h.tensor_tensor(rcs_t[:, sl], Q1[:, 192 + 8 * gg:200 + 8 * gg],
                                        cs_tok[:, sl], OP.subtract).then_inc(sV.h, 1)
                    kb.v(vh1)
                    sV.n += 1
                    ck_rcs = sV.n

                    def sh1(h, gg=gg, c=ck_rcs):
                        h.wait_ge(sV.h, c)
                        sl = slice(gg * 8, gg * 8 + 8)
                        h.activation(spare[:, sl], rcs_t[:, sl], AF.Exp)
                        h.activation(expcs[:, sl], cs_tok[:, sl], AF.Exp)
                        h.activation(gamt[:, sl], Q1[:, 192 + 8 * gg:200 + 8 * gg],
                                     AF.Exp).then_inc(sS.h, 1)
                    kb.s(sh1)
                    sS.n += 1
                    ck_sh = sS.n

                    def vh2(h, gg=gg, c=ck_sh):
                        h.wait_ge(sS.h, c)
                        sl = slice(gg * 8, gg * 8 + 8)
                        h.tensor_tensor(V16[:, sl], spare[:, sl], dt_tok[:, sl],
                                        OP.mult).then_inc(sV.h, 1)
                    kb.v(vh2)
                    sV.n += 1
                    grp_sm = sV.n

                    for q4 in range(2):
                        rslot = (2 * gg + q4) % 2
                        rgate = r_ck[rslot]

                        def tr(h, gg=gg, q4=q4, rslot=rslot, rgate=rgate,
                               cfl=ck_fl):
                            h.wait_ge(dS.h, cfl)
                            h.wait_ge(sS.h, rgate)
                            rhs = bass.AP(CFL, (q4 * 4) * CH,
                                          [[8 * CH, 1], [CH, 4], [1, CH]])
                            h.matmul(Q2[:, rslot * 512:(rslot + 1) * 512],
                                     kb.co1f[:], rhs, start=True, stop=False)
                            h.matmul(Q2[:, rslot * 512:(rslot + 1) * 512],
                                     kb.cl1[:], kb.csub4[:], start=False,
                                     stop=True).then_inc(sT.h, 1)
                        kb.t(tr)
                        sT.n += 1
                        ck_r = sT.n

                        def se(h, gg=gg, q4=q4, rslot=rslot, c=ck_r,
                               ceg=e8_gate, cb=ck_bias):
                            h.wait_ge(sT.h, c)
                            h.wait_ge(sT.h, ceg)
                            h.wait_ge(sV.h, cb)
                            for j in range(4):
                                hg = gg * 8 + q4 * 4 + j
                                mm = h.activation(
                                    E8[:, (q4 * 4 + j) * 128:(q4 * 4 + j + 1) * 128],
                                    Q2[:, rslot * 512 + j * 128:rslot * 512 + (j + 1) * 128],
                                    AF.Exp, bias=biasc[:, hg:hg + 1], scale=1.0)
                            mm.then_inc(sS.h, 1)
                        kb.s(se)
                        sS.n += 1
                        r_ck[rslot] = sS.n
                    ck_e8 = sS.n
                    fl_gate = sT.n

                    def vm(h, c=ck_e8, cs=ck_spm):
                        h.wait_ge(sS.h, c)
                        h.wait_ge(sV.h, cs)
                        h.tensor_tensor(
                            E8[:, :].rearrange("p (r f) -> p r f", r=8),
                            E8[:, :].rearrange("p (r f) -> p r f", r=8),
                            SPM[:, :].unsqueeze(1).broadcast_to([128, 8, 128]),
                            OP.mult).then_inc(sV.h, 1)
                    kb.v(vm)
                    sV.n += 1
                    ck_m8 = sV.n

                    def vdx(h, gg=gg, cxt=ck_xt, cscr=st["scrGate"][1]):
                        h.wait_ge(sS.h, cxt)
                        if cscr:
                            h.wait_ge(sT.h, cscr)
                        h.tensor_tensor(
                            SCR[:, :].rearrange("p (r f) -> p r f", r=8),
                            XT[:, gg * 512:(gg + 1) * 512].rearrange("p (r f) -> p r f", r=8),
                            kb.cdbc[:, gg * 8:(gg + 1) * 8].unsqueeze(2).broadcast_to([128, 8, 64]),
                            OP.mult).then_inc(sV.h, 1)
                    kb.v(vdx)
                    sV.n += 1
                    ck_dx = sV.n

                    def tin(h, gg=gg, cm=ck_m8, cdx=ck_dx,
                            cy=yevac_ck, chu=hupd_ck, cvo_off=cvo_off,
                            cms=ck_mset):
                        h.wait_ge(kb.gC.h, cms)
                        h.wait_ge(sV.h, cm)
                        h.wait_ge(sV.h, cy)
                        for j in range(8):
                            h.matmul(Q3[:, j * 64:(j + 1) * 64],
                                     E8[:, j * 128:(j + 1) * 128],
                                     XT[:, (gg * 8 + j) * 64:(gg * 8 + j + 1) * 64],
                                     start=(j == 0), stop=False)
                        h.wait_ge(sV.h, cdx)
                        h.matmul(Q3[:, 0:512], kb.cid16[:, :], SCR[:, 0:512],
                                 start=False, stop=True)
                        h.wait_ge(sV.h, chu)
                        h.matmul(Q3[:, 512:1024],
                                 CVO[:, 17 * SLAB + cvo_off:17 * SLAB + cvo_off + CH],
                                 H16[:, gg * 512:(gg + 1) * 512], start=True,
                                 stop=True).then_inc(sT.h, 1)
                    kb.t(tin)
                    sT.n += 1
                    st["lastIntra"] = sT.n
                    e8_gate = sT.n
                    ck_tin = sT.n

                    def vxh(h, gg=gg, c=ck_tin, cv=grp_sm):
                        h.wait_ge(sT.h, c)
                        h.wait_ge(sV.h, cv)
                        h.tensor_tensor(
                            SCR[:, :].rearrange("p (r f) -> p r f", r=8),
                            XT[:, gg * 512:(gg + 1) * 512].rearrange("p (r f) -> p r f", r=8),
                            V16[:, gg * 8:(gg + 1) * 8].unsqueeze(2).broadcast_to([128, 8, 64]),
                            OP.mult).then_inc(sV.h, 1)
                    kb.v(vxh)
                    sV.n += 1
                    ck_xh = sV.n

                    def tp8(h, c=ck_xh, cbt=ck_btk, cq0=q0_ck[0]):
                        h.wait_ge(sV.h, c)
                        h.wait_ge(sS.h, cbt)
                        h.wait_ge(cq0[0].h, cq0[1])
                        h.matmul(Q0[:, 0:512], BTK[:], SCR[:, 0:512],
                                 start=True, stop=True).then_inc(sT.h, 1)
                    kb.t(tp8)
                    sT.n += 1
                    ck_p8 = sT.n
                    st["scrGate"] = (sT, ck_p8)

                    def vy(h, gg=gg, c=ck_tin):
                        h.wait_ge(sT.h, c)
                        h.tensor_tensor(
                            SQT[:, :].rearrange("p (r f) -> p r f", r=8),
                            Q3[:, 512:1024].rearrange("p (r f) -> p r f", r=8),
                            expcs[:, gg * 8:(gg + 1) * 8].unsqueeze(2).broadcast_to([128, 8, 64]),
                            OP.mult)
                        h.tensor_tensor(YT[:, gg * 512:(gg + 1) * 512],
                                        SQT[:, 0:512], Q3[:, 0:512],
                                        OP.add).then_inc(sV.h, 1)
                    kb.v(vy)
                    sV.n += 1
                    yevac_ck = sV.n
                    st["lastY"] = sV.n

                    def vhu(h, gg=gg, c=ck_p8):
                        h.wait_ge(sT.h, c)
                        h.tensor_tensor(
                            SQT[:, :].rearrange("p (r f) -> p r f", r=8),
                            H16[:, gg * 512:(gg + 1) * 512].rearrange("p (r f) -> p r f", r=8),
                            gamt[:, gg * 8:(gg + 1) * 8].unsqueeze(2).broadcast_to([128, 8, 64]),
                            OP.mult)
                        h.tensor_tensor(H16[:, gg * 512:(gg + 1) * 512],
                                        SQT[:, 0:512], Q0[:, 0:512],
                                        OP.add).then_inc(sV.h, 1)
                    kb.v(vhu)
                    sV.n += 1
                    hupd_ck = sV.n
                    st["lastH"] = sV.n
                    q0_ck[0] = (sV, sV.n)
                st["lastE"] = max(r_ck)
                st["lastFl"] = ck_fl
                st["lastBcast"] = fl_gate

                if ch == 15:
                    kb.tap_raw("zb", ZB[:])
                    kb.tap_raw("ytraw", YT[:])
                    kb.tap_raw("csf", CSF[:])
                    kb.tap_raw("cstok", TKS[:])
                    kb.tap_raw("v16", V16[:])
                    kb.tap_raw("spm", SPM[:])
                    kb.tap_raw("xt", XT[:])
                    kb.tap_raw("e8", E8[:])

                ck_groups_v = sV.n
                ck_groups_e = max(r_ck)

                def vz1(h, czb=ck_zb):
                    h.wait_ge(sS.h, czb)
                    h.tensor_tensor(YT[:], YT[:], ZB[:], OP.mult).then_inc(sV.h, 1)
                kb.v(vz1)
                sV.n += 1
                ck_yz = sV.n
                st["zbGate"] = ck_yz

                def sq1(h, c=ck_yz, ct=st["scrGate"][1]):
                    h.wait_ge(sV.h, c)
                    h.wait_ge(sT.h, ct)
                    h.activation(XT[:], YT[:], AF.Square,
                                 accum_out=ROWA[:]).then_inc(sS.h, 1)
                kb.s(sq1)
                sS.n += 1
                ck_ra = sS.n

                def vr1(h, c=ck_ra):
                    h.wait_ge(sS.h, c)
                    h.tensor_scalar(ROWA[:], ROWA[:], 1.0 / DI, 1e-5, OP.mult,
                                    OP.add).then_inc(sV.h, 1)
                kb.v(vr1)
                sV.n += 1
                ck_rb = sV.n

                def sr1(h, c=ck_rb):
                    h.wait_ge(sV.h, c)
                    h.activation(ROWB[:, 0:1], ROWA[:], AF.Sqrt).then_inc(sS.h, 1)
                kb.s(sr1)
                sS.n += 1
                ck_rc = sS.n

                def vr2(h, c=ck_rc):
                    h.wait_ge(sS.h, c)
                    h.reciprocal(ROWA[:], ROWB[:, 0:1])
                    h.memset(kb.spc[:, 0:2], 0.0)
                    h.memset(kb.spc[:, 2:4], 0.0)
                    h.tensor_scalar(YT[:], YT[:], ROWA[:], None, OP.mult).then_inc(sV.h, 1)
                kb.v(vr2)
                sV.n += 1
                ck_yn = sV.n

                def tyt(h, c=ck_yn, ce=ck_groups_e):
                    h.wait_ge(sV.h, c)
                    h.wait_ge(sS.h, ce)
                    for e in range(16):
                        mm = h.transpose(Q2b[:, e * 128:(e + 1) * 128],
                                         YT[:, e * 128:(e + 1) * 128],
                                         kb.cid16[:, :])
                    mm.then_inc(sT.h, 1)
                kb.t(tyt)
                sT.n += 1
                ck_tyt = sT.n

                def syn(h, c=ck_tyt, chl=chl):
                    h.wait_ge(sT.h, c)
                    for j in range(4):
                        dst = bass.AP(YNT, (4 * j) * 512 + chl * 128,
                                      [[16 * 512, 128], [512, 4], [1, 128]])
                        src = Q2b[:, j * 512:(j + 1) * 512].rearrange(
                            "p (a b) -> p a b", a=4)
                        mm = h.copy(dst, src)
                    mm.then_inc(sS.h, 1)
                kb.s(syn)
                sS.n += 1
                st["ynDone"] = sS.n
                st["lastE"] = max(st["lastE"], st["ynDone"])

        if sc2 == 3:
            kb.tap_raw("h16", H16[:])
            kb.tap_raw("ynt", YNT[:])

        if sc2 < 3:
            emitA(2 * sc2 + 2)

        WOB = ZB
        ck_pre_v, ck_pre_s = sV.n, sS.n
        wo_cons = [(sV, ck_pre_v)] * 16
        for k in range(16):
            gate = wo_cons[k - 2] if k >= 2 else (sV, ck_pre_v)

            dx = kb.dX[k % 2]

            def gw(h, k=k, sc2=sc2, gate=gate, dx=dx):
                if gate[1]:
                    h.wait_ge(gate[0].h, gate[1])
                h.dma_start(out=WOB[:, (k % 2) * 1024:(k % 2) * 1024 + 1024],
                            in_=di["wout"][k * 128:(k + 1) * 128, :]).then_inc(dx.h, 16)
            kb.g(gw)
            dx.n += 16
            ck_l = (dx, dx.n)

            def vw(h, k=k, c=ck_l):
                h.wait_ge(c[0].h, c[1])
                h.tensor_scalar(WOB[:, (k % 2) * 1024:(k % 2) * 1024 + 1024],
                                WOB[:, (k % 2) * 1024:(k % 2) * 1024 + 1024],
                                kb.cnormw[:, k:k + 1], None, OP.mult).then_inc(sV.h, 1)
            kb.v(vw)
            sV.n += 1
            ck_s = sV.n

            def tw(h, k=k, c=ck_s, cv=ck_pre_v, cs=ck_pre_s):
                h.wait_ge(sV.h, c)
                if k == 0:
                    h.wait_ge(sV.h, cv)
                    h.wait_ge(sS.h, cs)
                for dd in range(8):
                    mm = h.matmul(kb.Q[dd // 2][:, (dd % 2) * 512:(dd % 2) * 512 + 512],
                                  WOB[:, (k % 2) * 1024 + dd * 128:(k % 2) * 1024 + (dd + 1) * 128],
                                  YNT[:, k * 512:(k + 1) * 512],
                                  start=(k == 0), stop=(k == 15))
                mm.then_inc(sT.h, 1)
            kb.t(tw)
            sT.n += 1
            wo_cons[k] = (sT, sT.n)
        ck_op = sT.n

        for dd in range(8):
            def vo(h, dd=dd, sc2=sc2, c=ck_op):
                h.wait_ge(sT.h, c)
                h.tensor_tensor(SM[:, dd * L + sc2 * 512:dd * L + sc2 * 512 + 512],
                                kb.Q[dd // 2][:, (dd % 2) * 512:(dd % 2) * 512 + 512],
                                XN[:, dd * L + sc2 * 512:dd * L + sc2 * 512 + 512],
                                OP.add).then_inc(sV.h, 1)
            kb.v(vo)
            sV.n += 1
        ck_sm = sV.n

        prev_mm = 0
        for dd in range(8):
            def sql(h, dd=dd, sc2=sc2, c=ck_sm, pm=prev_mm):
                h.wait_ge(sV.h, c)
                if pm:
                    h.wait_ge(sT.h, pm)
                h.activation(SQT[:], SM[:, dd * L + sc2 * 512:dd * L + sc2 * 512 + 512],
                             AF.Square).then_inc(sS.h, 1)
            kb.s(sql)
            sS.n += 1
            ck_q = sS.n

            def tst(h, dd=dd, sc2=sc2, c=ck_q, cv=ck_sm):
                h.wait_ge(sS.h, c)
                h.wait_ge(sV.h, cv)
                h.matmul(Q0[0:1, 0:512], kb.cocol[:],
                         SM[:, dd * L + sc2 * 512:dd * L + sc2 * 512 + 512],
                         start=(dd == 0), stop=(dd == 7))
                h.matmul(Q0[0:1, 512:1024], kb.cocol[:], SQT[:],
                         start=(dd == 0), stop=(dd == 7)).then_inc(sT.h, 1)
            kb.t(tst)
            sT.n += 1
            prev_mm = sT.n
        ck_lnst = sT.n

        def vln(h, c=ck_lnst):
            h.wait_ge(sT.h, c)
            h.tensor_scalar_mul(CFL[0:1, 0:512], Q0[0:1, 0:512], 1.0 / D)
            h.tensor_scalar_mul(CFL[0:1, 512:1024], Q0[0:1, 512:1024], 1.0 / D)
            h.tensor_tensor(SPRB[:], CFL[0:1, 0:512], CFL[0:1, 0:512], OP.mult)
            h.memset(kb.ceps[:], 1e-5)
            h.memset(kb.spc[:, 2:4], 0.0)
            h.tensor_tensor(CFL[0:1, 512:1024], CFL[0:1, 512:1024], SPRB[:],
                            OP.subtract).then_inc(sV.h, 1)
        kb.v(vln)
        sV.n += 1
        ck_lv = sV.n

        def sln(h, c=ck_lv):
            h.wait_ge(sV.h, c)
            h.activation(SPRB[:], CFL[0:1, 512:1024], AF.Sqrt,
                         bias=kb.ceps[0:1, 0:1], scale=1.0).then_inc(sS.h, 1)
        kb.s(sln)
        sS.n += 1
        ck_ls = sS.n

        def vln2(h, c=ck_ls):
            h.wait_ge(sS.h, c)
            h.reciprocal(SPRB[:], SPRB[:]).then_inc(sV.h, 1)
        kb.v(vln2)
        sV.n += 1
        ck_lr = sV.n

        def tbc(h, c=ck_lr):
            h.wait_ge(sV.h, c)
            h.matmul(Q1[:, 0:512], kb.co1f[:], CFL[0:1, 0:512], start=True,
                     stop=True)
            h.matmul(Q1[:, 512:1024], kb.co1f[:], SPRB[:], start=True,
                     stop=True).then_inc(sT.h, 1)
        kb.t(tbc)
        sT.n += 1
        ck_lb = sT.n

        for dd in range(8):
            def vap(h, dd=dd, sc2=sc2, c=ck_lb):
                h.wait_ge(sT.h, c)
                h.tensor_tensor(SQT[:], SM[:, dd * L + sc2 * 512:dd * L + sc2 * 512 + 512],
                                Q1[:, 0:512], OP.subtract)
                h.tensor_tensor(SQ2[:], SQT[:], Q1[:, 512:1024], OP.mult)
                h.tensor_scalar(SM[:, dd * L + sc2 * 512:dd * L + sc2 * 512 + 512],
                                SQ2[:], kb.clnw[:, dd:dd + 1],
                                kb.clnb[:, dd:dd + 1], OP.mult,
                                OP.add).then_inc(sV.h, 1)
            kb.v(vap)
            sV.n += 1


def emit_moe(kb, mcx):
    nc, di = kb.nc, kb.di
    dG, dS, sT, sV, sS = kb.dG, kb.dS, kb.sT, kb.sV, kb.sS
    Q0, Q1, Q2, Q3 = kb.Q
    XN, SM = kb.XN, kb.SM
    sbl = lambda n, s, d: mcx.enter_context(nc.sbuf_tensor(n, s, d))
    GAC = sbl("GAC", [128, 8 * L], BF16)
    HS = sbl("HS", [128, 16 * 1024], BF16)
    W1B = sbl("W1B", [128, 3 * 1024], BF16)
    W2B = sbl("W2B", [128, 2 * 2048], BF16)
    RWB = sbl("RWB", [128, 64], BF16)
    LGF = sbl("LGF", [8, L], F32)
    LGT = sbl("LGT", [128, 128], F32)
    GFM = sbl("GFM", [8, L], BF16)
    GFL = sbl("GFL", [1, 8 * L], BF16)
    GBC = sbl("GBC", [128, L], BF16)
    TPS = sbl("TPS_m", [128, 16], F32)
    TPR = sbl("TPR_m", [128, 16], F32)
    GE8 = sbl("GE8_m", [128, 16], F32)
    EXM = sbl("EXM_m", [128, 16], F32)
    SQM = sbl("SQM", [128, L], BF16)
    XLD = GBC
    Q2b = Q2[:, :].bitcast(BF16)

    v0, s0 = sV.n, sS.n

    def g0(h, v0=v0, s0=s0):
        h.wait_ge(sV.h, v0)
        h.wait_ge(sS.h, s0)
        h.memset(GAC[:], 0.0).then_inc(kb.gC.h, 1)
        src = bass.AP(di["routw"].tensor, 0, [[8, 128], [1024, 8], [1, 8]])
        h.dma_start(out=RWB[:], in_=src).then_inc(dG.h, 16)
    kb.g(g0)
    dG.n += 16
    kb.gC.n += 1
    ck_rw = dG.n
    ck_gmset = kb.gC.n

    q0g = [(sS, 0), (sS, 0)]
    for tc in range(4):
        gate = q0g[tc % 2]

        def t0(h, tc=tc, c=ck_rw, gate=gate):
            h.wait_ge(dG.h, c)
            if gate[1]:
                h.wait_ge(gate[0].h, gate[1])
            for k in range(8):
                mm = h.matmul(Q0[0:8, (tc % 2) * 512:(tc % 2) * 512 + 512],
                              RWB[:, k * 8:(k + 1) * 8],
                              XN[:, k * L + tc * 512:k * L + tc * 512 + 512],
                              start=(k == 0), stop=(k == 7))
            mm.then_inc(sT.h, 1)
        kb.t(t0)
        sT.n += 1
        ck_mm = sT.n

        def s0e(h, tc=tc, c=ck_mm):
            h.wait_ge(sT.h, c)
            h.copy(LGF[:, tc * 512:(tc + 1) * 512],
                   Q0[0:8, (tc % 2) * 512:(tc % 2) * 512 + 512]).then_inc(sS.h, 1)
        kb.s(s0e)
        sS.n += 1
        q0g[tc % 2] = (sS, sS.n)
    ck_lgf = sS.n

    def t1(h, c=ck_lgf):
        h.wait_ge(sS.h, c)
        for tt in range(16):
            mm = h.transpose(Q1[:, tt * 8:(tt + 1) * 8],
                             LGF[0:8, tt * 128:(tt + 1) * 128], kb.cidf[0:8, 0:8])
        mm.then_inc(sT.h, 1)
    kb.t(t1)
    sT.n += 1
    ck_tr = sT.n

    def s1(h, c=ck_tr):
        h.wait_ge(sT.h, c)
        h.copy(LGT[:], Q1[:, 0:128]).then_inc(sS.h, 1)
    kb.s(s1)
    sS.n += 1
    ck_lgt = sS.n

    gt_cons = [(sS, 0), (sS, 0)]
    for pp in range(8):
        ta, tb = 2 * pp, 2 * pp + 1
        la = LGT[:, ta * 8:(ta + 1) * 8]
        lb = LGT[:, tb * 8:(tb + 1) * 8]
        A = lambda j: TPR[:, j:j + 1]
        B = lambda j: TPR[:, 8 + j:8 + j + 1]

        def vtp(h, la=la, lb=lb, c=ck_lgt):
            h.wait_ge(sS.h, c)
            h.tensor_reduce(TPR[:, 0:1], la, mybir.AxisListType.X, OP.max)
            h.tensor_reduce(TPR[:, 8:9], lb, mybir.AxisListType.X, OP.max)
            h.tensor_tensor(TPS[:, 0:8], la,
                            TPR[:, 0:1].broadcast_to([128, 8]), OP.is_ge)
            h.tensor_tensor(TPS[:, 8:16], lb,
                            TPR[:, 8:9].broadcast_to([128, 8]), OP.is_ge)
            h.tensor_scalar_mul(TPS[:, 0:8], TPS[:, 0:8], -1e30)
            h.tensor_scalar_mul(TPS[:, 8:16], TPS[:, 8:16], -1e30)
            h.tensor_tensor(TPS[:, 0:8], TPS[:, 0:8], la, OP.add)
            h.tensor_tensor(TPS[:, 8:16], TPS[:, 8:16], lb, OP.add)
            h.tensor_reduce(TPR[:, 1:2], TPS[:, 0:8], mybir.AxisListType.X, OP.max)
            h.tensor_reduce(TPR[:, 9:10], TPS[:, 8:16], mybir.AxisListType.X, OP.max)
            h.tensor_tensor(TPR[:, 2:3], TPR[:, 1:2], TPR[:, 0:1], OP.subtract)
            h.tensor_tensor(TPR[:, 10:11], TPR[:, 9:10], TPR[:, 8:9], OP.subtract)
            h.tensor_scalar_mul(TPR[:, 3:4], TPR[:, 0:1], -1.0)
            h.tensor_scalar_mul(TPR[:, 11:12], TPR[:, 8:9],
                                -1.0).then_inc(sV.h, 1)
        kb.v(vtp)
        sV.n += 1
        ck_v1 = sV.n

        def stp(h, la=la, lb=lb, c=ck_v1):
            h.wait_ge(sV.h, c)
            h.activation(TPR[:, 4:5], TPR[:, 2:3], AF.Exp)
            h.activation(TPR[:, 12:13], TPR[:, 10:11], AF.Exp)
            h.activation(EXM[:, 0:8], la, AF.Exp, bias=TPR[:, 3:4], scale=1.0)
            h.activation(EXM[:, 8:16], lb, AF.Exp, bias=TPR[:, 11:12],
                         scale=1.0).then_inc(sS.h, 1)
        kb.s(stp)
        sS.n += 1
        ck_s1 = sS.n

        def vtp2(h, la=la, lb=lb, ta=ta, tb=tb, c=ck_s1):
            h.wait_ge(sS.h, c)
            h.tensor_scalar_add(TPR[:, 5:6], TPR[:, 4:5], 1.0)
            h.tensor_scalar_add(TPR[:, 13:14], TPR[:, 12:13], 1.0)
            h.reciprocal(TPR[:, 6:7], TPR[:, 5:6])
            h.reciprocal(TPR[:, 14:15], TPR[:, 13:14])
            h.tensor_tensor(GE8[:, 0:8], la,
                            TPR[:, 1:2].broadcast_to([128, 8]), OP.is_ge)
            h.tensor_tensor(GE8[:, 8:16], lb,
                            TPR[:, 9:10].broadcast_to([128, 8]), OP.is_ge)
            h.tensor_tensor(TPS[:, 0:8], EXM[:, 0:8], GE8[:, 0:8], OP.mult)
            h.tensor_tensor(TPS[:, 8:16], EXM[:, 8:16], GE8[:, 8:16], OP.mult)
            h.tensor_tensor(TPS[:, 0:8], TPS[:, 0:8],
                            TPR[:, 6:7].broadcast_to([128, 8]), OP.mult)
            h.tensor_tensor(TPS[:, 8:16], TPS[:, 8:16],
                            TPR[:, 14:15].broadcast_to([128, 8]), OP.mult)
            h.tensor_copy(SQM[:, ta * 8:ta * 8 + 8], TPS[:, 0:8])
            h.tensor_copy(SQM[:, tb * 8:tb * 8 + 8],
                          TPS[:, 8:16]).then_inc(sV.h, 1)
        kb.v(vtp2)
        sV.n += 1
        ck_v3 = sV.n

        for tt in (ta, tb):
            gt_gate = gt_cons[tt % 2]

            def t2(h, tt=tt, c=ck_v3, gate=gt_gate):
                h.wait_ge(sV.h, c)
                if gate[1]:
                    h.wait_ge(gate[0].h, gate[1])
                h.transpose(Q2b[0:8, (tt % 2) * 1024:(tt % 2) * 1024 + 128],
                            SQM[:, tt * 8:tt * 8 + 8],
                            kb.cid16[:, :]).then_inc(sT.h, 1)
            kb.t(t2)
            sT.n += 1
            ck_t2 = sT.n

            def s2(h, tt=tt, c=ck_t2):
                h.wait_ge(sT.h, c)
                h.copy(GFM[0:8, tt * 128:(tt + 1) * 128],
                       Q2b[0:8, (tt % 2) * 1024:(tt % 2) * 1024 + 128]).then_inc(sS.h, 1)
            kb.s(s2)
            sS.n += 1
            gt_cons[tt % 2] = (sS, sS.n)
    ck_gfm = sS.n

    def yfl(h, c=ck_gfm):
        h.wait_ge(sS.h, c)
        h.dma_start(out=bass.AP(GFL, 0, [[8 * L, 1], [L, 8], [1, L]]),
                    in_=GFM[:]).then_inc(dS.h, 16)
    kb.y(yfl)
    dS.n += 16
    ck_gfl = dS.n

    hs_gate = (sV, 0)
    gbc_gate = (sV, 0)
    w1_cons = [(sV, 0)] * 16
    w2_mm = [(sV, 0)] * 8
    q0m = [(sS, ck_lgf), (sS, ck_lgf)]
    q1m = [(sV, 0), (sV, 0)]
    for e in range(NEXP):
        rnd_gate = gbc_gate
        for rnd in range(2):
            def tgb(h, e=e, rnd=rnd, c=ck_gfl, gb=rnd_gate):
                h.wait_ge(dS.h, c)
                if gb[1]:
                    h.wait_ge(gb[0].h, gb[1])
                for j in range(2):
                    tc = rnd * 2 + j
                    mm = h.matmul(Q3[:, j * 512:(j + 1) * 512], kb.co1b[:],
                                  GFL[0:1, e * L + tc * 512:e * L + tc * 512 + 512],
                                  start=True, stop=True)
                mm.then_inc(sT.h, 1)
            kb.t(tgb)
            sT.n += 1
            ck_gb = sT.n

            def sgb(h, rnd=rnd, c=ck_gb):
                h.wait_ge(sT.h, c)
                h.copy(GBC[:, rnd * 1024:(rnd + 1) * 1024],
                       Q3[:, :]).then_inc(sS.h, 1)
            kb.s(sgb)
            sS.n += 1
            rnd_gate = (sS, sS.n)
        ck_gbc = sS.n

        for half in range(2):
            ht0 = half * 1024
            for hh in range(16):
                gate = w1_cons[hh - 3] if hh >= 3 else w1_cons[[15, 13, 14][hh]]

                dx = kb.dX[hh % 4]

                def gw1(h, e=e, hh=hh, gate=gate, dx=dx):
                    if gate[1]:
                        h.wait_ge(gate[0].h, gate[1])
                    src = bass.AP(di["w1"].tensor,
                                  (e * 16 + hh) * 128 * 1024,
                                  [[1024, 128], [1, 1024]])
                    h.dma_start(out=W1B[:, (hh % 3) * 1024:(hh % 3) * 1024 + 1024],
                                in_=src).then_inc(dx.h, 16)
                kb.g(gw1)
                dx.n += 16
                ck_w1 = (dx, dx.n)

                for tt2 in range(2):
                    def th(h, hh=hh, tt2=tt2, ht0=ht0, c=ck_w1, gate=q0m[tt2]):
                        h.wait_ge(c[0].h, c[1])
                        if gate[1]:
                            h.wait_ge(gate[0].h, gate[1])
                        for k in range(8):
                            mm = h.matmul(Q0[:, tt2 * 512:(tt2 + 1) * 512],
                                          W1B[:, (hh % 3) * 1024 + k * 128:(hh % 3) * 1024 + (k + 1) * 128],
                                          XN[:, k * L + ht0 + tt2 * 512:k * L + ht0 + tt2 * 512 + 512],
                                          start=(k == 0), stop=(k == 7))
                        mm.then_inc(sT.h, 1)
                    kb.t(th)
                    sT.n += 1
                    ck_h = sT.n

                    def sh(h, hh=hh, tt2=tt2, c=ck_h, hg=hs_gate):
                        h.wait_ge(sT.h, c)
                        if hg[1]:
                            h.wait_ge(hg[0].h, hg[1])
                        h.activation(HS[:, hh * 1024 + tt2 * 512:hh * 1024 + tt2 * 512 + 512],
                                     Q0[:, tt2 * 512:(tt2 + 1) * 512],
                                     AF.Relu).then_inc(sS.h, 1)
                    kb.s(sh)
                    sS.n += 1
                    q0m[tt2] = (sS, sS.n)
                w1_cons[hh] = (sT, sT.n)
                ck_hh = sS.n

                def vhs(h, hh=hh, c=ck_hh):
                    h.wait_ge(sS.h, c)
                    h.tensor_tensor(HS[:, hh * 1024:(hh + 1) * 1024],
                                    HS[:, hh * 1024:(hh + 1) * 1024],
                                    HS[:, hh * 1024:(hh + 1) * 1024],
                                    OP.mult).then_inc(sV.h, 1)
                kb.v(vhs)
                sV.n += 1
            ck_sq = sV.n

            for dd in range(8):
                gate = w2_mm[dd - 2] if dd >= 2 else w2_mm[6 + dd]

                dx = kb.dX[dd % 4]

                def gw2(h, e=e, dd=dd, gate=gate, dx=dx):
                    if gate[1]:
                        h.wait_ge(gate[0].h, gate[1])
                    src = bass.AP(di["w2"].tensor,
                                  (e * 8 + dd) * 128 * 2048,
                                  [[2048, 128], [1, 2048]])
                    h.dma_start(out=W2B[:, (dd % 2) * 2048:(dd % 2) * 2048 + 2048],
                                in_=src).then_inc(dx.h, 16)
                kb.g(gw2)
                dx.n += 16
                ck_w2 = (dx, dx.n)

                for tt2 in range(2):
                    def ty(h, dd=dd, tt2=tt2, c=ck_w2, cq=ck_sq, gate=q1m[tt2]):
                        h.wait_ge(c[0].h, c[1])
                        h.wait_ge(sV.h, cq)
                        if gate[1]:
                            h.wait_ge(gate[0].h, gate[1])
                        for k in range(16):
                            mm = h.matmul(Q1[:, tt2 * 512:(tt2 + 1) * 512],
                                          W2B[:, (dd % 2) * 2048 + k * 128:(dd % 2) * 2048 + (k + 1) * 128],
                                          HS[:, k * 1024 + tt2 * 512:k * 1024 + tt2 * 512 + 512],
                                          start=(k == 0), stop=(k == 15))
                        mm.then_inc(sT.h, 1)
                    kb.t(ty)
                    sT.n += 1
                    ck_y = sT.n

                    def vy2(h, dd=dd, tt2=tt2, ht0=ht0, c=ck_y, cg=ck_gbc,
                            cm=ck_gmset):
                        h.wait_ge(kb.gC.h, cm)
                        h.wait_ge(sT.h, c)
                        h.wait_ge(sS.h, cg)
                        h.tensor_tensor(SQM[:, 0:512], Q1[:, tt2 * 512:(tt2 + 1) * 512],
                                        GBC[:, ht0 + tt2 * 512:ht0 + tt2 * 512 + 512],
                                        OP.mult)
                        tgt = GAC[:, dd * L + ht0 + tt2 * 512:dd * L + ht0 + tt2 * 512 + 512]
                        h.tensor_tensor(tgt, tgt, SQM[:, 0:512],
                                        OP.add).then_inc(sV.h, 1)
                    kb.v(vy2)
                    sV.n += 1
                    q1m[tt2] = (sV, sV.n)
                w2_mm[dd] = (sT, sT.n)
            hs_gate = (sT, sT.n)
        gbc_gate = (sV, sV.n)

    kb.tap_raw("lgt", LGT[:])
    kb.tap_raw("sqmg", SQM[:])
    kb.tap_raw("gfm", GFM[:])
    kb.tap_raw("gac", GAC[:])

    ck_moe_v = sV.n
    xld_gate = (sV, ck_moe_v)
    for dd in range(8):
        def gx(h, dd=dd, gate=xld_gate):
            if gate[1]:
                h.wait_ge(gate[0].h, gate[1])
            h.dma_start(out=XLD[:], in_=di["x0"][dd * 128:(dd + 1) * 128, :]).then_inc(dG.h, 16)
        kb.g(gx)
        dG.n += 16
        ck_x = dG.n

        def scb(h, dd=dd, c=ck_moe_v, pv=xld_gate):
            h.wait_ge(sV.h, c)
            if pv[1]:
                h.wait_ge(pv[0].h, pv[1])
            h.activation(SQM[:], GAC[:, dd * L:(dd + 1) * L],
                         AF.Silu).then_inc(sS.h, 1)
        kb.s(scb)
        sS.n += 1
        ck_g = sS.n

        def vcb(h, dd=dd, c=ck_g, cx=ck_x):
            h.wait_ge(sS.h, c)
            h.wait_ge(dG.h, cx)
            h.tensor_tensor(SQM[:], SM[:, dd * L:(dd + 1) * L], SQM[:], OP.mult)
            h.tensor_tensor(SM[:, dd * L:(dd + 1) * L], XLD[:], SQM[:],
                            OP.add).then_inc(sV.h, 1)
        kb.v(vcb)
        sV.n += 1
        xld_gate = (sV, sV.n)
        ck_cb = sV.n

        def gpark(h, dd=dd, c=ck_cb):
            h.wait_ge(sV.h, c)
            h.dma_start(out=kb.dout[dd * 128:(dd + 1) * 128, :],
                        in_=SM[:, dd * L:(dd + 1) * L]).then_inc(kb.dX[3].h, 16)
        kb.g(gpark)
        kb.dX[3].n += 16
    kb.x1_park_ck = kb.dX[3].n


def emit_attn(kb, acx, S1):
    nc, di = kb.nc, kb.di
    dG, dS, sT, sV, sS = kb.dG, kb.dS, kb.sT, kb.sV, kb.sS
    Q0, Q1, Q2, Q3 = kb.Q
    XN = kb.XN
    sbl = lambda n, s, d: acx.enter_context(nc.sbuf_tensor(n, s, d))
    QP = sbl("QP", [128, 8 * L], BF16)
    KVT = sbl("KVT", [128, 2048], BF16)
    BD = sbl("BD", [128, 8 * 128], BF16)
    KSEL = sbl("KSEL", [128, 128], BF16)
    KSR = sbl("KSR", [1, 1024], F32)
    KST = sbl("KST", [128, 8], F32)
    NRM = sbl("NRM", [16, 512], F32)
    NRB = sbl("NRB", [16, 512], BF16)
    SC1 = sbl("SC1", [128, 512], BF16)
    SC2 = sbl("SC2", [128, 512], BF16)
    SC3 = sbl("SC3", [128, 512], BF16)
    crep = sbl("crep", [16, 1024], BF16)
    kb.crep = crep

    def gcr(h):
        h.dma_start(out=crep[:], in_=di["repsel"]).then_inc(dG.h, 16)
    kb.g(gcr)
    dG.n += 16

    with contextlib.ExitStack() as wcx:
        WQ = wcx.enter_context(nc.sbuf_tensor("WQ", [128, 8 * 3072], BF16))
        v0, s0 = sV.n, sS.n

        def gq(h, v0=v0, s0=s0):
            h.wait_ge(sV.h, v0)
            h.wait_ge(sS.h, s0)
            for k in range(8):
                h.dma_start(out=WQ[:, k * 3072:(k + 1) * 3072],
                            in_=di["wqkv"][k * 128:(k + 1) * 128, :]).then_inc(dG.h, 16)
        kb.g(gq)
        dG.n += 8 * 16
        ck_wq = dG.n

        qslot = [(sV, 0), (sV, 0)]
        for qq in range(8):
            for tc in range(4):
                slot = (qq * 4 + tc) % 2
                gate = qslot[slot]

                def tq(h, qq=qq, tc=tc, slot=slot, gate=gate, c=ck_wq):
                    h.wait_ge(dG.h, c)
                    if gate[1]:
                        h.wait_ge(gate[0].h, gate[1])
                    for k in range(8):
                        mm = h.matmul(Q0[:, slot * 512:(slot + 1) * 512],
                                      WQ[:, k * 3072 + qq * 128:k * 3072 + (qq + 1) * 128],
                                      XN[:, k * L + tc * 512:k * L + tc * 512 + 512],
                                      start=(k == 0), stop=(k == 7))
                    mm.then_inc(sT.h, 1)
                kb.t(tq)
                sT.n += 1
                ck_q = sT.n

                def vq(h, slot=slot, c=ck_q, ps=sS.n):
                    h.wait_ge(sT.h, c)
                    h.wait_ge(sS.h, ps)
                    h.tensor_scalar(SC1[:], Q0[:, slot * 512:(slot + 1) * 512],
                                    0.0, None, OP.min)
                    h.tensor_scalar(SC2[:], Q0[:, slot * 512:(slot + 1) * 512],
                                    0.0, None, OP.max).then_inc(sV.h, 1)
                kb.v(vq)
                sV.n += 1
                ck_qv = sV.n
                qslot[slot] = (sV, ck_qv)

                def sq(h, c=ck_qv):
                    h.wait_ge(sV.h, c)
                    h.activation(SC3[:], SC1[:], AF.Exp).then_inc(sS.h, 1)
                kb.s(sq)
                sS.n += 1
                ck_qs = sS.n

                def vq2(h, qq=qq, tc=tc, c=ck_qs):
                    h.wait_ge(sS.h, c)
                    h.tensor_tensor(QP[:, qq * L + tc * 512:qq * L + tc * 512 + 512],
                                    SC3[:], SC2[:], OP.add).then_inc(sV.h, 1)
                kb.v(vq2)
                sV.n += 1
        ck_qp = sV.n

        kvt_gate = (sT, 0)
        for tt in range(16):
            def tkv(h, tt=tt, c=ck_wq, cq=ck_qp, pv=sV.n, ps=sS.n):
                if tt == 0:
                    h.wait_ge(sV.h, cq)
                h.wait_ge(sV.h, pv)
                h.wait_ge(sS.h, ps)
                for k in range(8):
                    for j in range(2):
                        h.matmul(Q1[:, j * 512:(j + 1) * 512],
                                 XN[:, k * L + tt * 128:k * L + (tt + 1) * 128],
                                 WQ[:, k * 3072 + 1024 + j * 512:k * 3072 + 1024 + (j + 1) * 512],
                                 start=(k == 0), stop=(k == 7))
                        mm = h.matmul(Q2[:, j * 512:(j + 1) * 512],
                                      XN[:, k * L + tt * 128:k * L + (tt + 1) * 128],
                                      WQ[:, k * 3072 + 2048 + j * 512:k * 3072 + 2048 + (j + 1) * 512],
                                      start=(k == 0), stop=(k == 7))
                mm.then_inc(sT.h, 1)
            kb.t(tkv)
            sT.n += 1
            ck_kv = sT.n

            def vkf(h, c=ck_kv, kg=kvt_gate):
                h.wait_ge(sT.h, c)
                if kg[1]:
                    h.wait_ge(kg[0].h, kg[1])
                h.tensor_scalar(SC1[:], Q1[:, 0:512], 0.0, None, OP.min)
                h.tensor_scalar(KVT[:, 0:512], Q1[:, 0:512], 0.0, None, OP.max)
                h.tensor_copy(SC2[:, 0:512], SC1[:])
                h.tensor_scalar(SC1[:], Q1[:, 512:1024], 0.0, None, OP.min)
                h.tensor_scalar(KVT[:, 512:1024], Q1[:, 512:1024], 0.0, None,
                                OP.max)
                h.tensor_copy(SC3[:, 0:512], SC1[:]).then_inc(sV.h, 1)
            kb.v(vkf)
            sV.n += 1
            ck_kf = sV.n

            def skf(h, c=ck_kf):
                h.wait_ge(sV.h, c)
                h.activation(SC2[:], SC2[:], AF.Exp)
                h.activation(SC3[:], SC3[:], AF.Exp)
                h.copy(KVT[:, 1024:1536], Q2[:, 0:512])
                h.copy(KVT[:, 1536:2048], Q2[:, 512:1024]).then_inc(sS.h, 1)
            kb.s(skf)
            sS.n += 1
            ck_ks = sS.n

            def vka(h, c=ck_ks):
                h.wait_ge(sS.h, c)
                h.tensor_tensor(KVT[:, 0:512], KVT[:, 0:512], SC2[:], OP.add)
                h.tensor_tensor(KVT[:, 512:1024], KVT[:, 512:1024], SC3[:],
                                OP.add).then_inc(sV.h, 1)
            kb.v(vka)
            sV.n += 1
            ck_ka = sV.n

            def tac(h, tt=tt, c=ck_ka):
                h.wait_ge(sV.h, c)
                for hh in range(16):
                    pr = (hh % 2) * 64
                    h.matmul(Q3[pr:pr + 64, (hh // 2) * 64:(hh // 2 + 1) * 64],
                             KVT[:, hh * 64:(hh + 1) * 64],
                             KVT[:, 1024 + hh * 64:1024 + (hh + 1) * 64],
                             start=(tt == 0 and hh == 0),
                             stop=(tt == 15 and hh == 15),
                             skip_group_check=True)
                h.matmul(Q0[0:1, 0:512], kb.cocol[:], KVT[:, 0:512],
                         start=(tt == 0), stop=(tt == 15))
                h.matmul(Q0[0:1, 512:1024], kb.cocol[:], KVT[:, 512:1024],
                         start=(tt == 0), stop=(tt == 15)).then_inc(sT.h, 1)
            kb.t(tac)
            sT.n += 1
            kvt_gate = (sT, sT.n)
        ck_acc = sT.n

        def vbd(h, c=ck_acc):
            h.wait_ge(sT.h, c)
            h.memset(BD[:], 0.0)
            h.memset(KSEL[:], 0.0).then_inc(sV.h, 1)
        kb.v(vbd)
        sV.n += 1
        ck_bd0 = sV.n

        def sbd(h, c=ck_bd0):
            h.wait_ge(sV.h, c)
            for hh in range(16):
                pr = (hh % 2) * 64
                h.copy(BD[pr:pr + 64, (hh // 2) * 128 + pr:(hh // 2) * 128 + pr + 64],
                       Q3[pr:pr + 64, (hh // 2) * 64:(hh // 2 + 1) * 64])
            h.copy(KSR[:], Q0[0:1, 0:1024]).then_inc(sS.h, 1)
        kb.s(sbd)
        sS.n += 1
        ck_bd = sS.n

        def tks(h, c=ck_bd):
            h.wait_ge(sS.h, c)
            for k in range(8):
                mm = h.transpose(Q1[:, k:k + 1], KSR[0:1, k * 128:(k + 1) * 128],
                                 kb.cidf[0:1, 0:1])
            mm.then_inc(sT.h, 1)
        kb.t(tks)
        sT.n += 1
        ck_kt = sT.n

        def skt(h, c=ck_kt):
            h.wait_ge(sT.h, c)
            h.copy(KST[:], Q1[:, 0:8]).then_inc(sS.h, 1)
        kb.s(skt)
        sS.n += 1
        ck_kst = sS.n

        def vks(h, c=ck_kst):
            h.wait_ge(sS.h, c)
            for k in range(8):
                h.tensor_copy(KSEL[0:64, k * 16 + 2 * k:k * 16 + 2 * k + 1],
                              KST[0:64, k:k + 1])
                mm = h.tensor_copy(KSEL[64:128, k * 16 + 2 * k + 1:k * 16 + 2 * k + 2],
                                   KST[64:128, k:k + 1])
            mm.then_inc(sV.h, 1)
        kb.v(vks)
        sV.n += 1
        ck_ksel = sV.n

    with contextlib.ExitStack() as pcx:
        AO = pcx.enter_context(nc.sbuf_tensor("AO", [128, 8 * L], BF16))
        WP = pcx.enter_context(nc.sbuf_tensor("WP", [128, 8 * 1024], BF16))

        def gp(h, c=ck_acc, cv=sV.n, cs=sS.n):
            h.wait_ge(sT.h, c)
            h.wait_ge(sV.h, cv)
            h.wait_ge(sS.h, cs)
            for k in range(8):
                h.dma_start(out=WP[:, k * 1024:(k + 1) * 1024],
                            in_=di["wprj"][k * 128:(k + 1) * 128, :]).then_inc(dG.h, 16)
        kb.g(gp)
        dG.n += 8 * 16
        ck_wp = dG.n

        aslot = [(sV, 0), (sV, 0)]
        rslot = [(sS, 0), (sS, 0)]
        prev_nr = 0
        prev_tat = 0
        for tc in range(4):
            def tnm(h, tc=tc, c=ck_ksel, prevnrm=rslot[0]):
                h.wait_ge(sV.h, c)
                if prevnrm[1]:
                    h.wait_ge(prevnrm[0].h, prevnrm[1])
                for k in range(8):
                    mm = h.matmul(Q0[0:16, 0:512], KSEL[:, k * 16:(k + 1) * 16],
                                  QP[:, k * L + tc * 512:k * L + tc * 512 + 512],
                                  start=(k == 0), stop=(k == 7))
                mm.then_inc(sT.h, 1)
            kb.t(tnm)
            sT.n += 1
            ck_nm = sT.n

            def snm(h, c=ck_nm, pv=prev_nr):
                h.wait_ge(sT.h, c)
                if pv:
                    h.wait_ge(sV.h, pv)
                h.activation(NRM[:], Q0[0:16, 0:512], AF.Copy, bias=1e-6,
                             scale=1.0).then_inc(sS.h, 1)
            kb.s(snm)
            sS.n += 1
            ck_ne = sS.n
            rslot[0] = (sS, ck_ne)

            def vnm(h, c=ck_ne, pt=prev_tat):
                h.wait_ge(sS.h, c)
                if pt:
                    h.wait_ge(sT.h, pt)
                h.reciprocal(NRM[:], NRM[:])
                h.tensor_copy(NRB[:], NRM[:]).then_inc(sV.h, 1)
            kb.v(vnm)
            sV.n += 1
            ck_nr = sV.n
            prev_nr = ck_nr

            for qq in range(8):
                slot = qq % 2
                gate = aslot[slot]

                def tat(h, qq=qq, tc=tc, slot=slot, gate=gate, c=ck_nr):
                    h.wait_ge(sV.h, c)
                    if gate[1]:
                        h.wait_ge(gate[0].h, gate[1])
                    h.matmul(Q1[:, slot * 512:(slot + 1) * 512],
                             BD[:, qq * 128:(qq + 1) * 128],
                             QP[:, qq * L + tc * 512:qq * L + tc * 512 + 512],
                             start=True, stop=True)
                    h.matmul(Q2[:, slot * 512:(slot + 1) * 512],
                             kb.crep[:, qq * 128:(qq + 1) * 128], NRB[:],
                             start=True, stop=True).then_inc(sT.h, 1)
                kb.t(tat)
                sT.n += 1
                ck_at = sT.n

                def sat(h, slot=slot, c=ck_at):
                    h.wait_ge(sT.h, c)
                    h.copy(SC1[:], Q2[:, slot * 512:(slot + 1) * 512]).then_inc(sS.h, 1)
                kb.s(sat)
                sS.n += 1
                ck_rb = sS.n

                def vat(h, qq=qq, tc=tc, slot=slot, c=ck_rb):
                    h.wait_ge(sS.h, c)
                    h.tensor_tensor(AO[:, qq * L + tc * 512:qq * L + tc * 512 + 512],
                                    Q1[:, slot * 512:(slot + 1) * 512], SC1[:],
                                    OP.mult).then_inc(sV.h, 1)
                kb.v(vat)
                sV.n += 1
                aslot[slot] = (sV, sV.n)
            prev_tat = sT.n
        ck_ao = sV.n

        pslot = [(sS, 0), (sS, 0)]
        for dd in range(8):
            for tc in range(4):
                slot = (dd * 4 + tc) % 2
                gate = pslot[slot]

                def tpj(h, dd=dd, tc=tc, slot=slot, gate=gate, c=ck_ao, cw=ck_wp):
                    h.wait_ge(dG.h, cw)
                    h.wait_ge(sV.h, c)
                    if gate[1]:
                        h.wait_ge(gate[0].h, gate[1])
                    for k in range(8):
                        mm = h.matmul(Q3[:, slot * 512:(slot + 1) * 512],
                                      WP[:, k * 1024 + dd * 128:k * 1024 + (dd + 1) * 128],
                                      AO[:, k * L + tc * 512:k * L + tc * 512 + 512],
                                      start=(k == 0), stop=(k == 7))
                    mm.then_inc(sT.h, 1)
                kb.t(tpj)
                sT.n += 1
                ck_pj = sT.n

                def spj(h, dd=dd, tc=tc, slot=slot, c=ck_pj,
                        cp=getattr(kb, "x1_park_ck", 0)):
                    h.wait_ge(sT.h, c)
                    if cp:
                        h.wait_ge(kb.dX[3].h, cp)
                    h.tensor_scalar(S1[:, dd * L + tc * 512:dd * L + tc * 512 + 512],
                                    Q3[:, slot * 512:(slot + 1) * 512],
                                    kb.cbprj[:, dd:dd + 1], None,
                                    OP.add).then_inc(sV.h, 1)
                kb.v(spj)
                sV.n += 1
                pslot[slot] = (sV, sV.n)


def emit_ffn(kb, fcx, S1):
    nc, di = kb.nc, kb.di
    dG, dS, sT, sV, sS = kb.dG, kb.dS, kb.sT, kb.sV, kb.sS
    Q0, Q1, Q2, Q3 = kb.Q
    XN = kb.XN
    I8 = mybir.dt.int8
    sbl = lambda n, s, d: fcx.enter_context(nc.sbuf_tensor(n, s, d))
    TFC = sbl("TFC", [128, 8 * FH], I8)
    TPR = sbl("TPR", [128, 32 * 1024], I8)
    SCF = sbl("SCF", [128, 32], F32)
    SCP = sbl("SCP", [128, 8], F32)
    UTS = sbl("UTS", [16, FH], BF16)
    FBT = sbl("FBT", [16, 16], BF16)
    VTB = sbl("VTB", [128, 128], BF16)
    GCL = sbl("GCL", [16, 1], F32)

    with contextlib.ExitStack() as pcx:
        psb = lambda n, s, d: pcx.enter_context(nc.sbuf_tensor(n, s, d))
        WFB = psb("WFB", [128, FH], BF16)
        WFA = psb("WFA", [128, FH], BF16)
        SROW = psb("SROW", [1, FH + 1024], F32)
        HBC = psb("HBC", [128, 512], BF16)
        NHB = psb("NHB", [128, 512], BF16)
        HB2 = psb("HB2", [128, 1024], BF16)
        NH2 = psb("NH2", [128, 1024], BF16)
        UTB = psb("UTB", [16, FH], BF16)
        RSC = psb("RSC", [1, FH], BF16)
        WFAi = WFA[:, :].bitcast(I8)

        v0, s0 = sV.n, sS.n

        def g0(h, v0=v0, s0=s0):
            h.wait_ge(sV.h, v0)
            h.wait_ge(sS.h, s0)
            src = bass.AP(di["vt"].tensor, 0, [[16, 128], [16 * 128, 8], [1, 16]])
            h.dma_start(out=VTB[:], in_=src).then_inc(dG.h, 16)
            h.dma_start(out=FBT[:], in_=di["fbt"]).then_inc(dG.h, 16)
            h.dma_start(out=UTB[:], in_=di["ut"]).then_inc(dG.h, 16)
        kb.g(g0)
        dG.n += 48
        ck_small = dG.n

        gsb = psb("GSB", [1, 1], F32)

        def g0b(h):
            h.dma_start(out=gsb[:], in_=di["gsc"]).then_inc(dG.h, 16)
        kb.g(g0b)
        dG.n += 16
        ck_gs = dG.n

        def tg(h, c=ck_gs):
            h.wait_ge(dG.h, c)
            h.matmul(Q0[0:16, 0:1], kb.co1f[0:1, 0:16], gsb[:], start=True,
                     stop=True).then_inc(sT.h, 1)
        kb.t(tg)
        sT.n += 1
        ck_tg = sT.n

        def sg(h, c=ck_tg):
            h.wait_ge(sT.h, c)
            h.copy(GCL[:], Q0[0:16, 0:1]).then_inc(sS.h, 1)
        kb.s(sg)
        sS.n += 1

        wfb_gate = (sV, 0)
        for hh in range(8):
            def gfc(h, hh=hh, g=wfb_gate):
                if g[1]:
                    h.wait_ge(g[0].h, g[1])
                src = bass.AP(di["wfc"].tensor, hh * 512,
                              [[FH, 128], [128 * FH, 8], [1, 512]])
                h.dma_start(out=WFB[:], in_=src).then_inc(dG.h, 16)
            kb.g(gfc)
            dG.n += 16
            ck_w = dG.n

            def vab(h, c=ck_w):
                h.wait_ge(dG.h, c)
                h.activation(WFA[:], WFB[:], AF.Abs).then_inc(sS.h, 1)
            kb.s(vab)
            sS.n += 1
            ck_ab = sS.n

            def tcs2(h, hh=hh, c=ck_ab, pv=sV.n):
                h.wait_ge(sS.h, c)
                h.wait_ge(sV.h, pv)
                for k in range(8):
                    h.matmul(Q1[0:1, 0:512], kb.cocol[:],
                             WFA[:, k * 512:(k + 1) * 512],
                             start=(k == 0), stop=(k == 7))
                for hs in range(4):
                    for k in range(8):
                        mm = h.matmul(Q1[:, 512 + hs:512 + hs + 1],
                                      WFA[:, k * 512 + hs * 128:k * 512 + hs * 128 + 128],
                                      kb.cocol[:], start=(k == 0 and hs == 0),
                                      stop=(k == 7 and hs == 3))
                mm.then_inc(sT.h, 1)
            kb.t(tcs2)
            sT.n += 1
            ck_cs = sT.n

            def vsc(h, hh=hh, c=ck_cs):
                h.wait_ge(sT.h, c)
                h.tensor_scalar(SROW[0:1, hh * 512:(hh + 1) * 512], Q1[0:1, 0:512],
                                1.0 / D, 1e-5, OP.mult, OP.max)
                h.tensor_scalar(SCF[:, hh * 4:(hh + 1) * 4], Q1[:, 512:516],
                                1.0 / D, 1e-5, OP.mult, OP.max).then_inc(sV.h, 1)
            kb.v(vsc)
            sV.n += 1
            ck_sc = sV.n

            def tbc2(h, hh=hh, c=ck_sc):
                h.wait_ge(sV.h, c)
                h.matmul(Q2[:, 0:512], kb.co1f[:],
                         SROW[0:1, hh * 512:(hh + 1) * 512], start=True,
                         stop=True).then_inc(sT.h, 1)
            kb.t(tbc2)
            sT.n += 1
            ck_b = sT.n

            def vhb(h, c=ck_b):
                h.wait_ge(sT.h, c)
                h.tensor_scalar(HBC[:], Q2[:, 0:512], 0.5, None, OP.mult)
                h.tensor_scalar(NHB[:], Q2[:, 0:512], -0.5, None, OP.mult).then_inc(sV.h, 1)
            kb.v(vhb)
            sV.n += 1
            ck_hb = sV.n

            def vtn(h, hh=hh, c=ck_hb):
                rep = HBC[:, :].unsqueeze(1).broadcast_to([128, 8, 512])
                nrep = NHB[:, :].unsqueeze(1).broadcast_to([128, 8, 512])
                w3 = WFB[:, :].rearrange("p (r f) -> p r f", r=8)
                tdst = bass.AP(TFC, hh * 512, [[8 * FH, 128], [FH, 8], [1, 512]])
                h.tensor_tensor(tdst, w3, rep, OP.is_ge)
                h.tensor_tensor(WFAi[:, 0:FH].rearrange("p (r f) -> p r f", r=8),
                                w3, nrep, OP.is_le)
                h.tensor_tensor(tdst, tdst,
                                WFAi[:, 0:FH].rearrange("p (r f) -> p r f", r=8),
                                OP.subtract).then_inc(sV.h, 1)
            kb.v(vtn)
            sV.n += 1
            wfb_gate = (sV, sV.n)
        ck_tfc = sV.n

        def vrs(h, c=ck_tfc):
            with nc.allow_low_precision(reason="intentional bf16 1/scale for u-path"):
                h.reciprocal(RSC[:], SROW[0:1, 0:FH]).then_inc(sV.h, 1)
        kb.v(vrs)
        sV.n += 1
        ck_rs = sV.n

        ub_gate = 0
        for j in range(8):
            def tub(h, j=j, c=ck_rs, g=ub_gate):
                h.wait_ge(sV.h, c)
                if g:
                    h.wait_ge(sV.h, g)
                h.matmul(Q2[0:16, 0:512], kb.co1b[0:1, 0:16],
                         RSC[0:1, j * 512:(j + 1) * 512], start=True,
                         stop=True).then_inc(sT.h, 1)
            kb.t(tub)
            sT.n += 1
            ck_u = sT.n

            def vub(h, j=j, c=ck_u, cs=ck_gs):
                h.wait_ge(sT.h, c)
                h.wait_ge(dG.h, cs)
                h.tensor_tensor(UTS[:, j * 512:(j + 1) * 512],
                                UTB[:, j * 512:(j + 1) * 512], Q2[0:16, 0:512],
                                OP.mult).then_inc(sV.h, 1)
            kb.v(vub)
            sV.n += 1
            ub_gate = sV.n
        ck_uts = sV.n

        def tpa_setup(h):
            pass
        for k in range(32):
            def gpa(h, k=k, g=wfb_gate):
                if g[1]:
                    h.wait_ge(g[0].h, g[1])
                src = bass.AP(di["wpr"].tensor, k * 128 * 1024,
                              [[1024, 128], [1, 1024]])
                h.dma_start(out=WFB[:, 0:1024], in_=src).then_inc(dG.h, 16)
            kb.g(gpa)
            dG.n += 16
            ck_w = dG.n

            def vab2(h, k=k, c=ck_w):
                h.wait_ge(dG.h, c)
                h.activation(WFA[:, 0:1024], WFB[:, 0:1024],
                             AF.Abs).then_inc(sS.h, 1)
            kb.s(vab2)
            sS.n += 1
            ck_ab = sS.n
            wfb_gate = (sS, ck_ab)

            def tca(h, k=k, c=ck_ab):
                h.wait_ge(sS.h, c)
                h.matmul(Q3[0:1, 0:512], kb.cocol[:], WFA[:, 0:512],
                         start=(k == 0), stop=(k == 31))
                h.matmul(Q3[0:1, 512:1024], kb.cocol[:], WFA[:, 512:1024],
                         start=(k == 0), stop=(k == 31))
                for dd in range(8):
                    mm = h.matmul(Q2[:, 512 + dd:512 + dd + 1],
                                  WFA[:, dd * 128:(dd + 1) * 128], kb.cocol[:],
                                  start=(k == 0 and dd == 0),
                                  stop=(k == 31 and dd == 7))
                mm.then_inc(sT.h, 1)
            kb.t(tca)
            sT.n += 1
        ck_pa = sT.n

        def vsp2(h, c=ck_pa):
            h.wait_ge(sT.h, c)
            h.tensor_scalar(SROW[0:1, FH:FH + 1024],
                            Q3[0:1, 0:1024], 1.0 / FH, 1e-5, OP.mult, OP.max)
            h.tensor_scalar(SCP[:], Q2[:, 512:520], 1.0 / FH, 1e-5, OP.mult,
                            OP.max).then_inc(sV.h, 1)
        kb.v(vsp2)
        sV.n += 1
        ck_sp = sV.n

        def tb2(h, c=ck_sp):
            h.wait_ge(sV.h, c)
            h.matmul(Q2[:, 0:512], kb.co1f[:], SROW[0:1, FH:FH + 512],
                     start=True, stop=True)
            h.matmul(Q2[:, 512:1024], kb.co1f[:], SROW[0:1, FH + 512:FH + 1024],
                     start=True, stop=True).then_inc(sT.h, 1)
        kb.t(tb2)
        sT.n += 1
        ck_b2 = sT.n

        def vh2(h, c=ck_b2):
            h.wait_ge(sT.h, c)
            h.tensor_scalar(HB2[:], Q2[:, :], 0.5, None, OP.mult)
            h.tensor_scalar(NH2[:], Q2[:, :], -0.5, None, OP.mult).then_inc(sV.h, 1)
        kb.v(vh2)
        sV.n += 1
        ck_h2 = sV.n

        for k in range(32):
            def gpb(h, k=k, g=wfb_gate):
                if g[1]:
                    h.wait_ge(g[0].h, g[1])
                src = bass.AP(di["wpr"].tensor, k * 128 * 1024,
                              [[1024, 128], [1, 1024]])
                h.dma_start(out=WFB[:, 0:1024], in_=src).then_inc(dG.h, 16)
            kb.g(gpb)
            dG.n += 16
            ck_w = dG.n

            def vtn2(h, k=k, c=ck_w, ch=ck_h2):
                h.wait_ge(dG.h, c)
                h.wait_ge(sV.h, ch)
                tdst = TPR[:, k * 1024:(k + 1) * 1024]
                h.tensor_tensor(tdst, WFB[:, 0:1024], HB2[:], OP.is_ge)
                h.tensor_tensor(WFAi[:, 0:1024], WFB[:, 0:1024], NH2[:], OP.is_le)
                h.tensor_tensor(tdst, tdst, WFAi[:, 0:1024],
                                OP.subtract).then_inc(sV.h, 1)
            kb.v(vtn2)
            sV.n += 1
            wfb_gate = (sV, sV.n)
        ck_tpr = sV.n
        kb.barrier()

    with contextlib.ExitStack() as qcx:
        qsb = lambda n, s, d: qcx.enter_context(nc.sbuf_tensor(n, s, d))
        RR = qsb("RR", [128, 32 * 512], BF16)
        WFE = qsb("WFE", [128, 2 * 2048], BF16)
        WPE = qsb("WPE", [128, 2 * 1024], BF16)
        T1S = qsb("T1S", [16, 512], BF16)
        T2G = qsb("T2G", [16, 512], BF16)
        G1T = qsb("G1T", [128, 512], BF16)
        X2O = qsb("X2O", [128, 512], F32)
        X1L = qsb("X1L", [128, 512], BF16)

        wfe_cons = [(sT, 0), (sT, 0)]
        wpe_mm = [(sT, 0), (sT, 0)]
        x2_gate = (dS, 0)
        x1_gate = (sV, 0)
        for q in range(4):
            qt0 = q * 512
            def tt1(h, qt0=qt0, pv=sV.n, ps=sS.n):
                h.wait_ge(sV.h, pv)
                h.wait_ge(sS.h, ps)
                for k in range(8):
                    mm = h.matmul(Q1[0:16, 0:512], VTB[:, k * 16:(k + 1) * 16],
                                  XN[:, k * L + qt0:k * L + qt0 + 512],
                                  start=(k == 0), stop=(k == 7))
                mm.then_inc(sT.h, 1)
            kb.t(tt1)
            sT.n += 1
            ck_t1 = sT.n

            def st1(h, c=ck_t1):
                h.wait_ge(sT.h, c)
                h.copy(T1S[:], Q1[0:16, 0:512]).then_inc(sS.h, 1)
            kb.s(st1)
            sS.n += 1
            ck_t1s = sS.n

            def tt2(h, c=ck_t1s):
                h.wait_ge(sS.h, c)
                h.matmul(Q1[0:16, 512:1024], FBT[:], T1S[:], start=True,
                         stop=True).then_inc(sT.h, 1)
            kb.t(tt2)
            sT.n += 1
            ck_t2 = sT.n

            def vt2(h, c=ck_t2):
                h.wait_ge(sT.h, c)
                h.tensor_scalar(T2G[:], Q1[0:16, 512:1024], GCL[:], None,
                                OP.mult).then_inc(sV.h, 1)
            kb.v(vt2)
            sV.n += 1
            ck_t2g = sV.n

            q0f = [(sS, 0), (sS, 0)]
            for hh in range(8):
                for hf in range(2):
                    wslot = (hh * 2 + hf) % 2

                    def vcf(h, hh=hh, hf=hf, wslot=wslot, g=wfe_cons[wslot]):
                        if g[1]:
                            h.wait_ge(g[0].h, g[1])
                        src = bass.AP(TFC, hh * 512 + hf * 256,
                                      [[8 * FH, 128], [FH, 8], [1, 256]])
                        h.tensor_copy(
                            WFE[:, wslot * 2048:(wslot + 1) * 2048].rearrange(
                                "p (r f) -> p r f", r=8), src).then_inc(sV.h, 1)
                    kb.v(vcf)
                    sV.n += 1
                    ck_cast = sV.n

                    for hs2 in range(2):
                        hs = hf * 2 + hs2
                        slot = hs % 2
                        gate = q0f[slot]

                        def tf(h, hh=hh, hs=hs, hs2=hs2, slot=slot, qt0=qt0,
                               wslot=wslot, gate=gate, c=ck_cast, ct=ck_t2g):
                            h.wait_ge(sV.h, c)
                            h.wait_ge(sV.h, ct)
                            if gate[1]:
                                h.wait_ge(gate[0].h, gate[1])
                            h.matmul(Q0[:, slot * 512:(slot + 1) * 512],
                                     UTS[:, hh * 512 + hs * 128:hh * 512 + (hs + 1) * 128],
                                     T2G[:], start=True, stop=False)
                            for k in range(8):
                                mm = h.matmul(Q0[:, slot * 512:(slot + 1) * 512],
                                              WFE[:, wslot * 2048 + k * 256 + hs2 * 128:wslot * 2048 + k * 256 + hs2 * 128 + 128],
                                              XN[:, k * L + qt0:k * L + qt0 + 512],
                                              start=False, stop=(k == 7))
                            mm.then_inc(sT.h, 1)
                        kb.t(tf)
                        sT.n += 1
                        ck_f = sT.n

                        def sf(h, hh=hh, hs=hs, slot=slot, c=ck_f):
                            h.wait_ge(sT.h, c)
                            j = hh * 4 + hs
                            h.activation(RR[:, j * 512:(j + 1) * 512],
                                         Q0[:, slot * 512:(slot + 1) * 512],
                                         AF.Relu,
                                         scale=SCF[:, j:j + 1]).then_inc(sS.h, 1)
                        kb.s(sf)
                        sS.n += 1
                        q0f[slot] = (sS, sS.n)
                    wfe_cons[wslot] = (sT, sT.n)
            ck_fc = sS.n

            def vsq2(h, c=ck_fc):
                h.wait_ge(sS.h, c)
                for j in range(8):
                    mm = h.tensor_tensor(RR[:, j * 2048:(j + 1) * 2048],
                                         RR[:, j * 2048:(j + 1) * 2048],
                                         RR[:, j * 2048:(j + 1) * 2048], OP.mult)
                mm.then_inc(sV.h, 1)
            kb.v(vsq2)
            sV.n += 1
            ck_rr = sV.n

            for k in range(32):
                wslot = k % 2

                def vcp(h, k=k, wslot=wslot, g=wpe_mm[wslot]):
                    if g[1]:
                        h.wait_ge(g[0].h, g[1])
                    h.tensor_copy(WPE[:, wslot * 1024:(wslot + 1) * 1024],
                                  TPR[:, k * 1024:(k + 1) * 1024]).then_inc(sV.h, 1)
                kb.v(vcp)
                sV.n += 1
                ck_c = sV.n

                def tpj2(h, k=k, wslot=wslot, c=ck_c, cr=ck_rr,
                         cq0=q0f[0], cq1=q0f[1]):
                    h.wait_ge(sV.h, c)
                    h.wait_ge(sV.h, cr)
                    if k == 0:
                        h.wait_ge(cq0[0].h, cq0[1])
                        h.wait_ge(cq1[0].h, cq1[1])
                    for dd in range(8):
                        mm = h.matmul(kb.Q[dd // 2][:, (dd % 2) * 512:(dd % 2) * 512 + 512],
                                      WPE[:, wslot * 1024 + dd * 128:wslot * 1024 + (dd + 1) * 128],
                                      RR[:, k * 512:(k + 1) * 512],
                                      start=(k == 0), stop=(k == 31))
                    mm.then_inc(sT.h, 1)
                kb.t(tpj2)
                sT.n += 1
                wpe_mm[wslot] = (sT, sT.n)
            ck_pj = sT.n

            g1_gate = (sV, 0)
            for dd in range(8):
                def sg1(h, dd=dd, c=ck_pj, g=g1_gate):
                    h.wait_ge(sT.h, c)
                    if g[1]:
                        h.wait_ge(g[0].h, g[1])
                    h.activation(G1T[:], kb.Q[dd // 2][:, (dd % 2) * 512:(dd % 2) * 512 + 512],
                                 AF.Silu, scale=SCP[:, dd:dd + 1]).then_inc(sS.h, 1)
                kb.s(sg1)
                sS.n += 1
                ck_g1 = sS.n

                def gx1(h, dd=dd, q=q, g=x1_gate):
                    if g[1]:
                        h.wait_ge(g[0].h, g[1])
                    h.dma_start(out=X1L[:],
                                in_=kb.dout[dd * 128:(dd + 1) * 128,
                                            q * 512:(q + 1) * 512]).then_inc(dG.h, 16)
                kb.g(gx1)
                dG.n += 16
                ck_x1 = dG.n

                def vx2(h, dd=dd, q=q, c=ck_g1, cx=ck_x1, g=x2_gate):
                    h.wait_ge(sS.h, c)
                    h.wait_ge(dG.h, cx)
                    if g[1]:
                        h.wait_ge(g[0].h, g[1])
                    h.tensor_tensor(G1T[:], kb.SM[:, dd * L + q * 512:dd * L + q * 512 + 512],
                                    G1T[:], OP.mult)
                    h.tensor_tensor(X2O[:], X1L[:], G1T[:],
                                    OP.add).then_inc(sV.h, 1)
                kb.v(vx2)
                sV.n += 1
                ck_x2 = sV.n
                x1_gate = (sV, ck_x2)
                g1_gate = (sV, ck_x2)

                def yx2(h, dd=dd, q=q, c=ck_x2):
                    h.wait_ge(sV.h, c)
                    h.dma_start(out=kb.dout[dd * 128:(dd + 1) * 128,
                                            q * 512:(q + 1) * 512],
                                in_=X2O[:]).then_inc(dS.h, 16)
                kb.y(yx2)
                dS.n += 16
                x2_gate = (dS, dS.n)



def build_core_program(taps=()):
    nc = bass.Bass("TRN2", target_bir_lowering=False, debug=False,
                   detect_race_conditions=False)
    di = {}

    def dinp(name, shape, dt=F32):
        di[name] = nc.dram_tensor(name, list(shape), dt, kind="ExternalInput").ap()

    dinp("x0", (D, L))
    dinp("win", (D, DIP))
    dinp("wout", (DI, D))
    dinp("convw", (128, NCT * 4))
    dinp("convb", (128, NCT))
    dinp("dtb", (1, NH))
    dinp("alog", (NH, 1))
    dinp("dvec", (1, NH))
    dinp("normw", (128, 16))
    dinp("lnw0", (128, 8))
    dinp("lnb0", (128, 8))
    dinp("routw", (D, NEXP))
    dinp("w1", (NEXP * 16 * 128, 8 * 128))
    dinp("w2", (NEXP * 8 * 128, 16 * 128))
    dinp("wqkv", (D, 3 * D))
    dinp("wprj", (D, D))
    dinp("bprj", (128, 8))
    dinp("wfc", (D, FH))
    dinp("wpr", (FH, D))
    dinp("ut", (16, FH))
    dinp("vt", (D, 16))
    dinp("fbt", (16, 16))
    dinp("gsc", (1, 1))
    dinp("id16", (128, 128), BF16)
    dinp("idf", (128, 128), F32)
    dinp("l1tri", (128, 128), BF16)
    dinp("sub4", (128, 512), BF16)
    dinp("repsel", (16, 1024), BF16)
    dinp("ones1f", (1, 128), F32)
    dinp("ones1b", (1, 128), BF16)
    dinp("onesr512", (1, 512), F32)
    dinp("onescol", (128, 1), BF16)

    dout = nc.dram_tensor("out", [D, L], F32, kind="ExternalOutput").ap()
    tap_d = {}
    for tname, tshape, tdt in taps:
        tap_d[tname] = nc.dram_tensor("tap_" + tname, list(tshape), tdt,
                                      kind="ExternalOutput").ap()

    P = Prog()
    with contextlib.ExitStack() as ctx:
        kb = KB(nc, ctx, P, di, dout, tap_d)
        emit_consts(kb)

        with contextlib.ExitStack() as lnc:
            emit_ln_fm(kb, lnc, kb.XN, src_dram=di["x0"])
            kb.barrier()
        kb.tap_fm("xn", kb.XN)

        with contextlib.ExitStack() as mc:
            emit_mamba(kb, mc)
            kb.barrier()
        kb.tap_fm("sm", kb.SM)

        with contextlib.ExitStack() as mcx:
            emit_moe(kb, mcx)
            kb.barrier()
        kb.tap_fm("x1", kb.SM)

        with contextlib.ExitStack() as lnc2:
            emit_ln_fm(kb, lnc2, kb.XN, src_sbuf=kb.SM)
            kb.barrier()
        kb.tap_fm("xn1", kb.XN)

        with contextlib.ExitStack() as acx:
            emit_attn(kb, acx, kb.SM)
            kb.barrier()
        kb.tap_fm("s1", kb.SM)

        with contextlib.ExitStack() as fcx:
            emit_ffn(kb, fcx, kb.SM)

        fin = kb.dS.n

        def wfin(h, f=fin, dS=kb.dS):
            h.wait_ge(dS.h, f)
        kb.y(wfin)

        block = ctx.enter_context(nc.Block())
        P.flush(block)
    return nc


def _consts_host():
    i = np.arange(128)
    l1 = (i[:, None] <= i[None, :]).astype(BF)
    sub = np.zeros((128, 512), np.float32)
    for rep in range(4):
        for t in range(128):
            if t + 1 < 128:
                sub[t + 1, rep * 128 + t] = NEGBIG
    repsel = np.zeros((16, 1024), np.float32)
    for qq in range(8):
        for j in range(128):
            repsel[2 * qq + j // 64, qq * 128 + j] = 1.0
    return {
        "id16": np.eye(128, dtype=BF),
        "idf": np.eye(128, dtype=np.float32),
        "l1tri": l1,
        "sub4": sub.astype(BF),
        "repsel": repsel.astype(BF),
        "ones1f": np.ones((1, 128), np.float32),
        "ones1b": np.ones((1, 128), BF),
        "onesr512": np.ones((1, 512), np.float32),
        "onescol": np.ones((128, 1), BF),
    }


def _prep_weights(kw):
    f32 = np.float32
    c = lambda a: np.ascontiguousarray(a, dtype=f32)
    d = {}
    d["win"] = c(np.asarray(kw["L0_attn_in_proj_w"]).T)
    d["wout"] = c(np.asarray(kw["L0_attn_out_proj_w"]).T)
    cw = np.asarray(kw["L0_attn_conv_w"], f32)
    d["convw"] = c(cw.reshape(NCT, 128, 4).transpose(1, 0, 2).reshape(128, NCT * 4))
    cb = np.asarray(kw["L0_attn_conv_b"], f32)
    d["convb"] = c(cb.reshape(NCT, 128).T)
    d["dtb"] = c(np.asarray(kw["L0_attn_dt_bias"]).reshape(1, NH))
    d["alog"] = c(np.asarray(kw["L0_attn_A_log"]).reshape(NH, 1))
    d["dvec"] = c(np.asarray(kw["L0_attn_D"]).reshape(1, NH))
    d["normw"] = c(np.asarray(kw["L0_attn_norm_w"]).reshape(16, 128).T)
    d["lnw0"] = c(np.asarray(kw["L0_attn_ln_w"]).reshape(8, 128).T)
    d["lnb0"] = c(np.asarray(kw["L0_attn_ln_b"]).reshape(8, 128).T)
    d["routw"] = c(np.asarray(kw["L0_ffn_router_w"]).T)
    w1 = np.asarray(kw["L0_ffn_w1"], f32)
    d["w1"] = c(w1.reshape(8, 8, 128, 16, 128).transpose(0, 3, 2, 1, 4)
                .reshape(8 * 16 * 128, 8 * 128))
    w2 = np.asarray(kw["L0_ffn_w2"], f32)
    d["w2"] = c(w2.reshape(8, 16, 128, 8, 128).transpose(0, 3, 2, 1, 4)
                .reshape(8 * 8 * 128, 16 * 128))
    d["wqkv"] = c(np.asarray(kw["L1_attn_Wqkv"]).T)
    d["wprj"] = c(np.asarray(kw["L1_attn_Wproj"]).T)
    d["bprj"] = c(np.asarray(kw["L1_attn_bproj"]).reshape(8, 128).T)
    d["wfc"] = c(np.asarray(kw["L1_ffn_w_fc"]).T)
    d["wpr"] = c(np.asarray(kw["L1_ffn_w_proj"]).T)
    d["ut"] = c(np.asarray(kw["L1_ffn_u"]).T)
    d["vt"] = c(np.asarray(kw["L1_ffn_v"]).T)
    d["fbt"] = c(np.asarray(kw["L1_ffn_fast_b"]).T)
    d["gsc"] = c(np.asarray(kw["L1_ffn_gate"]).reshape(1, 1))
    d.update(_consts_host())
    return d


def kernel(**inputs):
    x = np.asarray(inputs["L0_attn_x"], np.float32)
    shared = _prep_weights(inputs)
    nc = build_core_program()
    in_maps = []
    for b in range(8):
        m = dict(shared)
        m["x0"] = np.ascontiguousarray(x[b].T)
        in_maps.append(m)
    res = run_bass_kernel_spmd(nc, in_maps, core_ids=list(range(8)))
    out = np.empty((8, L, D), np.float32)
    for b in range(8):
        out[b] = res.results[b]["out"].T
    return out


# version 17 (final)
# speedup vs baseline: 1.3553x; 1.0368x over previous
import contextlib
import numpy as np
import ml_dtypes
import concourse.bass as bass
import concourse.mybir as mybir
from concourse.alu_op_type import AluOpType as OP
from concourse.bass_utils import run_bass_kernel_spmd

F32 = mybir.dt.float32
BF16 = mybir.dt.bfloat16
AF = mybir.ActivationFunctionType
BF = ml_dtypes.bfloat16

L = 2048
D = 1024
DI = 2048
DS = 128
NH = 32
HP = 64
CD = DI + 2 * DS
NCT = 18
DIP = 4384
CH = 128
SLAB = 256
NSLAB = L // SLAB
HEXP = 2048
NEXP = 8
FH = 4096
NEGBIG = -30000.0


class Sem:
    __slots__ = ("h", "n")

    def __init__(self, h):
        self.h = h
        self.n = 0


class Prog:
    def __init__(self):
        self.q = {k: [] for k in ("tensor", "vector", "scalar", "gpsimd", "sync")}

    def add(self, eng, fn):
        self.q[eng].append(fn)

    def flush(self, block):
        for eng, fns in self.q.items():
            if not fns:
                continue

            def run(h, fns=tuple(fns)):
                for fn in fns:
                    fn(h)
            getattr(block, eng)(run)


class KB:

    def __init__(self, nc, ctx, P, di, dout, tap_d):
        self.nc, self.ctx, self.P, self.di = nc, ctx, P, di
        self.dout, self.tap_d = dout, tap_d
        sb = lambda n, s, d: ctx.enter_context(nc.sbuf_tensor(n, s, d))
        self.sb = sb
        self.XN = sb("XN", [128, 8 * L], BF16)
        self.SM = sb("SM", [128, 8 * L], BF16)
        self.cid16 = sb("cid16", [128, 128], BF16)
        self.cidf = sb("cidf", [128, 128], F32)
        self.cl1 = sb("cl1", [128, 128], BF16)
        self.csub4 = sb("csub4", [128, 512], BF16)
        self.co1f = sb("co1f", [1, 128], F32)
        self.co1b = sb("co1b", [1, 128], BF16)
        self.cor512 = sb("cor512", [1, 512], F32)
        self.cocol = sb("cocol", [128, 1], BF16)
        self.ccvw = sb("ccvw", [128, NCT * 4], F32)
        self.ccvb = sb("ccvb", [128, NCT], F32)
        self.cdtb = sb("cdtb", [1, NH], F32)
        self.calog = sb("calog", [NH, 1], F32)
        self.cdvec = sb("cdvec", [1, NH], F32)
        self.cnormw = sb("cnormw", [128, 16], F32)
        self.clnw = sb("clnw", [128, 8], F32)
        self.clnb = sb("clnb", [128, 8], F32)
        self.cbprj = sb("cbprj", [128, 8], F32)
        self.cacol = sb("cacol", [NH, 1], F32)
        self.cdbc = sb("cdbc", [128, NH], BF16)
        self.bscr = sb("bscr", [1, 8], F32)
        self.spc = sb("spc", [128, 4], F32)
        self.ceps = sb("ceps", [1, 1], F32)
        self.Q = [ctx.enter_context(nc.psum_tensor("q%d" % i, [128, 1024], F32))
                  for i in range(4)]
        self.dG = Sem(ctx.enter_context(nc.semaphore("dG")))
        self.dX = [Sem(ctx.enter_context(nc.semaphore("dX%d" % i)))
                   for i in range(4)]
        self.gC = Sem(ctx.enter_context(nc.semaphore("gC")))
        self.dS = Sem(ctx.enter_context(nc.semaphore("dS")))
        self.sT = Sem(ctx.enter_context(nc.semaphore("sT")))
        self.sV = Sem(ctx.enter_context(nc.semaphore("sV")))
        self.sS = Sem(ctx.enter_context(nc.semaphore("sS")))

    def t(self, fn):
        self.P.add("tensor", fn)

    def v(self, fn):
        self.P.add("vector", fn)

    def s(self, fn):
        self.P.add("scalar", fn)

    def g(self, fn):
        self.P.add("gpsimd", fn)

    def y(self, fn):
        self.P.add("sync", fn)

    def barrier(self):
        sT, sV, sS = self.sT, self.sV, self.sS
        t0, v0, s0 = sT.n, sV.n, sS.n

        def bt(h, v0=v0, s0=s0):
            h.wait_ge(sV.h, v0)
            h.wait_ge(sS.h, s0)
            h.nop().then_inc(sT.h, 1)

        def bv(h, t0=t0, s0=s0, b=self.bscr):
            h.wait_ge(sT.h, t0)
            h.wait_ge(sS.h, s0)
            h.memset(b[0:1, 0:2], 0.0).then_inc(sV.h, 1)

        def bs(h, t0=t0, v0=v0, b=self.bscr):
            h.wait_ge(sT.h, t0)
            h.wait_ge(sV.h, v0)
            h.memzero(b[0:1, 2:4]).then_inc(sS.h, 1)
        self.t(bt)
        self.v(bv)
        self.s(bs)
        sT.n += 1
        sV.n += 1
        sS.n += 1
        tb, vb, ssn = sT.n, sV.n, sS.n

        def bg(h, tb=tb, vb=vb, ssn=ssn):
            h.wait_ge(sT.h, tb)
            h.wait_ge(sV.h, vb)
            h.wait_ge(sS.h, ssn)
        self.g(bg)
        self.y(bg)

    def tap_raw(self, name, src_ap):
        if name not in self.tap_d:
            return
        dG = self.dG
        v0, s0, t0 = self.sV.n, self.sS.n, self.sT.n
        dst = self.tap_d[name]

        def go(h, v0=v0, s0=s0, t0=t0, dst=dst, src=src_ap, sV=self.sV,
               sS=self.sS, sT=self.sT, dG=dG):
            h.wait_ge(sV.h, v0)
            h.wait_ge(sS.h, s0)
            h.wait_ge(sT.h, t0)
            h.dma_start(out=dst, in_=src).then_inc(dG.h, 16)
        self.g(go)
        dG.n += 16
        fin = dG.n

        def wt(h, fin=fin, dG=dG):
            h.wait_ge(dG.h, fin)
        self.g(wt)
        self.t(wt)
        self.v(wt)
        self.s(wt)

    def tap_fm(self, name, src):
        if name not in self.tap_d:
            return
        dG = self.dG
        v0, s0 = self.sV.n, self.sS.n
        dst = self.tap_d[name]

        def go(h, v0=v0, s0=s0, dst=dst, src=src, sV=self.sV, sS=self.sS, dG=dG):
            h.wait_ge(sV.h, v0)
            h.wait_ge(sS.h, s0)
            for k in range(8):
                h.dma_start(out=dst[k * 128:(k + 1) * 128, :],
                            in_=src[:, k * L:(k + 1) * L]).then_inc(dG.h, 16)
        self.g(go)
        dG.n += 128
        fin = dG.n

        def wt(h, fin=fin, dG=dG):
            h.wait_ge(dG.h, fin)
        self.g(wt)



def emit_consts(kb):
    di, dS, sT, sV, sS = kb.di, kb.dS, kb.sT, kb.sV, kb.sS
    cl = [("id16", kb.cid16), ("idf", kb.cidf), ("l1tri", kb.cl1),
          ("sub4", kb.csub4), ("ones1f", kb.co1f), ("ones1b", kb.co1b),
          ("onesr512", kb.cor512), ("onescol", kb.cocol), ("convw", kb.ccvw),
          ("convb", kb.ccvb), ("dtb", kb.cdtb), ("alog", kb.calog),
          ("dvec", kb.cdvec), ("normw", kb.cnormw), ("lnw0", kb.clnw),
          ("lnb0", kb.clnb), ("bprj", kb.cbprj)]

    def p0(h, cl=tuple(cl)):
        for nm, dst in cl:
            h.dma_start(out=dst[:], in_=di[nm]).then_inc(dS.h, 16)
    kb.y(p0)

    dS.n += 16 * len(cl)
    ck_cl = dS.n

    def p0s(h, c=ck_cl, kb=kb):
        h.wait_ge(dS.h, c)
        h.activation(kb.cacol[:], kb.calog[:], AF.Exp).then_inc(sS.h, 1)
    kb.s(p0s)
    sS.n += 1
    ck_ac = sS.n

    def p0v(h, c=ck_ac, kb=kb):
        h.wait_ge(sS.h, c)
        h.tensor_scalar_mul(kb.cacol[:], kb.cacol[:], -1.0).then_inc(sV.h, 1)
    kb.v(p0v)
    sV.n += 1

    def p0t(h, c=ck_cl, kb=kb):
        h.wait_ge(dS.h, c)
        h.matmul(kb.Q[0][:, 0:NH], kb.co1f[:], kb.cdvec[:],
                 start=True, stop=True).then_inc(sT.h, 1)
    kb.t(p0t)
    sT.n += 1
    ck_db = sT.n

    def p0s2(h, c=ck_db, kb=kb):
        h.wait_ge(sT.h, c)
        h.copy(kb.cdbc[:], kb.Q[0][:, 0:NH]).then_inc(sS.h, 1)
    kb.s(p0s2)
    sS.n += 1
    kb.barrier()


_LN_N = [0]


def emit_ln_fm(kb, lnc, dst, src_dram=None, src_sbuf=None):
    nc, dG, dS, sT, sV, sS = kb.nc, kb.dG, kb.dS, kb.sT, kb.sV, kb.sS
    Q = kb.Q
    _LN_N[0] += 1
    sfx = str(_LN_N[0])
    sbl = lambda n, s, d: lnc.enter_context(nc.sbuf_tensor(n + sfx, s, d))
    xsq = sbl("ln_xsq", [128, L], BF16)
    srow = sbl("ln_srow", [1, 3 * L], F32)
    mrb = sbl("ln_mrb", [128, 2 * L], BF16)
    xld = sbl("ln_xld", [128, 2 * L], BF16) if src_dram is not None else None

    def tile_src(k):
        if src_dram is not None:
            return xld[:, (k % 2) * L:(k % 2) * L + L]
        return src_sbuf[:, k * L:(k + 1) * L]

    def emit_load(k, gate_sem, gate_val):
        if src_dram is None:
            return None
        dx = kb.dX[k % 2]

        def ld(h, k=k, gs=gate_sem, gv=gate_val, dx=dx):
            if gs is not None:
                h.wait_ge(gs.h, gv)
            h.dma_start(out=xld[:, (k % 2) * L:(k % 2) * L + L],
                        in_=src_dram[k * 128:(k + 1) * 128, :]).then_inc(dx.h, 16)
        kb.g(ld)
        dx.n += 16
        return (dx, dx.n)

    cons_ck = []
    for k in range(8):
        gate = (sT, cons_ck[k - 2]) if (src_dram is not None and k >= 2) else (None, 0)
        dld = emit_load(k, gate[0], gate[1])
        src = tile_src(k)

        prevmm = cons_ck[k - 1] if k >= 1 else 0

        def s1(h, src=src, dld=dld, pm=prevmm):
            if dld is not None:
                h.wait_ge(dld[0].h, dld[1])
            if pm:
                h.wait_ge(sT.h, pm)
            h.activation(xsq[:], src, AF.Square).then_inc(sS.h, 1)
        kb.s(s1)
        sS.n += 1
        ck_sq = sS.n

        def t1(h, k=k, src=src, dld=dld, c=ck_sq):
            if dld is not None:
                h.wait_ge(dld[0].h, dld[1])
            h.wait_ge(sS.h, c)
            for tc in range(4):
                qa = Q[tc // 2]
                qb = Q[2 + tc // 2]
                off = (tc % 2) * 512
                h.matmul(qa[0:1, off:off + 512], kb.cocol[:],
                         src[:, tc * 512:(tc + 1) * 512],
                         start=(k == 0), stop=(k == 7))
                mm = h.matmul(qb[0:1, off:off + 512], kb.cocol[:],
                              xsq[:, tc * 512:(tc + 1) * 512],
                              start=(k == 0), stop=(k == 7))
            mm.then_inc(sT.h, 1)
        kb.t(t1)
        sT.n += 1
        cons_ck.append(sT.n)
    ck_st = sT.n

    def v1(h, c=ck_st):
        h.wait_ge(sT.h, c)
        h.tensor_scalar_mul(srow[0:1, 0:1024], Q[0][0:1, :], 1.0 / D)
        h.tensor_scalar_mul(srow[0:1, 1024:L], Q[1][0:1, :], 1.0 / D)
        h.tensor_scalar_mul(srow[0:1, L:L + 1024], Q[2][0:1, :], 1.0 / D)
        h.tensor_scalar_mul(srow[0:1, L + 1024:2 * L], Q[3][0:1, :], 1.0 / D)
        h.tensor_tensor(srow[0:1, 2 * L:3 * L], srow[0:1, 0:L], srow[0:1, 0:L],
                        OP.mult)
        h.tensor_tensor(srow[0:1, L:2 * L], srow[0:1, L:2 * L],
                        srow[0:1, 2 * L:3 * L], OP.subtract)
        h.tensor_scalar_add(srow[0:1, L:2 * L], srow[0:1, L:2 * L],
                            1e-5).then_inc(sV.h, 1)
    kb.v(v1)
    sV.n += 1
    ck_v = sV.n

    def s2(h, c=ck_v):
        h.wait_ge(sV.h, c)
        h.activation(srow[0:1, 2 * L:3 * L], srow[0:1, L:2 * L], AF.Sqrt).then_inc(sS.h, 1)
    kb.s(s2)
    sS.n += 1
    ck_sd = sS.n

    def v2(h, c=ck_sd):
        h.wait_ge(sS.h, c)
        h.reciprocal(srow[0:1, L:2 * L], srow[0:1, 2 * L:3 * L]).then_inc(sV.h, 1)
    kb.v(v2)
    sV.n += 1
    ck_r = sV.n

    def t2(h, c=ck_r):
        h.wait_ge(sV.h, c)
        for tc in range(4):
            qm = Q[tc // 2]
            qr = Q[2 + tc // 2]
            off = (tc % 2) * 512
            h.matmul(qm[:, off:off + 512], kb.co1f[:],
                     srow[0:1, tc * 512:(tc + 1) * 512], start=True, stop=True)
            mm = h.matmul(qr[:, off:off + 512], kb.co1f[:],
                          srow[0:1, L + tc * 512:L + (tc + 1) * 512], start=True,
                          stop=True)
        mm.then_inc(sT.h, 1)
    kb.t(t2)
    sT.n += 1
    ck_bc = sT.n

    def s3(h, c=ck_bc):
        h.wait_ge(sT.h, c)
        h.copy(mrb[:, 0:1024], Q[0][:, :])
        h.copy(mrb[:, 1024:L], Q[1][:, :])
        h.copy(mrb[:, L:L + 1024], Q[2][:, :])
        h.copy(mrb[:, L + 1024:2 * L], Q[3][:, :]).then_inc(sS.h, 1)
    kb.s(s3)
    sS.n += 1
    ck_m = sS.n

    app_ck = []
    for k in range(8):
        gate = (sV, app_ck[k - 2]) if (src_dram is not None and k >= 2) else \
               ((sS, ck_m) if src_dram is not None else (None, 0))
        dld = emit_load(k, gate[0], gate[1])
        src = tile_src(k)

        def v3(h, k=k, src=src, dld=dld, c=ck_m):
            h.wait_ge(sS.h, c)
            if dld is not None:
                h.wait_ge(dld[0].h, dld[1])
            h.tensor_tensor(xsq[:], src, mrb[:, 0:L], OP.subtract)
            h.tensor_tensor(dst[:, k * L:(k + 1) * L], xsq[:],
                            mrb[:, L:2 * L], OP.mult).then_inc(sV.h, 1)
        kb.v(v3)
        sV.n += 1
        app_ck.append(sV.n)


def emit_mamba(kb, mc):
    nc, di = kb.nc, kb.di
    dG, dS, sT, sV, sS = kb.dG, kb.dS, kb.sT, kb.sV, kb.sS
    Q0, Q1, Q2, Q3 = kb.Q
    XN, SM = kb.XN, kb.SM
    sbl = lambda n, s, d: mc.enter_context(nc.sbuf_tensor(n, s, d))
    WIN = sbl("WIN", [128, 8 * DIP], BF16)
    ZB = sbl("ZB", [128, DI], BF16)
    XBC = sbl("XBC", [128, NCT * (SLAB + 3)], BF16)
    CVO = sbl("CVO", [128, NCT * SLAB], BF16)
    XT = sbl("XT", [128, DI], BF16)
    BTK = sbl("BTK", [128, 128], BF16)
    SCR = sbl("SCR", [128, 512], BF16)
    YT = sbl("YT", [128, DI], BF16)
    YNT = sbl("YNT", [128, 16 * 512], BF16)
    H16 = sbl("H16", [128, DI], BF16)
    E8 = sbl("E8", [128, 1024], BF16)
    SPM = sbl("SPM", [128, 128], BF16)
    DTF = sbl("DTF", [NH, SLAB], F32)
    DTA = sbl("DTA", [NH, CH], F32)
    ZRO = sbl("ZRO", [NH, CH], F32)
    CSF = sbl("CSF", [NH, CH], F32)
    CFL = sbl("CFL", [1, 8 * CH], F32)
    TKS = sbl("TKS", [128, 8 * NH], F32)
    V16 = sbl("V16", [128, NH], BF16)
    SPRB = sbl("SPRB", [1, 512], F32)
    SQT = sbl("SQT", [128, 512], BF16)
    SQ2 = sbl("SQ2", [128, 512], BF16)
    ROWA = sbl("ROWA", [128, 1], F32)
    ROWB = sbl("ROWB", [128, 4], F32)
    Q1b = Q1[:, :].bitcast(BF16)
    Q2b = Q2[:, :].bitcast(BF16)
    cs_tok = TKS[:, 0 * NH:1 * NH]
    dt_tok = TKS[:, 1 * NH:2 * NH]
    logdt = TKS[:, 2 * NH:3 * NH]
    biasc = TKS[:, 3 * NH:4 * NH]
    rcs_t = TKS[:, 4 * NH:5 * NH]
    expcs = TKS[:, 5 * NH:6 * NH]
    gamt = TKS[:, 6 * NH:7 * NH]
    spare = TKS[:, 7 * NH:8 * NH]

    v0, s0 = sV.n, sS.n

    def pre_g(h, v0=v0, s0=s0):
        h.wait_ge(sV.h, v0)
        h.wait_ge(sS.h, s0)
        h.memset(ZRO[:], 0.0)
        h.memset(H16[:], 0.0)
        for e in range(NCT):
            h.memset(XBC[:, e * (SLAB + 3):e * (SLAB + 3) + 3], 0.0)
        h.memset(XBC[:, 0:1], 0.0).then_inc(kb.gC.h, 1)
        for k in range(8):
            h.dma_start(out=WIN[:, k * DIP:(k + 1) * DIP],
                        in_=di["win"][k * 128:(k + 1) * 128, :]).then_inc(dG.h, 16)
    kb.g(pre_g)
    dG.n += 16 * 8
    kb.gC.n += 1
    ck_win = dG.n
    ck_mset = kb.gC.n

    q0_ck = [(sS, sS.n), (sS, sS.n)]
    st = dict(xbc_evac=[0] * NCT, tail=sV.n, dtf=sV.n, cvo=[0] * NCT,
              lastFl=0, lastBcast=0, lastSpm=0, scrGate=(sT, 0))

    def emitA(slab):
        t0 = slab * SLAB
        for e in range(NCT):
            slot = e % 2
            gate = q0_ck[slot]
            ckw = ck_win
            first = (slab == 0 and e < 2)
            pv = st["tail"]

            def ta(h, e=e, slot=slot, gate=gate, ckw=ckw, t0=t0, first=first):
                if first:
                    h.wait_ge(dG.h, ckw)
                h.wait_ge(gate[0].h, gate[1])
                for k in range(8):
                    mm = h.matmul(Q0[:, slot * 512:slot * 512 + SLAB],
                                  WIN[:, k * DIP + DI + e * 128:k * DIP + DI + (e + 1) * 128],
                                  XN[:, k * L + t0:k * L + t0 + SLAB],
                                  start=(k == 0), stop=(k == 7))
                mm.then_inc(sT.h, 1)
            kb.t(ta)
            sT.n += 1
            ckt = sT.n

            def sa(h, e=e, slot=slot, ckt=ckt, pv=pv):
                h.wait_ge(sT.h, ckt)
                h.wait_ge(sV.h, pv)
                h.copy(XBC[:, e * (SLAB + 3) + 3:(e + 1) * (SLAB + 3)],
                       Q0[:, slot * 512:slot * 512 + SLAB]).then_inc(sS.h, 1)
            kb.s(sa)
            sS.n += 1
            q0_ck[slot] = (sS, sS.n)
            st["xbc_evac"][e] = sS.n

    emitA(0)
    for sc2 in range(4):
        for sl in range(2):
            slab = 2 * sc2 + sl
            t0 = slab * SLAB
            def tdt(h, t0=t0, pv=sV.n, ps=sS.n):
                h.wait_ge(sV.h, pv)
                h.wait_ge(sS.h, ps)
                for k in range(8):
                    h.matmul(Q1[0:NH, 224:224 + SLAB],
                             WIN[:, k * DIP + DI + CD:k * DIP + DIP],
                             XN[:, k * L + t0:k * L + t0 + SLAB],
                             start=(k == 0), stop=False)
                h.matmul(Q1[0:NH, 224:224 + SLAB], kb.cdtb[:],
                         kb.cor512[0:1, 0:SLAB], start=False,
                         stop=True).then_inc(sT.h, 1)
            kb.t(tdt)
            sT.n += 1
            ck_dtr = sT.n

            def vsp(h, c=ck_dtr):
                h.wait_ge(sT.h, c)
                h.tensor_scalar(DTF[:, :], Q1[0:NH, 224:224 + SLAB], 0.0, None,
                                OP.max).then_inc(sV.h, 1)
            kb.v(vsp)
            sV.n += 1
            ck_sp1 = sV.n

            def ssp(h, c=ck_sp1):
                h.wait_ge(sV.h, c)
                scr = SQT[:, :].bitcast(F32)
                h.activation(scr[0:NH, 0:SLAB], Q1[0:NH, 224:224 + SLAB], AF.Abs)
                h.activation(scr[NH:2 * NH, 0:SLAB], scr[0:NH, 0:SLAB], AF.Exp,
                             bias=0.0, scale=-1.0)
                h.activation(scr[0:NH, 0:SLAB], scr[NH:2 * NH, 0:SLAB], AF.Ln,
                             bias=1.0, scale=1.0).then_inc(sS.h, 1)
            kb.s(ssp)
            sS.n += 1
            ck_sp2 = sS.n

            def vsp2(h, c=ck_sp2):
                h.wait_ge(sS.h, c)
                scr = SQT[:, :].bitcast(F32)
                h.tensor_tensor(DTF[:, :], DTF[:, :], scr[0:NH, 0:SLAB],
                                OP.add).then_inc(sV.h, 1)
            kb.v(vsp2)
            sV.n += 1
            st["dtf"] = sV.n

            prev_silu = 0
            for e in range(NCT):
                ck_e = st["xbc_evac"][e]

                def vc(h, e=e, c=ck_e, pcv=prev_silu, cm=ck_mset):
                    h.wait_ge(sS.h, c)
                    h.wait_ge(kb.gC.h, cm)
                    if pcv:
                        h.wait_ge(sS.h, pcv)
                    if pcv:
                        h.wait_ge(sS.h, pcv)
                    o = e * (SLAB + 3)
                    h.tensor_scalar(SQ2[:, 0:SLAB], XBC[:, o:o + SLAB],
                                    kb.ccvw[:, e * 4:e * 4 + 1], None, OP.mult)
                    for kk in range(1, 4):
                        mm = h.scalar_tensor_tensor(SQ2[:, 0:SLAB],
                                                    XBC[:, o + kk:o + kk + SLAB],
                                                    kb.ccvw[:, e * 4 + kk:e * 4 + kk + 1],
                                                    SQ2[:, 0:SLAB], OP.mult, OP.add)
                    mm.then_inc(sV.h, 1)
                kb.v(vc)
                sV.n += 1
                ck_cv = sV.n

                def sc_(h, e=e, c=ck_cv):
                    h.wait_ge(sV.h, c)
                    h.activation(CVO[:, e * SLAB:(e + 1) * SLAB], SQ2[:, 0:SLAB],
                                 AF.Silu, bias=kb.ccvb[:, e:e + 1],
                                 scale=1.0).then_inc(sS.h, 1)
                kb.s(sc_)
                sS.n += 1
                st["cvo"][e] = sS.n
                prev_silu = sS.n
            ck_cvo_all = sS.n

            def vt(h, c=ck_cvo_all):
                h.wait_ge(sS.h, c)
                pch = NCT * (SLAB + 3)
                h.tensor_copy(bass.AP(XBC, 0, [[pch, 128], [SLAB + 3, NCT], [1, 3]]),
                              bass.AP(XBC, SLAB, [[pch, 128], [SLAB + 3, NCT], [1, 3]])
                              ).then_inc(sV.h, 1)
            kb.v(vt)
            sV.n += 1
            st["tail"] = sV.n

            if sl == 0:
                emitA(slab + 1)

            if slab == NSLAB - 1:
                kb.tap_raw("cvo", CVO[:])
                kb.tap_raw("dtf", DTF[:])

            for cc in range(SLAB // CH):
                ch = slab * 2 + cc
                chl = ch % 4
                ct0 = t0 + cc * CH
                cvo_off = cc * CH

                zb_gate = st.get("zbGate", 0)
                for zf in range(4):
                    slot = zf % 2
                    gate = q0_ck[slot]

                    def tz(h, zf=zf, slot=slot, gate=gate, ct0=ct0):
                        h.wait_ge(gate[0].h, gate[1])
                        for k in range(8):
                            mm = h.matmul(Q0[:, slot * 512:(slot + 1) * 512],
                                          XN[:, k * L + ct0:k * L + ct0 + CH],
                                          WIN[:, k * DIP + zf * 512:k * DIP + (zf + 1) * 512],
                                          start=(k == 0), stop=(k == 7))
                        mm.then_inc(sT.h, 1)
                    kb.t(tz)
                    sT.n += 1
                    ckz = sT.n

                    def sz(h, zf=zf, slot=slot, c=ckz, zg=zb_gate):
                        h.wait_ge(sT.h, c)
                        if zg:
                            h.wait_ge(sV.h, zg)
                        h.activation(ZB[:, zf * 512:(zf + 1) * 512],
                                     Q0[:, slot * 512:(slot + 1) * 512],
                                     AF.Silu).then_inc(sS.h, 1)
                    kb.s(sz)
                    sS.n += 1
                    q0_ck[slot] = (sS, sS.n)
                ck_zb = sS.n

                def vda(h, cc=cc, cdtf=st["dtf"], cfl=st["lastFl"], cm=ck_mset):
                    h.wait_ge(kb.gC.h, cm)
                    h.wait_ge(sV.h, cdtf)
                    if cfl:
                        h.wait_ge(dS.h, cfl)
                    h.tensor_scalar(DTA[:], DTF[:, cc * CH:cc * CH + CH],
                                    kb.cacol[:], None, OP.mult)
                    h.tensor_tensor_scan(CSF[:], DTA[:], ZRO[:], 0.0, OP.add,
                                         OP.add).then_inc(sV.h, 1)
                kb.v(vda)
                sV.n += 1
                ck_csf = sV.n

                def tcs(h, cc=cc, c=ck_csf, csp=ck_sp2, pv=sV.n, ps=sS.n):
                    h.wait_ge(sV.h, c)
                    h.wait_ge(sS.h, csp)
                    h.wait_ge(sV.h, pv)
                    h.wait_ge(sS.h, ps)
                    h.transpose(Q1[:, 128:160], CSF[:], kb.cidf[0:NH, 0:NH])
                    h.transpose(Q1[:, 160:192], DTF[:, cc * CH:cc * CH + CH],
                                kb.cidf[0:NH, 0:NH]).then_inc(sT.h, 1)
                kb.t(tcs)
                sT.n += 1
                ck_trc = sT.n

                def stok(h, c=ck_trc):
                    h.wait_ge(sT.h, c)
                    h.copy(cs_tok, Q1[:, 128:160])
                    h.copy(dt_tok, Q1[:, 160:192])
                    h.activation(logdt, Q1[:, 160:192], AF.Ln).then_inc(sS.h, 1)
                kb.s(stok)
                sS.n += 1
                ck_tok = sS.n

                def vbias(h, c=ck_tok):
                    h.wait_ge(sS.h, c)
                    h.tensor_tensor(biasc, logdt, cs_tok,
                                    OP.subtract).then_inc(sV.h, 1)
                kb.v(vbias)
                sV.n += 1
                ck_bias = sV.n

                def tbs(h, cvo_off=cvo_off, c=ck_cvo_all, ct=ck_tok,
                        pv=sV.n, ps=sS.n):
                    h.wait_ge(sS.h, c)
                    h.wait_ge(sS.h, ct)
                    h.wait_ge(sV.h, pv)
                    h.wait_ge(sS.h, ps)
                    h.transpose(Q1b[:, 1024:1152],
                                CVO[:, 16 * SLAB + cvo_off:16 * SLAB + cvo_off + CH],
                                kb.cid16[:, :])
                    h.matmul(Q1[:, 0:128],
                             CVO[:, 16 * SLAB + cvo_off:16 * SLAB + cvo_off + CH],
                             CVO[:, 17 * SLAB + cvo_off:17 * SLAB + cvo_off + CH],
                             start=True, stop=True).then_inc(sT.h, 1)
                kb.t(tbs)
                sT.n += 1
                ck_bs = sT.n

                def sbt(h, c=ck_bs, cp=st["scrGate"][1]):
                    h.wait_ge(sT.h, c)
                    if cp:
                        h.wait_ge(sT.h, cp)
                    h.copy(BTK[:], Q1b[:, 1024:1152]).then_inc(sS.h, 1)
                kb.s(sbt)
                sS.n += 1
                ck_btk = sS.n

                def vspm(h, c=ck_bs):
                    h.wait_ge(sT.h, c)
                    h.tensor_tensor(SPM[:], Q1[:, 0:128], kb.cl1[:],
                                    OP.mult).then_inc(sV.h, 1)
                kb.v(vspm)
                sV.n += 1
                ck_spm = sV.n
                st["lastSpm"] = ck_spm

                def txs(h, cvo_off=cvo_off, cR=st.get("lastR", 0), cE=st.get("lastE", 0)):
                    h.wait_ge(sS.h, cE)
                    for e in range(16):
                        mm = h.transpose(Q2b[:, e * 128:(e + 1) * 128],
                                         CVO[:, e * SLAB + cvo_off:e * SLAB + cvo_off + CH],
                                         kb.cid16[:, :])
                    mm.then_inc(sT.h, 1)
                kb.t(txs)
                sT.n += 1
                ck_txs = sT.n

                def sxt(h, c=ck_txs):
                    h.wait_ge(sT.h, c)
                    for j in range(4):
                        mm = h.copy(XT[:, j * 512:(j + 1) * 512],
                                    Q2b[:, j * 512:(j + 1) * 512])
                    mm.then_inc(sS.h, 1)
                kb.s(sxt)
                sS.n += 1
                ck_xt = sS.n

                yevac_ck = st.get("lastY", 0)
                hupd_ck = st.get("lastH", 0)
                e8_gate = st.get("lastIntra", 0)
                r_ck = [ck_xt, ck_xt]
                fl_gate = st["lastBcast"]
                ck_fl = 0

                for gg in range(4):
                    def yfl(h, gg=gg, c=ck_csf, cb=fl_gate):
                        h.wait_ge(sV.h, c)
                        if cb:
                            h.wait_ge(sT.h, cb)
                        h.dma_start(out=bass.AP(CFL, 0, [[8 * CH, 1], [CH, 8], [1, CH]]),
                                    in_=CSF[gg * 8:(gg + 1) * 8, :]).then_inc(dS.h, 16)
                    kb.y(yfl)
                    dS.n += 16
                    ck_fl = dS.n

                    def tgb(h, gg=gg, c=ck_fl, pv=sV.n, ps=sS.n):
                        h.wait_ge(dS.h, c)
                        h.wait_ge(sV.h, pv)
                        h.wait_ge(sS.h, ps)
                        gr = bass.AP(CFL, CH - 1, [[8 * CH, 1], [CH, 8]])
                        h.matmul(Q1[:, 192 + 8 * gg:200 + 8 * gg], kb.co1f[:],
                                 gr, start=True, stop=True).then_inc(sT.h, 1)
                    kb.t(tgb)
                    sT.n += 1
                    ck_gb = sT.n

                    def vh1(h, gg=gg, c=ck_gb, ct=ck_tok, cb=ck_bias):
                        h.wait_ge(sT.h, c)
                        h.wait_ge(sS.h, ct)
                        h.wait_ge(sV.h, cb)
                        sl = slice(gg * 8, gg * 8 + 8)
                        h.tensor_tensor(rcs_t[:, sl], Q1[:, 192 + 8 * gg:200 + 8 * gg],
                                        cs_tok[:, sl], OP.subtract).then_inc(sV.h, 1)
                    kb.v(vh1)
                    sV.n += 1
                    ck_rcs = sV.n

                    def sh1(h, gg=gg, c=ck_rcs):
                        h.wait_ge(sV.h, c)
                        sl = slice(gg * 8, gg * 8 + 8)
                        h.activation(spare[:, sl], rcs_t[:, sl], AF.Exp)
                        h.activation(expcs[:, sl], cs_tok[:, sl], AF.Exp)
                        h.activation(gamt[:, sl], Q1[:, 192 + 8 * gg:200 + 8 * gg],
                                     AF.Exp).then_inc(sS.h, 1)
                    kb.s(sh1)
                    sS.n += 1
                    ck_sh = sS.n

                    def vh2(h, gg=gg, c=ck_sh):
                        h.wait_ge(sS.h, c)
                        sl = slice(gg * 8, gg * 8 + 8)
                        h.tensor_tensor(V16[:, sl], spare[:, sl], dt_tok[:, sl],
                                        OP.mult).then_inc(sV.h, 1)
                    kb.v(vh2)
                    sV.n += 1
                    grp_sm = sV.n

                    for q4 in range(2):
                        rslot = (2 * gg + q4) % 2
                        rgate = r_ck[rslot]

                        def tr(h, gg=gg, q4=q4, rslot=rslot, rgate=rgate,
                               cfl=ck_fl):
                            h.wait_ge(dS.h, cfl)
                            h.wait_ge(sS.h, rgate)
                            rhs = bass.AP(CFL, (q4 * 4) * CH,
                                          [[8 * CH, 1], [CH, 4], [1, CH]])
                            h.matmul(Q2[:, rslot * 512:(rslot + 1) * 512],
                                     kb.co1f[:], rhs, start=True, stop=False)
                            h.matmul(Q2[:, rslot * 512:(rslot + 1) * 512],
                                     kb.cl1[:], kb.csub4[:], start=False,
                                     stop=True).then_inc(sT.h, 1)
                        kb.t(tr)
                        sT.n += 1
                        ck_r = sT.n

                        def se(h, gg=gg, q4=q4, rslot=rslot, c=ck_r,
                               ceg=e8_gate, cb=ck_bias):
                            h.wait_ge(sT.h, c)
                            h.wait_ge(sT.h, ceg)
                            h.wait_ge(sV.h, cb)
                            for j in range(4):
                                hg = gg * 8 + q4 * 4 + j
                                mm = h.activation(
                                    E8[:, (q4 * 4 + j) * 128:(q4 * 4 + j + 1) * 128],
                                    Q2[:, rslot * 512 + j * 128:rslot * 512 + (j + 1) * 128],
                                    AF.Exp, bias=biasc[:, hg:hg + 1], scale=1.0)
                            mm.then_inc(sS.h, 1)
                        kb.s(se)
                        sS.n += 1
                        r_ck[rslot] = sS.n
                    ck_e8 = sS.n
                    fl_gate = sT.n

                    def vm(h, c=ck_e8, cs=ck_spm):
                        h.wait_ge(sS.h, c)
                        h.wait_ge(sV.h, cs)
                        h.tensor_tensor(
                            E8[:, :].rearrange("p (r f) -> p r f", r=8),
                            E8[:, :].rearrange("p (r f) -> p r f", r=8),
                            SPM[:, :].unsqueeze(1).broadcast_to([128, 8, 128]),
                            OP.mult).then_inc(sV.h, 1)
                    kb.v(vm)
                    sV.n += 1
                    ck_m8 = sV.n

                    def vdx(h, gg=gg, cxt=ck_xt, cscr=st["scrGate"][1]):
                        h.wait_ge(sS.h, cxt)
                        if cscr:
                            h.wait_ge(sT.h, cscr)
                        h.tensor_tensor(
                            SCR[:, :].rearrange("p (r f) -> p r f", r=8),
                            XT[:, gg * 512:(gg + 1) * 512].rearrange("p (r f) -> p r f", r=8),
                            kb.cdbc[:, gg * 8:(gg + 1) * 8].unsqueeze(2).broadcast_to([128, 8, 64]),
                            OP.mult).then_inc(sV.h, 1)
                    kb.v(vdx)
                    sV.n += 1
                    ck_dx = sV.n

                    def tin(h, gg=gg, cm=ck_m8, cdx=ck_dx,
                            cy=yevac_ck, chu=hupd_ck, cvo_off=cvo_off,
                            cms=ck_mset):
                        h.wait_ge(kb.gC.h, cms)
                        h.wait_ge(sV.h, cm)
                        h.wait_ge(sV.h, cy)
                        for j in range(8):
                            h.matmul(Q3[:, j * 64:(j + 1) * 64],
                                     E8[:, j * 128:(j + 1) * 128],
                                     XT[:, (gg * 8 + j) * 64:(gg * 8 + j + 1) * 64],
                                     start=(j == 0), stop=False)
                        h.wait_ge(sV.h, cdx)
                        h.matmul(Q3[:, 0:512], kb.cid16[:, :], SCR[:, 0:512],
                                 start=False, stop=True)
                        h.wait_ge(sV.h, chu)
                        h.matmul(Q3[:, 512:1024],
                                 CVO[:, 17 * SLAB + cvo_off:17 * SLAB + cvo_off + CH],
                                 H16[:, gg * 512:(gg + 1) * 512], start=True,
                                 stop=True).then_inc(sT.h, 1)
                    kb.t(tin)
                    sT.n += 1
                    st["lastIntra"] = sT.n
                    e8_gate = sT.n
                    ck_tin = sT.n

                    def vxh(h, gg=gg, c=ck_tin, cv=grp_sm):
                        h.wait_ge(sT.h, c)
                        h.wait_ge(sV.h, cv)
                        h.tensor_tensor(
                            SCR[:, :].rearrange("p (r f) -> p r f", r=8),
                            XT[:, gg * 512:(gg + 1) * 512].rearrange("p (r f) -> p r f", r=8),
                            V16[:, gg * 8:(gg + 1) * 8].unsqueeze(2).broadcast_to([128, 8, 64]),
                            OP.mult).then_inc(sV.h, 1)
                    kb.v(vxh)
                    sV.n += 1
                    ck_xh = sV.n

                    def tp8(h, c=ck_xh, cbt=ck_btk, cq0=q0_ck[0]):
                        h.wait_ge(sV.h, c)
                        h.wait_ge(sS.h, cbt)
                        h.wait_ge(cq0[0].h, cq0[1])
                        h.matmul(Q0[:, 0:512], BTK[:], SCR[:, 0:512],
                                 start=True, stop=True).then_inc(sT.h, 1)
                    kb.t(tp8)
                    sT.n += 1
                    ck_p8 = sT.n
                    st["scrGate"] = (sT, ck_p8)

                    def vy(h, gg=gg, c=ck_tin):
                        h.wait_ge(sT.h, c)
                        h.tensor_tensor(
                            SQT[:, :].rearrange("p (r f) -> p r f", r=8),
                            Q3[:, 512:1024].rearrange("p (r f) -> p r f", r=8),
                            expcs[:, gg * 8:(gg + 1) * 8].unsqueeze(2).broadcast_to([128, 8, 64]),
                            OP.mult)
                        h.tensor_tensor(YT[:, gg * 512:(gg + 1) * 512],
                                        SQT[:, 0:512], Q3[:, 0:512],
                                        OP.add).then_inc(sV.h, 1)
                    kb.v(vy)
                    sV.n += 1
                    yevac_ck = sV.n
                    st["lastY"] = sV.n

                    def vhu(h, gg=gg, c=ck_p8):
                        h.wait_ge(sT.h, c)
                        h.tensor_tensor(
                            SQT[:, :].rearrange("p (r f) -> p r f", r=8),
                            H16[:, gg * 512:(gg + 1) * 512].rearrange("p (r f) -> p r f", r=8),
                            gamt[:, gg * 8:(gg + 1) * 8].unsqueeze(2).broadcast_to([128, 8, 64]),
                            OP.mult)
                        h.tensor_tensor(H16[:, gg * 512:(gg + 1) * 512],
                                        SQT[:, 0:512], Q0[:, 0:512],
                                        OP.add).then_inc(sV.h, 1)
                    kb.v(vhu)
                    sV.n += 1
                    hupd_ck = sV.n
                    st["lastH"] = sV.n
                    q0_ck[0] = (sV, sV.n)
                st["lastE"] = max(r_ck)
                st["lastFl"] = ck_fl
                st["lastBcast"] = fl_gate

                if ch == 15:
                    kb.tap_raw("zb", ZB[:])
                    kb.tap_raw("ytraw", YT[:])
                    kb.tap_raw("csf", CSF[:])
                    kb.tap_raw("cstok", TKS[:])
                    kb.tap_raw("v16", V16[:])
                    kb.tap_raw("spm", SPM[:])
                    kb.tap_raw("xt", XT[:])
                    kb.tap_raw("e8", E8[:])

                ck_groups_v = sV.n
                ck_groups_e = max(r_ck)

                def vz1(h, czb=ck_zb):
                    h.wait_ge(sS.h, czb)
                    h.tensor_tensor(YT[:], YT[:], ZB[:], OP.mult).then_inc(sV.h, 1)
                kb.v(vz1)
                sV.n += 1
                ck_yz = sV.n
                st["zbGate"] = ck_yz

                def sq1(h, c=ck_yz, ct=st["scrGate"][1]):
                    h.wait_ge(sV.h, c)
                    h.wait_ge(sT.h, ct)
                    h.activation(XT[:], YT[:], AF.Square,
                                 accum_out=ROWA[:]).then_inc(sS.h, 1)
                kb.s(sq1)
                sS.n += 1
                ck_ra = sS.n

                def vr1(h, c=ck_ra):
                    h.wait_ge(sS.h, c)
                    h.tensor_scalar(ROWA[:], ROWA[:], 1.0 / DI, 1e-5, OP.mult,
                                    OP.add).then_inc(sV.h, 1)
                kb.v(vr1)
                sV.n += 1
                ck_rb = sV.n

                def sr1(h, c=ck_rb):
                    h.wait_ge(sV.h, c)
                    h.activation(ROWB[:, 0:1], ROWA[:], AF.Sqrt).then_inc(sS.h, 1)
                kb.s(sr1)
                sS.n += 1
                ck_rc = sS.n

                def vr2(h, c=ck_rc):
                    h.wait_ge(sS.h, c)
                    h.reciprocal(ROWA[:], ROWB[:, 0:1])
                    h.memset(kb.spc[:, 0:2], 0.0)
                    h.memset(kb.spc[:, 2:4], 0.0)
                    h.tensor_scalar(YT[:], YT[:], ROWA[:], None, OP.mult).then_inc(sV.h, 1)
                kb.v(vr2)
                sV.n += 1
                ck_yn = sV.n

                def tyt(h, c=ck_yn, ce=ck_groups_e):
                    h.wait_ge(sV.h, c)
                    h.wait_ge(sS.h, ce)
                    for e in range(16):
                        mm = h.transpose(Q2b[:, e * 128:(e + 1) * 128],
                                         YT[:, e * 128:(e + 1) * 128],
                                         kb.cid16[:, :])
                    mm.then_inc(sT.h, 1)
                kb.t(tyt)
                sT.n += 1
                ck_tyt = sT.n

                def syn(h, c=ck_tyt, chl=chl):
                    h.wait_ge(sT.h, c)
                    for j in range(4):
                        dst = bass.AP(YNT, (4 * j) * 512 + chl * 128,
                                      [[16 * 512, 128], [512, 4], [1, 128]])
                        src = Q2b[:, j * 512:(j + 1) * 512].rearrange(
                            "p (a b) -> p a b", a=4)
                        mm = h.copy(dst, src)
                    mm.then_inc(sS.h, 1)
                kb.s(syn)
                sS.n += 1
                st["ynDone"] = sS.n
                st["lastE"] = max(st["lastE"], st["ynDone"])

        if sc2 == 3:
            kb.tap_raw("h16", H16[:])
            kb.tap_raw("ynt", YNT[:])

        if sc2 < 3:
            emitA(2 * sc2 + 2)

        WOB = ZB
        ck_pre_v, ck_pre_s = sV.n, sS.n
        wo_cons = [(sV, ck_pre_v)] * 16
        for k in range(16):
            gate = wo_cons[k - 2] if k >= 2 else (sV, ck_pre_v)

            dx = kb.dX[k % 2]

            def gw(h, k=k, sc2=sc2, gate=gate, dx=dx):
                if gate[1]:
                    h.wait_ge(gate[0].h, gate[1])
                h.dma_start(out=WOB[:, (k % 2) * 1024:(k % 2) * 1024 + 1024],
                            in_=di["wout"][k * 128:(k + 1) * 128, :]).then_inc(dx.h, 16)
            kb.g(gw)
            dx.n += 16
            ck_l = (dx, dx.n)

            def vw(h, k=k, c=ck_l):
                h.wait_ge(c[0].h, c[1])
                h.tensor_scalar(WOB[:, (k % 2) * 1024:(k % 2) * 1024 + 1024],
                                WOB[:, (k % 2) * 1024:(k % 2) * 1024 + 1024],
                                kb.cnormw[:, k:k + 1], None, OP.mult).then_inc(sV.h, 1)
            kb.v(vw)
            sV.n += 1
            ck_s = sV.n

            def tw(h, k=k, c=ck_s, cv=ck_pre_v, cs=ck_pre_s):
                h.wait_ge(sV.h, c)
                if k == 0:
                    h.wait_ge(sV.h, cv)
                    h.wait_ge(sS.h, cs)
                for dd in range(8):
                    mm = h.matmul(kb.Q[dd // 2][:, (dd % 2) * 512:(dd % 2) * 512 + 512],
                                  WOB[:, (k % 2) * 1024 + dd * 128:(k % 2) * 1024 + (dd + 1) * 128],
                                  YNT[:, k * 512:(k + 1) * 512],
                                  start=(k == 0), stop=(k == 15))
                mm.then_inc(sT.h, 1)
            kb.t(tw)
            sT.n += 1
            wo_cons[k] = (sT, sT.n)
        ck_op = sT.n

        for dd in range(8):
            def vo(h, dd=dd, sc2=sc2, c=ck_op):
                h.wait_ge(sT.h, c)
                h.tensor_tensor(SM[:, dd * L + sc2 * 512:dd * L + sc2 * 512 + 512],
                                kb.Q[dd // 2][:, (dd % 2) * 512:(dd % 2) * 512 + 512],
                                XN[:, dd * L + sc2 * 512:dd * L + sc2 * 512 + 512],
                                OP.add).then_inc(sV.h, 1)
            kb.v(vo)
            sV.n += 1
        ck_sm = sV.n

        prev_mm = 0
        for dd in range(8):
            def sql(h, dd=dd, sc2=sc2, c=ck_sm, pm=prev_mm):
                h.wait_ge(sV.h, c)
                if pm:
                    h.wait_ge(sT.h, pm)
                h.activation(SQT[:], SM[:, dd * L + sc2 * 512:dd * L + sc2 * 512 + 512],
                             AF.Square).then_inc(sS.h, 1)
            kb.s(sql)
            sS.n += 1
            ck_q = sS.n

            def tst(h, dd=dd, sc2=sc2, c=ck_q, cv=ck_sm):
                h.wait_ge(sS.h, c)
                h.wait_ge(sV.h, cv)
                h.matmul(Q0[0:1, 0:512], kb.cocol[:],
                         SM[:, dd * L + sc2 * 512:dd * L + sc2 * 512 + 512],
                         start=(dd == 0), stop=(dd == 7))
                h.matmul(Q0[0:1, 512:1024], kb.cocol[:], SQT[:],
                         start=(dd == 0), stop=(dd == 7)).then_inc(sT.h, 1)
            kb.t(tst)
            sT.n += 1
            prev_mm = sT.n
        ck_lnst = sT.n

        def vln(h, c=ck_lnst):
            h.wait_ge(sT.h, c)
            h.tensor_scalar_mul(CFL[0:1, 0:512], Q0[0:1, 0:512], 1.0 / D)
            h.tensor_scalar_mul(CFL[0:1, 512:1024], Q0[0:1, 512:1024], 1.0 / D)
            h.tensor_tensor(SPRB[:], CFL[0:1, 0:512], CFL[0:1, 0:512], OP.mult)
            h.memset(kb.ceps[:], 1e-5)
            h.memset(kb.spc[:, 2:4], 0.0)
            h.tensor_tensor(CFL[0:1, 512:1024], CFL[0:1, 512:1024], SPRB[:],
                            OP.subtract).then_inc(sV.h, 1)
        kb.v(vln)
        sV.n += 1
        ck_lv = sV.n

        def sln(h, c=ck_lv):
            h.wait_ge(sV.h, c)
            h.activation(SPRB[:], CFL[0:1, 512:1024], AF.Sqrt,
                         bias=kb.ceps[0:1, 0:1], scale=1.0).then_inc(sS.h, 1)
        kb.s(sln)
        sS.n += 1
        ck_ls = sS.n

        def vln2(h, c=ck_ls):
            h.wait_ge(sS.h, c)
            h.reciprocal(SPRB[:], SPRB[:]).then_inc(sV.h, 1)
        kb.v(vln2)
        sV.n += 1
        ck_lr = sV.n

        def tbc(h, c=ck_lr):
            h.wait_ge(sV.h, c)
            h.matmul(Q1[:, 0:512], kb.co1f[:], CFL[0:1, 0:512], start=True,
                     stop=True)
            h.matmul(Q1[:, 512:1024], kb.co1f[:], SPRB[:], start=True,
                     stop=True).then_inc(sT.h, 1)
        kb.t(tbc)
        sT.n += 1
        ck_lb = sT.n

        for dd in range(8):
            def vap(h, dd=dd, sc2=sc2, c=ck_lb):
                h.wait_ge(sT.h, c)
                h.tensor_tensor(SQT[:], SM[:, dd * L + sc2 * 512:dd * L + sc2 * 512 + 512],
                                Q1[:, 0:512], OP.subtract)
                h.tensor_tensor(SQ2[:], SQT[:], Q1[:, 512:1024], OP.mult)
                h.tensor_scalar(SM[:, dd * L + sc2 * 512:dd * L + sc2 * 512 + 512],
                                SQ2[:], kb.clnw[:, dd:dd + 1],
                                kb.clnb[:, dd:dd + 1], OP.mult,
                                OP.add).then_inc(sV.h, 1)
            kb.v(vap)
            sV.n += 1


def emit_moe(kb, mcx):
    nc, di = kb.nc, kb.di
    dG, dS, sT, sV, sS = kb.dG, kb.dS, kb.sT, kb.sV, kb.sS
    Q0, Q1, Q2, Q3 = kb.Q
    XN, SM = kb.XN, kb.SM
    sbl = lambda n, s, d: mcx.enter_context(nc.sbuf_tensor(n, s, d))
    GAC = sbl("GAC", [128, 8 * L], BF16)
    HS = sbl("HS", [128, 16 * 1024], BF16)
    W1B = sbl("W1B", [128, 3 * 1024], BF16)
    W2B = sbl("W2B", [128, 2 * 2048], BF16)
    RWB = sbl("RWB", [128, 64], BF16)
    LGF = sbl("LGF", [8, L], F32)
    LGT = sbl("LGT", [128, 128], F32)
    GFM = sbl("GFM", [8, L], BF16)
    GFL = sbl("GFL", [1, 8 * L], BF16)
    GBC = sbl("GBC", [128, L], BF16)
    TPS = sbl("TPS_m", [128, 16], F32)
    TPR = sbl("TPR_m", [128, 16], F32)
    GE8 = sbl("GE8_m", [128, 16], F32)
    EXM = sbl("EXM_m", [128, 16], F32)
    SQM = sbl("SQM", [128, L], BF16)
    XLD = GBC
    Q2b = Q2[:, :].bitcast(BF16)

    v0, s0 = sV.n, sS.n

    def g0(h, v0=v0, s0=s0):
        h.wait_ge(sV.h, v0)
        h.wait_ge(sS.h, s0)
        h.memset(GAC[:], 0.0).then_inc(kb.gC.h, 1)
        src = bass.AP(di["routw"].tensor, 0, [[8, 128], [1024, 8], [1, 8]])
        h.dma_start(out=RWB[:], in_=src).then_inc(dG.h, 16)
    kb.g(g0)
    dG.n += 16
    kb.gC.n += 1
    ck_rw = dG.n
    ck_gmset = kb.gC.n

    q0g = [(sS, 0), (sS, 0)]
    for tc in range(4):
        gate = q0g[tc % 2]

        def t0(h, tc=tc, c=ck_rw, gate=gate):
            h.wait_ge(dG.h, c)
            if gate[1]:
                h.wait_ge(gate[0].h, gate[1])
            for k in range(8):
                mm = h.matmul(Q0[0:8, (tc % 2) * 512:(tc % 2) * 512 + 512],
                              RWB[:, k * 8:(k + 1) * 8],
                              XN[:, k * L + tc * 512:k * L + tc * 512 + 512],
                              start=(k == 0), stop=(k == 7))
            mm.then_inc(sT.h, 1)
        kb.t(t0)
        sT.n += 1
        ck_mm = sT.n

        def s0e(h, tc=tc, c=ck_mm):
            h.wait_ge(sT.h, c)
            h.copy(LGF[:, tc * 512:(tc + 1) * 512],
                   Q0[0:8, (tc % 2) * 512:(tc % 2) * 512 + 512]).then_inc(sS.h, 1)
        kb.s(s0e)
        sS.n += 1
        q0g[tc % 2] = (sS, sS.n)
    ck_lgf = sS.n

    def t1(h, c=ck_lgf):
        h.wait_ge(sS.h, c)
        for tt in range(16):
            mm = h.transpose(Q1[:, tt * 8:(tt + 1) * 8],
                             LGF[0:8, tt * 128:(tt + 1) * 128], kb.cidf[0:8, 0:8])
        mm.then_inc(sT.h, 1)
    kb.t(t1)
    sT.n += 1
    ck_tr = sT.n

    def s1(h, c=ck_tr):
        h.wait_ge(sT.h, c)
        h.copy(LGT[:], Q1[:, 0:128]).then_inc(sS.h, 1)
    kb.s(s1)
    sS.n += 1
    ck_lgt = sS.n

    gt_cons = [(sS, 0), (sS, 0)]
    for pp in range(8):
        ta, tb = 2 * pp, 2 * pp + 1
        la = LGT[:, ta * 8:(ta + 1) * 8]
        lb = LGT[:, tb * 8:(tb + 1) * 8]
        A = lambda j: TPR[:, j:j + 1]
        B = lambda j: TPR[:, 8 + j:8 + j + 1]

        def vtp(h, la=la, lb=lb, c=ck_lgt):
            h.wait_ge(sS.h, c)
            h.tensor_reduce(TPR[:, 0:1], la, mybir.AxisListType.X, OP.max)
            h.tensor_reduce(TPR[:, 8:9], lb, mybir.AxisListType.X, OP.max)
            h.tensor_tensor(TPS[:, 0:8], la,
                            TPR[:, 0:1].broadcast_to([128, 8]), OP.is_ge)
            h.tensor_tensor(TPS[:, 8:16], lb,
                            TPR[:, 8:9].broadcast_to([128, 8]), OP.is_ge)
            h.tensor_scalar_mul(TPS[:, 0:8], TPS[:, 0:8], -1e30)
            h.tensor_scalar_mul(TPS[:, 8:16], TPS[:, 8:16], -1e30)
            h.tensor_tensor(TPS[:, 0:8], TPS[:, 0:8], la, OP.add)
            h.tensor_tensor(TPS[:, 8:16], TPS[:, 8:16], lb, OP.add)
            h.tensor_reduce(TPR[:, 1:2], TPS[:, 0:8], mybir.AxisListType.X, OP.max)
            h.tensor_reduce(TPR[:, 9:10], TPS[:, 8:16], mybir.AxisListType.X, OP.max)
            h.tensor_tensor(TPR[:, 2:3], TPR[:, 1:2], TPR[:, 0:1], OP.subtract)
            h.tensor_tensor(TPR[:, 10:11], TPR[:, 9:10], TPR[:, 8:9], OP.subtract)
            h.tensor_scalar_mul(TPR[:, 3:4], TPR[:, 0:1], -1.0)
            h.tensor_scalar_mul(TPR[:, 11:12], TPR[:, 8:9],
                                -1.0).then_inc(sV.h, 1)
        kb.v(vtp)
        sV.n += 1
        ck_v1 = sV.n

        def stp(h, la=la, lb=lb, c=ck_v1):
            h.wait_ge(sV.h, c)
            h.activation(TPR[:, 4:5], TPR[:, 2:3], AF.Exp)
            h.activation(TPR[:, 12:13], TPR[:, 10:11], AF.Exp)
            h.activation(EXM[:, 0:8], la, AF.Exp, bias=TPR[:, 3:4], scale=1.0)
            h.activation(EXM[:, 8:16], lb, AF.Exp, bias=TPR[:, 11:12],
                         scale=1.0).then_inc(sS.h, 1)
        kb.s(stp)
        sS.n += 1
        ck_s1 = sS.n

        def vtp2(h, la=la, lb=lb, ta=ta, tb=tb, c=ck_s1):
            h.wait_ge(sS.h, c)
            h.tensor_scalar_add(TPR[:, 5:6], TPR[:, 4:5], 1.0)
            h.tensor_scalar_add(TPR[:, 13:14], TPR[:, 12:13], 1.0)
            h.reciprocal(TPR[:, 6:7], TPR[:, 5:6])
            h.reciprocal(TPR[:, 14:15], TPR[:, 13:14])
            h.tensor_tensor(GE8[:, 0:8], la,
                            TPR[:, 1:2].broadcast_to([128, 8]), OP.is_ge)
            h.tensor_tensor(GE8[:, 8:16], lb,
                            TPR[:, 9:10].broadcast_to([128, 8]), OP.is_ge)
            h.tensor_tensor(TPS[:, 0:8], EXM[:, 0:8], GE8[:, 0:8], OP.mult)
            h.tensor_tensor(TPS[:, 8:16], EXM[:, 8:16], GE8[:, 8:16], OP.mult)
            h.tensor_tensor(TPS[:, 0:8], TPS[:, 0:8],
                            TPR[:, 6:7].broadcast_to([128, 8]), OP.mult)
            h.tensor_tensor(TPS[:, 8:16], TPS[:, 8:16],
                            TPR[:, 14:15].broadcast_to([128, 8]), OP.mult)
            h.tensor_copy(SQM[:, ta * 8:ta * 8 + 8], TPS[:, 0:8])
            h.tensor_copy(SQM[:, tb * 8:tb * 8 + 8],
                          TPS[:, 8:16]).then_inc(sV.h, 1)
        kb.v(vtp2)
        sV.n += 1
        ck_v3 = sV.n

        for tt in (ta, tb):
            gt_gate = gt_cons[tt % 2]

            def t2(h, tt=tt, c=ck_v3, gate=gt_gate):
                h.wait_ge(sV.h, c)
                if gate[1]:
                    h.wait_ge(gate[0].h, gate[1])
                h.transpose(Q2b[0:8, (tt % 2) * 1024:(tt % 2) * 1024 + 128],
                            SQM[:, tt * 8:tt * 8 + 8],
                            kb.cid16[:, :]).then_inc(sT.h, 1)
            kb.t(t2)
            sT.n += 1
            ck_t2 = sT.n

            def s2(h, tt=tt, c=ck_t2):
                h.wait_ge(sT.h, c)
                h.copy(GFM[0:8, tt * 128:(tt + 1) * 128],
                       Q2b[0:8, (tt % 2) * 1024:(tt % 2) * 1024 + 128]).then_inc(sS.h, 1)
            kb.s(s2)
            sS.n += 1
            gt_cons[tt % 2] = (sS, sS.n)
    ck_gfm = sS.n

    def yfl(h, c=ck_gfm):
        h.wait_ge(sS.h, c)
        h.dma_start(out=bass.AP(GFL, 0, [[8 * L, 1], [L, 8], [1, L]]),
                    in_=GFM[:]).then_inc(dS.h, 16)
    kb.y(yfl)
    dS.n += 16
    ck_gfl = dS.n

    hs_gate = (sV, 0)
    gbc_gate = (sV, 0)
    w1_cons = [(sV, 0)] * 16
    w2_mm = [(sV, 0)] * 8
    q0m = [(sS, ck_lgf), (sS, ck_lgf)]
    q1m = [(sV, 0), (sV, 0)]
    for e in range(NEXP):
        rnd_gate = gbc_gate
        for rnd in range(2):
            def tgb(h, e=e, rnd=rnd, c=ck_gfl, gb=rnd_gate):
                h.wait_ge(dS.h, c)
                if gb[1]:
                    h.wait_ge(gb[0].h, gb[1])
                for j in range(2):
                    tc = rnd * 2 + j
                    mm = h.matmul(Q3[:, j * 512:(j + 1) * 512], kb.co1b[:],
                                  GFL[0:1, e * L + tc * 512:e * L + tc * 512 + 512],
                                  start=True, stop=True)
                mm.then_inc(sT.h, 1)
            kb.t(tgb)
            sT.n += 1
            ck_gb = sT.n

            def sgb(h, rnd=rnd, c=ck_gb):
                h.wait_ge(sT.h, c)
                h.copy(GBC[:, rnd * 1024:(rnd + 1) * 1024],
                       Q3[:, :]).then_inc(sS.h, 1)
            kb.s(sgb)
            sS.n += 1
            rnd_gate = (sS, sS.n)
        ck_gbc = sS.n

        for half in range(2):
            ht0 = half * 1024
            for hh in range(16):
                gate = w1_cons[hh - 3] if hh >= 3 else w1_cons[[15, 13, 14][hh]]

                dx = kb.dX[hh % 4]

                def gw1(h, e=e, hh=hh, gate=gate, dx=dx):
                    if gate[1]:
                        h.wait_ge(gate[0].h, gate[1])
                    src = bass.AP(di["w1"].tensor,
                                  (e * 16 + hh) * 128 * 1024,
                                  [[1024, 128], [1, 1024]])
                    h.dma_start(out=W1B[:, (hh % 3) * 1024:(hh % 3) * 1024 + 1024],
                                in_=src).then_inc(dx.h, 16)
                kb.g(gw1)
                dx.n += 16
                ck_w1 = (dx, dx.n)

                for tt2 in range(2):
                    def th(h, hh=hh, tt2=tt2, ht0=ht0, c=ck_w1, gate=q0m[tt2]):
                        h.wait_ge(c[0].h, c[1])
                        if gate[1]:
                            h.wait_ge(gate[0].h, gate[1])
                        for k in range(8):
                            mm = h.matmul(Q0[:, tt2 * 512:(tt2 + 1) * 512],
                                          W1B[:, (hh % 3) * 1024 + k * 128:(hh % 3) * 1024 + (k + 1) * 128],
                                          XN[:, k * L + ht0 + tt2 * 512:k * L + ht0 + tt2 * 512 + 512],
                                          start=(k == 0), stop=(k == 7))
                        mm.then_inc(sT.h, 1)
                    kb.t(th)
                    sT.n += 1
                    ck_h = sT.n

                    def sh(h, hh=hh, tt2=tt2, c=ck_h, hg=hs_gate):
                        h.wait_ge(sT.h, c)
                        if hg[1]:
                            h.wait_ge(hg[0].h, hg[1])
                        h.activation(HS[:, hh * 1024 + tt2 * 512:hh * 1024 + tt2 * 512 + 512],
                                     Q0[:, tt2 * 512:(tt2 + 1) * 512],
                                     AF.Relu).then_inc(sS.h, 1)
                    kb.s(sh)
                    sS.n += 1
                    q0m[tt2] = (sS, sS.n)
                w1_cons[hh] = (sT, sT.n)
                ck_hh = sS.n

                def vhs(h, hh=hh, c=ck_hh):
                    h.wait_ge(sS.h, c)
                    h.tensor_tensor(HS[:, hh * 1024:(hh + 1) * 1024],
                                    HS[:, hh * 1024:(hh + 1) * 1024],
                                    HS[:, hh * 1024:(hh + 1) * 1024],
                                    OP.mult).then_inc(sV.h, 1)
                kb.v(vhs)
                sV.n += 1
            ck_sq = sV.n

            for dd in range(8):
                gate = w2_mm[dd - 2] if dd >= 2 else w2_mm[6 + dd]

                dx = kb.dX[dd % 4]

                def gw2(h, e=e, dd=dd, gate=gate, dx=dx):
                    if gate[1]:
                        h.wait_ge(gate[0].h, gate[1])
                    src = bass.AP(di["w2"].tensor,
                                  (e * 8 + dd) * 128 * 2048,
                                  [[2048, 128], [1, 2048]])
                    h.dma_start(out=W2B[:, (dd % 2) * 2048:(dd % 2) * 2048 + 2048],
                                in_=src).then_inc(dx.h, 16)
                kb.g(gw2)
                dx.n += 16
                ck_w2 = (dx, dx.n)

                for tt2 in range(2):
                    def ty(h, dd=dd, tt2=tt2, c=ck_w2, cq=ck_sq, gate=q1m[tt2]):
                        h.wait_ge(c[0].h, c[1])
                        h.wait_ge(sV.h, cq)
                        if gate[1]:
                            h.wait_ge(gate[0].h, gate[1])
                        for k in range(16):
                            mm = h.matmul(Q1[:, tt2 * 512:(tt2 + 1) * 512],
                                          W2B[:, (dd % 2) * 2048 + k * 128:(dd % 2) * 2048 + (k + 1) * 128],
                                          HS[:, k * 1024 + tt2 * 512:k * 1024 + tt2 * 512 + 512],
                                          start=(k == 0), stop=(k == 15))
                        mm.then_inc(sT.h, 1)
                    kb.t(ty)
                    sT.n += 1
                    ck_y = sT.n

                    def vy2(h, dd=dd, tt2=tt2, ht0=ht0, c=ck_y, cg=ck_gbc,
                            cm=ck_gmset):
                        h.wait_ge(kb.gC.h, cm)
                        h.wait_ge(sT.h, c)
                        h.wait_ge(sS.h, cg)
                        h.tensor_tensor(SQM[:, 0:512], Q1[:, tt2 * 512:(tt2 + 1) * 512],
                                        GBC[:, ht0 + tt2 * 512:ht0 + tt2 * 512 + 512],
                                        OP.mult)
                        tgt = GAC[:, dd * L + ht0 + tt2 * 512:dd * L + ht0 + tt2 * 512 + 512]
                        h.tensor_tensor(tgt, tgt, SQM[:, 0:512],
                                        OP.add).then_inc(sV.h, 1)
                    kb.v(vy2)
                    sV.n += 1
                    q1m[tt2] = (sV, sV.n)
                w2_mm[dd] = (sT, sT.n)
            hs_gate = (sT, sT.n)
        gbc_gate = (sV, sV.n)

    kb.tap_raw("lgt", LGT[:])
    kb.tap_raw("sqmg", SQM[:])
    kb.tap_raw("gfm", GFM[:])
    kb.tap_raw("gac", GAC[:])

    ck_moe_v = sV.n
    xld_gate = (sV, ck_moe_v)
    for dd in range(8):
        def gx(h, dd=dd, gate=xld_gate):
            if gate[1]:
                h.wait_ge(gate[0].h, gate[1])
            h.dma_start(out=XLD[:], in_=di["x0"][dd * 128:(dd + 1) * 128, :]).then_inc(dG.h, 16)
        kb.g(gx)
        dG.n += 16
        ck_x = dG.n

        def scb(h, dd=dd, c=ck_moe_v, pv=xld_gate):
            h.wait_ge(sV.h, c)
            if pv[1]:
                h.wait_ge(pv[0].h, pv[1])
            h.activation(SQM[:], GAC[:, dd * L:(dd + 1) * L],
                         AF.Silu).then_inc(sS.h, 1)
        kb.s(scb)
        sS.n += 1
        ck_g = sS.n

        def vcb(h, dd=dd, c=ck_g, cx=ck_x):
            h.wait_ge(sS.h, c)
            h.wait_ge(dG.h, cx)
            h.tensor_tensor(SQM[:], SM[:, dd * L:(dd + 1) * L], SQM[:], OP.mult)
            h.tensor_tensor(SM[:, dd * L:(dd + 1) * L], XLD[:], SQM[:],
                            OP.add).then_inc(sV.h, 1)
        kb.v(vcb)
        sV.n += 1
        xld_gate = (sV, sV.n)
        ck_cb = sV.n

        def gpark(h, dd=dd, c=ck_cb):
            h.wait_ge(sV.h, c)
            h.dma_start(out=kb.dout[dd * 128:(dd + 1) * 128, :],
                        in_=SM[:, dd * L:(dd + 1) * L]).then_inc(kb.dX[3].h, 16)
        kb.g(gpark)
        kb.dX[3].n += 16
    kb.x1_park_ck = kb.dX[3].n


def emit_attn(kb, acx, S1):
    nc, di = kb.nc, kb.di
    dG, dS, sT, sV, sS = kb.dG, kb.dS, kb.sT, kb.sV, kb.sS
    Q0, Q1, Q2, Q3 = kb.Q
    XN = kb.XN
    sbl = lambda n, s, d: acx.enter_context(nc.sbuf_tensor(n, s, d))
    QP = sbl("QP", [128, 8 * L], BF16)
    KVT = sbl("KVT", [128, 2048], BF16)
    BD = sbl("BD", [128, 8 * 128], BF16)
    KSEL = sbl("KSEL", [128, 128], BF16)
    KSR = sbl("KSR", [1, 1024], F32)
    KST = sbl("KST", [128, 8], F32)
    NRM = sbl("NRM", [16, 512], F32)
    NRB = sbl("NRB", [16, 512], BF16)
    SC1 = sbl("SC1", [128, 512], BF16)
    SC2 = sbl("SC2", [128, 512], BF16)
    SC3 = sbl("SC3", [128, 512], BF16)
    crep = sbl("crep", [16, 1024], BF16)
    kb.crep = crep

    def gcr(h):
        h.dma_start(out=crep[:], in_=di["repsel"]).then_inc(dG.h, 16)
    kb.g(gcr)
    dG.n += 16

    with contextlib.ExitStack() as wcx:
        WQ = wcx.enter_context(nc.sbuf_tensor("WQ", [128, 8 * 3072], BF16))
        v0, s0 = sV.n, sS.n

        def gq(h, v0=v0, s0=s0):
            h.wait_ge(sV.h, v0)
            h.wait_ge(sS.h, s0)
            for k in range(8):
                h.dma_start(out=WQ[:, k * 3072:(k + 1) * 3072],
                            in_=di["wqkv"][k * 128:(k + 1) * 128, :]).then_inc(dG.h, 16)
        kb.g(gq)
        dG.n += 8 * 16
        ck_wq = dG.n

        qslot = [(sV, 0), (sV, 0)]
        for qq in range(8):
            for tc in range(4):
                slot = (qq * 4 + tc) % 2
                gate = qslot[slot]

                def tq(h, qq=qq, tc=tc, slot=slot, gate=gate, c=ck_wq):
                    h.wait_ge(dG.h, c)
                    if gate[1]:
                        h.wait_ge(gate[0].h, gate[1])
                    for k in range(8):
                        mm = h.matmul(Q0[:, slot * 512:(slot + 1) * 512],
                                      WQ[:, k * 3072 + qq * 128:k * 3072 + (qq + 1) * 128],
                                      XN[:, k * L + tc * 512:k * L + tc * 512 + 512],
                                      start=(k == 0), stop=(k == 7))
                    mm.then_inc(sT.h, 1)
                kb.t(tq)
                sT.n += 1
                ck_q = sT.n

                def vq(h, slot=slot, c=ck_q, ps=sS.n):
                    h.wait_ge(sT.h, c)
                    h.wait_ge(sS.h, ps)
                    h.tensor_scalar(SC1[:], Q0[:, slot * 512:(slot + 1) * 512],
                                    0.0, None, OP.min)
                    h.tensor_scalar(SC2[:], Q0[:, slot * 512:(slot + 1) * 512],
                                    0.0, None, OP.max).then_inc(sV.h, 1)
                kb.v(vq)
                sV.n += 1
                ck_qv = sV.n
                qslot[slot] = (sV, ck_qv)

                def sq(h, c=ck_qv):
                    h.wait_ge(sV.h, c)
                    h.activation(SC3[:], SC1[:], AF.Exp).then_inc(sS.h, 1)
                kb.s(sq)
                sS.n += 1
                ck_qs = sS.n

                def vq2(h, qq=qq, tc=tc, c=ck_qs):
                    h.wait_ge(sS.h, c)
                    h.tensor_tensor(QP[:, qq * L + tc * 512:qq * L + tc * 512 + 512],
                                    SC3[:], SC2[:], OP.add).then_inc(sV.h, 1)
                kb.v(vq2)
                sV.n += 1
        ck_qp = sV.n

        kvt_gate = (sT, 0)
        for tt in range(16):
            def tkv(h, tt=tt, c=ck_wq, cq=ck_qp, pv=sV.n, ps=sS.n):
                if tt == 0:
                    h.wait_ge(sV.h, cq)
                h.wait_ge(sV.h, pv)
                h.wait_ge(sS.h, ps)
                for k in range(8):
                    for j in range(2):
                        h.matmul(Q1[:, j * 512:(j + 1) * 512],
                                 XN[:, k * L + tt * 128:k * L + (tt + 1) * 128],
                                 WQ[:, k * 3072 + 1024 + j * 512:k * 3072 + 1024 + (j + 1) * 512],
                                 start=(k == 0), stop=(k == 7))
                        mm = h.matmul(Q2[:, j * 512:(j + 1) * 512],
                                      XN[:, k * L + tt * 128:k * L + (tt + 1) * 128],
                                      WQ[:, k * 3072 + 2048 + j * 512:k * 3072 + 2048 + (j + 1) * 512],
                                      start=(k == 0), stop=(k == 7))
                mm.then_inc(sT.h, 1)
            kb.t(tkv)
            sT.n += 1
            ck_kv = sT.n

            def vkf(h, c=ck_kv, kg=kvt_gate):
                h.wait_ge(sT.h, c)
                if kg[1]:
                    h.wait_ge(kg[0].h, kg[1])
                h.tensor_scalar(SC1[:], Q1[:, 0:512], 0.0, None, OP.min)
                h.tensor_scalar(KVT[:, 0:512], Q1[:, 0:512], 0.0, None, OP.max)
                h.tensor_copy(SC2[:, 0:512], SC1[:])
                h.tensor_scalar(SC1[:], Q1[:, 512:1024], 0.0, None, OP.min)
                h.tensor_scalar(KVT[:, 512:1024], Q1[:, 512:1024], 0.0, None,
                                OP.max)
                h.tensor_copy(SC3[:, 0:512], SC1[:]).then_inc(sV.h, 1)
            kb.v(vkf)
            sV.n += 1
            ck_kf = sV.n

            def skf(h, c=ck_kf):
                h.wait_ge(sV.h, c)
                h.activation(SC2[:], SC2[:], AF.Exp)
                h.activation(SC3[:], SC3[:], AF.Exp)
                h.copy(KVT[:, 1024:1536], Q2[:, 0:512])
                h.copy(KVT[:, 1536:2048], Q2[:, 512:1024]).then_inc(sS.h, 1)
            kb.s(skf)
            sS.n += 1
            ck_ks = sS.n

            def vka(h, c=ck_ks):
                h.wait_ge(sS.h, c)
                h.tensor_tensor(KVT[:, 0:512], KVT[:, 0:512], SC2[:], OP.add)
                h.tensor_tensor(KVT[:, 512:1024], KVT[:, 512:1024], SC3[:],
                                OP.add).then_inc(sV.h, 1)
            kb.v(vka)
            sV.n += 1
            ck_ka = sV.n

            def tac(h, tt=tt, c=ck_ka):
                h.wait_ge(sV.h, c)
                for hh in range(16):
                    pr = (hh % 2) * 64
                    h.matmul(Q3[pr:pr + 64, (hh // 2) * 64:(hh // 2 + 1) * 64],
                             KVT[:, hh * 64:(hh + 1) * 64],
                             KVT[:, 1024 + hh * 64:1024 + (hh + 1) * 64],
                             start=(tt == 0 and hh == 0),
                             stop=(tt == 15 and hh == 15),
                             skip_group_check=True)
                h.matmul(Q0[0:1, 0:512], kb.cocol[:], KVT[:, 0:512],
                         start=(tt == 0), stop=(tt == 15))
                h.matmul(Q0[0:1, 512:1024], kb.cocol[:], KVT[:, 512:1024],
                         start=(tt == 0), stop=(tt == 15)).then_inc(sT.h, 1)
            kb.t(tac)
            sT.n += 1
            kvt_gate = (sT, sT.n)
        ck_acc = sT.n

        def vbd(h, c=ck_acc):
            h.wait_ge(sT.h, c)
            h.memset(BD[:], 0.0)
            h.memset(KSEL[:], 0.0).then_inc(sV.h, 1)
        kb.v(vbd)
        sV.n += 1
        ck_bd0 = sV.n

        def sbd(h, c=ck_bd0):
            h.wait_ge(sV.h, c)
            for hh in range(16):
                pr = (hh % 2) * 64
                h.copy(BD[pr:pr + 64, (hh // 2) * 128 + pr:(hh // 2) * 128 + pr + 64],
                       Q3[pr:pr + 64, (hh // 2) * 64:(hh // 2 + 1) * 64])
            h.copy(KSR[:], Q0[0:1, 0:1024]).then_inc(sS.h, 1)
        kb.s(sbd)
        sS.n += 1
        ck_bd = sS.n

        def tks(h, c=ck_bd):
            h.wait_ge(sS.h, c)
            for k in range(8):
                mm = h.transpose(Q1[:, k:k + 1], KSR[0:1, k * 128:(k + 1) * 128],
                                 kb.cidf[0:1, 0:1])
            mm.then_inc(sT.h, 1)
        kb.t(tks)
        sT.n += 1
        ck_kt = sT.n

        def skt(h, c=ck_kt):
            h.wait_ge(sT.h, c)
            h.copy(KST[:], Q1[:, 0:8]).then_inc(sS.h, 1)
        kb.s(skt)
        sS.n += 1
        ck_kst = sS.n

        def vks(h, c=ck_kst):
            h.wait_ge(sS.h, c)
            for k in range(8):
                h.tensor_copy(KSEL[0:64, k * 16 + 2 * k:k * 16 + 2 * k + 1],
                              KST[0:64, k:k + 1])
                mm = h.tensor_copy(KSEL[64:128, k * 16 + 2 * k + 1:k * 16 + 2 * k + 2],
                                   KST[64:128, k:k + 1])
            mm.then_inc(sV.h, 1)
        kb.v(vks)
        sV.n += 1
        ck_ksel = sV.n

    with contextlib.ExitStack() as pcx:
        AO = pcx.enter_context(nc.sbuf_tensor("AO", [128, 8 * L], BF16))
        WP = pcx.enter_context(nc.sbuf_tensor("WP", [128, 8 * 1024], BF16))

        def gp(h, c=ck_acc, cv=sV.n, cs=sS.n):
            h.wait_ge(sT.h, c)
            h.wait_ge(sV.h, cv)
            h.wait_ge(sS.h, cs)
            for k in range(8):
                h.dma_start(out=WP[:, k * 1024:(k + 1) * 1024],
                            in_=di["wprj"][k * 128:(k + 1) * 128, :]).then_inc(dG.h, 16)
        kb.g(gp)
        dG.n += 8 * 16
        ck_wp = dG.n

        aslot = [(sV, 0), (sV, 0)]
        rslot = [(sS, 0), (sS, 0)]
        prev_nr = 0
        prev_tat = 0
        for tc in range(4):
            def tnm(h, tc=tc, c=ck_ksel, prevnrm=rslot[0]):
                h.wait_ge(sV.h, c)
                if prevnrm[1]:
                    h.wait_ge(prevnrm[0].h, prevnrm[1])
                for k in range(8):
                    mm = h.matmul(Q0[0:16, 0:512], KSEL[:, k * 16:(k + 1) * 16],
                                  QP[:, k * L + tc * 512:k * L + tc * 512 + 512],
                                  start=(k == 0), stop=(k == 7))
                mm.then_inc(sT.h, 1)
            kb.t(tnm)
            sT.n += 1
            ck_nm = sT.n

            def snm(h, c=ck_nm, pv=prev_nr):
                h.wait_ge(sT.h, c)
                if pv:
                    h.wait_ge(sV.h, pv)
                h.activation(NRM[:], Q0[0:16, 0:512], AF.Copy, bias=1e-6,
                             scale=1.0).then_inc(sS.h, 1)
            kb.s(snm)
            sS.n += 1
            ck_ne = sS.n
            rslot[0] = (sS, ck_ne)

            def vnm(h, c=ck_ne, pt=prev_tat):
                h.wait_ge(sS.h, c)
                if pt:
                    h.wait_ge(sT.h, pt)
                h.reciprocal(NRM[:], NRM[:])
                h.tensor_copy(NRB[:], NRM[:]).then_inc(sV.h, 1)
            kb.v(vnm)
            sV.n += 1
            ck_nr = sV.n
            prev_nr = ck_nr

            for qq in range(8):
                slot = qq % 2
                gate = aslot[slot]

                def tat(h, qq=qq, tc=tc, slot=slot, gate=gate, c=ck_nr):
                    h.wait_ge(sV.h, c)
                    if gate[1]:
                        h.wait_ge(gate[0].h, gate[1])
                    h.matmul(Q1[:, slot * 512:(slot + 1) * 512],
                             BD[:, qq * 128:(qq + 1) * 128],
                             QP[:, qq * L + tc * 512:qq * L + tc * 512 + 512],
                             start=True, stop=True)
                    h.matmul(Q2[:, slot * 512:(slot + 1) * 512],
                             kb.crep[:, qq * 128:(qq + 1) * 128], NRB[:],
                             start=True, stop=True).then_inc(sT.h, 1)
                kb.t(tat)
                sT.n += 1
                ck_at = sT.n

                def sat(h, slot=slot, c=ck_at):
                    h.wait_ge(sT.h, c)
                    h.copy(SC1[:], Q2[:, slot * 512:(slot + 1) * 512]).then_inc(sS.h, 1)
                kb.s(sat)
                sS.n += 1
                ck_rb = sS.n

                def vat(h, qq=qq, tc=tc, slot=slot, c=ck_rb):
                    h.wait_ge(sS.h, c)
                    h.tensor_tensor(AO[:, qq * L + tc * 512:qq * L + tc * 512 + 512],
                                    Q1[:, slot * 512:(slot + 1) * 512], SC1[:],
                                    OP.mult).then_inc(sV.h, 1)
                kb.v(vat)
                sV.n += 1
                aslot[slot] = (sV, sV.n)
            prev_tat = sT.n
        ck_ao = sV.n

        pslot = [(sS, 0), (sS, 0)]
        for dd in range(8):
            for tc in range(4):
                slot = (dd * 4 + tc) % 2
                gate = pslot[slot]

                def tpj(h, dd=dd, tc=tc, slot=slot, gate=gate, c=ck_ao, cw=ck_wp):
                    h.wait_ge(dG.h, cw)
                    h.wait_ge(sV.h, c)
                    if gate[1]:
                        h.wait_ge(gate[0].h, gate[1])
                    for k in range(8):
                        mm = h.matmul(Q3[:, slot * 512:(slot + 1) * 512],
                                      WP[:, k * 1024 + dd * 128:k * 1024 + (dd + 1) * 128],
                                      AO[:, k * L + tc * 512:k * L + tc * 512 + 512],
                                      start=(k == 0), stop=(k == 7))
                    mm.then_inc(sT.h, 1)
                kb.t(tpj)
                sT.n += 1
                ck_pj = sT.n

                def spj(h, dd=dd, tc=tc, slot=slot, c=ck_pj,
                        cp=getattr(kb, "x1_park_ck", 0)):
                    h.wait_ge(sT.h, c)
                    if cp:
                        h.wait_ge(kb.dX[3].h, cp)
                    h.tensor_scalar(S1[:, dd * L + tc * 512:dd * L + tc * 512 + 512],
                                    Q3[:, slot * 512:(slot + 1) * 512],
                                    kb.cbprj[:, dd:dd + 1], None,
                                    OP.add).then_inc(sV.h, 1)
                kb.v(spj)
                sV.n += 1
                pslot[slot] = (sV, sV.n)


def emit_ffn(kb, fcx, S1):
    nc, di = kb.nc, kb.di
    dG, dS, sT, sV, sS = kb.dG, kb.dS, kb.sT, kb.sV, kb.sS
    Q0, Q1, Q2, Q3 = kb.Q
    XN = kb.XN
    I8 = mybir.dt.int8
    sbl = lambda n, s, d: fcx.enter_context(nc.sbuf_tensor(n, s, d))
    TFC = sbl("TFC", [128, 8 * FH], I8)
    TPR = sbl("TPR", [128, 32 * 1024], I8)
    SCF = sbl("SCF", [128, 32], F32)
    SCP = sbl("SCP", [128, 8], F32)
    UTS = sbl("UTS", [16, FH], BF16)
    FBT = sbl("FBT", [16, 16], BF16)
    VTB = sbl("VTB", [128, 128], BF16)
    GCL = sbl("GCL", [16, 1], F32)

    with contextlib.ExitStack() as pcx:
        psb = lambda n, s, d: pcx.enter_context(nc.sbuf_tensor(n, s, d))
        WFB = psb("WFB", [128, FH], BF16)
        WFA = psb("WFA", [128, FH], BF16)
        SROW = psb("SROW", [1, FH + 1024], F32)
        HBC = psb("HBC", [128, 512], BF16)
        NHB = psb("NHB", [128, 512], BF16)
        HB2 = psb("HB2", [128, 1024], BF16)
        NH2 = psb("NH2", [128, 1024], BF16)
        UTB = psb("UTB", [16, FH], BF16)
        RSC = psb("RSC", [1, FH], BF16)
        WFAi = WFA[:, :].bitcast(I8)

        v0, s0 = sV.n, sS.n

        def g0(h, v0=v0, s0=s0):
            h.wait_ge(sV.h, v0)
            h.wait_ge(sS.h, s0)
            src = bass.AP(di["vt"].tensor, 0, [[16, 128], [16 * 128, 8], [1, 16]])
            h.dma_start(out=VTB[:], in_=src).then_inc(dG.h, 16)
            h.dma_start(out=FBT[:], in_=di["fbt"]).then_inc(dG.h, 16)
            h.dma_start(out=UTB[:], in_=di["ut"]).then_inc(dG.h, 16)
        kb.g(g0)
        dG.n += 48
        ck_small = dG.n

        gsb = psb("GSB", [1, 1], F32)

        def g0b(h):
            h.dma_start(out=gsb[:], in_=di["gsc"]).then_inc(dG.h, 16)
        kb.g(g0b)
        dG.n += 16
        ck_gs = dG.n

        def tg(h, c=ck_gs):
            h.wait_ge(dG.h, c)
            h.matmul(Q0[0:16, 0:1], kb.co1f[0:1, 0:16], gsb[:], start=True,
                     stop=True).then_inc(sT.h, 1)
        kb.t(tg)
        sT.n += 1
        ck_tg = sT.n

        def sg(h, c=ck_tg):
            h.wait_ge(sT.h, c)
            h.copy(GCL[:], Q0[0:16, 0:1]).then_inc(sS.h, 1)
        kb.s(sg)
        sS.n += 1

        wfb_gate = (sV, 0)
        for hh in range(8):
            def gfc(h, hh=hh, g=wfb_gate):
                if g[1]:
                    h.wait_ge(g[0].h, g[1])
                src = bass.AP(di["wfc"].tensor, hh * 512,
                              [[FH, 128], [128 * FH, 8], [1, 512]])
                h.dma_start(out=WFB[:], in_=src).then_inc(dG.h, 16)
            kb.g(gfc)
            dG.n += 16
            ck_w = dG.n

            def vab(h, c=ck_w):
                h.wait_ge(dG.h, c)
                h.activation(WFA[:], WFB[:], AF.Abs).then_inc(sS.h, 1)
            kb.s(vab)
            sS.n += 1
            ck_ab = sS.n

            def tcs2(h, hh=hh, c=ck_ab, pv=sV.n):
                h.wait_ge(sS.h, c)
                h.wait_ge(sV.h, pv)
                for k in range(8):
                    h.matmul(Q1[0:1, 0:512], kb.cocol[:],
                             WFA[:, k * 512:(k + 1) * 512],
                             start=(k == 0), stop=(k == 7))
                for hs in range(4):
                    for k in range(8):
                        mm = h.matmul(Q1[:, 512 + hs:512 + hs + 1],
                                      WFA[:, k * 512 + hs * 128:k * 512 + hs * 128 + 128],
                                      kb.cocol[:], start=(k == 0 and hs == 0),
                                      stop=(k == 7 and hs == 3))
                mm.then_inc(sT.h, 1)
            kb.t(tcs2)
            sT.n += 1
            ck_cs = sT.n

            def vsc(h, hh=hh, c=ck_cs):
                h.wait_ge(sT.h, c)
                h.tensor_scalar(SROW[0:1, hh * 512:(hh + 1) * 512], Q1[0:1, 0:512],
                                1.0 / D, 1e-5, OP.mult, OP.max)
                h.tensor_scalar(SCF[:, hh * 4:(hh + 1) * 4], Q1[:, 512:516],
                                1.0 / D, 1e-5, OP.mult, OP.max).then_inc(sV.h, 1)
            kb.v(vsc)
            sV.n += 1
            ck_sc = sV.n

            def tbc2(h, hh=hh, c=ck_sc):
                h.wait_ge(sV.h, c)
                h.matmul(Q2[:, 0:512], kb.co1f[:],
                         SROW[0:1, hh * 512:(hh + 1) * 512], start=True,
                         stop=True).then_inc(sT.h, 1)
            kb.t(tbc2)
            sT.n += 1
            ck_b = sT.n

            def vhb(h, c=ck_b):
                h.wait_ge(sT.h, c)
                h.tensor_scalar(HBC[:], Q2[:, 0:512], 0.5, None, OP.mult)
                h.tensor_scalar(NHB[:], Q2[:, 0:512], -0.5, None, OP.mult).then_inc(sV.h, 1)
            kb.v(vhb)
            sV.n += 1
            ck_hb = sV.n

            def vtn(h, hh=hh, c=ck_hb):
                rep = HBC[:, :].unsqueeze(1).broadcast_to([128, 8, 512])
                nrep = NHB[:, :].unsqueeze(1).broadcast_to([128, 8, 512])
                w3 = WFB[:, :].rearrange("p (r f) -> p r f", r=8)
                tdst = bass.AP(TFC, hh * 512, [[8 * FH, 128], [FH, 8], [1, 512]])
                h.tensor_tensor(tdst, w3, rep, OP.is_ge)
                h.tensor_tensor(WFAi[:, 0:FH].rearrange("p (r f) -> p r f", r=8),
                                w3, nrep, OP.is_le)
                h.tensor_tensor(tdst, tdst,
                                WFAi[:, 0:FH].rearrange("p (r f) -> p r f", r=8),
                                OP.subtract).then_inc(sV.h, 1)
            kb.v(vtn)
            sV.n += 1
            wfb_gate = (sV, sV.n)
        ck_tfc = sV.n

        def vrs(h, c=ck_tfc):
            with nc.allow_low_precision(reason="intentional bf16 1/scale for u-path"):
                h.reciprocal(RSC[:], SROW[0:1, 0:FH]).then_inc(sV.h, 1)
        kb.v(vrs)
        sV.n += 1
        ck_rs = sV.n

        ub_gate = 0
        for j in range(8):
            def tub(h, j=j, c=ck_rs, g=ub_gate):
                h.wait_ge(sV.h, c)
                if g:
                    h.wait_ge(sV.h, g)
                h.matmul(Q2[0:16, 0:512], kb.co1b[0:1, 0:16],
                         RSC[0:1, j * 512:(j + 1) * 512], start=True,
                         stop=True).then_inc(sT.h, 1)
            kb.t(tub)
            sT.n += 1
            ck_u = sT.n

            def vub(h, j=j, c=ck_u, cs=ck_gs):
                h.wait_ge(sT.h, c)
                h.wait_ge(dG.h, cs)
                h.tensor_tensor(UTS[:, j * 512:(j + 1) * 512],
                                UTB[:, j * 512:(j + 1) * 512], Q2[0:16, 0:512],
                                OP.mult).then_inc(sV.h, 1)
            kb.v(vub)
            sV.n += 1
            ub_gate = sV.n
        ck_uts = sV.n

        pa_ab = [wfb_gate, wfb_gate]
        pa_mm = [(sT, 0), (sT, 0)]
        for k in range(32):
            ws = k % 2

            def gpa(h, k=k, ws=ws, g=pa_ab[ws]):
                if g[1]:
                    h.wait_ge(g[0].h, g[1])
                src = bass.AP(di["wpr"].tensor, k * 128 * 1024,
                              [[1024, 128], [1, 1024]])
                h.dma_start(out=WFB[:, ws * 1024:ws * 1024 + 1024],
                            in_=src).then_inc(dG.h, 16)
            kb.g(gpa)
            dG.n += 16
            ck_w = dG.n

            def vab2(h, k=k, ws=ws, c=ck_w, g=pa_mm[ws]):
                h.wait_ge(dG.h, c)
                if g[1]:
                    h.wait_ge(g[0].h, g[1])
                h.activation(WFA[:, ws * 1024:ws * 1024 + 1024],
                             WFB[:, ws * 1024:ws * 1024 + 1024],
                             AF.Abs).then_inc(sS.h, 1)
            kb.s(vab2)
            sS.n += 1
            ck_ab = sS.n
            pa_ab[ws] = (sS, ck_ab)

            def tca(h, k=k, ws=ws, c=ck_ab):
                h.wait_ge(sS.h, c)
                h.matmul(Q3[0:1, 0:512], kb.cocol[:],
                         WFA[:, ws * 1024:ws * 1024 + 512],
                         start=(k == 0), stop=(k == 31))
                h.matmul(Q3[0:1, 512:1024], kb.cocol[:],
                         WFA[:, ws * 1024 + 512:ws * 1024 + 1024],
                         start=(k == 0), stop=(k == 31))
                for dd in range(8):
                    mm = h.matmul(Q2[:, 512 + dd:512 + dd + 1],
                                  WFA[:, ws * 1024 + dd * 128:ws * 1024 + (dd + 1) * 128],
                                  kb.cocol[:],
                                  start=(k == 0 and dd == 0),
                                  stop=(k == 31 and dd == 7))
                mm.then_inc(sT.h, 1)
            kb.t(tca)
            sT.n += 1
            pa_mm[ws] = (sT, sT.n)
        ck_pa = sT.n

        def vsp2(h, c=ck_pa):
            h.wait_ge(sT.h, c)
            h.tensor_scalar(SROW[0:1, FH:FH + 1024],
                            Q3[0:1, 0:1024], 1.0 / FH, 1e-5, OP.mult, OP.max)
            h.tensor_scalar(SCP[:], Q2[:, 512:520], 1.0 / FH, 1e-5, OP.mult,
                            OP.max).then_inc(sV.h, 1)
        kb.v(vsp2)
        sV.n += 1
        ck_sp = sV.n

        def tb2(h, c=ck_sp):
            h.wait_ge(sV.h, c)
            h.matmul(Q2[:, 0:512], kb.co1f[:], SROW[0:1, FH:FH + 512],
                     start=True, stop=True)
            h.matmul(Q2[:, 512:1024], kb.co1f[:], SROW[0:1, FH + 512:FH + 1024],
                     start=True, stop=True).then_inc(sT.h, 1)
        kb.t(tb2)
        sT.n += 1
        ck_b2 = sT.n

        def vh2(h, c=ck_b2):
            h.wait_ge(sT.h, c)
            h.tensor_scalar(HB2[:], Q2[:, :], 0.5, None, OP.mult)
            h.tensor_scalar(NH2[:], Q2[:, :], -0.5, None, OP.mult).then_inc(sV.h, 1)
        kb.v(vh2)
        sV.n += 1
        ck_h2 = sV.n

        pb_tn = list(pa_ab)
        for k in range(32):
            ws = k % 2

            def gpb(h, k=k, ws=ws, g=pb_tn[ws]):
                if g[1]:
                    h.wait_ge(g[0].h, g[1])
                src = bass.AP(di["wpr"].tensor, k * 128 * 1024,
                              [[1024, 128], [1, 1024]])
                h.dma_start(out=WFB[:, ws * 1024:ws * 1024 + 1024],
                            in_=src).then_inc(dG.h, 16)
            kb.g(gpb)
            dG.n += 16
            ck_w = dG.n

            def vtn2(h, k=k, ws=ws, c=ck_w, ch=ck_h2):
                h.wait_ge(dG.h, c)
                h.wait_ge(sV.h, ch)
                tdst = TPR[:, k * 1024:(k + 1) * 1024]
                h.tensor_tensor(tdst, WFB[:, ws * 1024:ws * 1024 + 1024],
                                HB2[:], OP.is_ge)
                h.tensor_tensor(WFAi[:, ws * 1024:ws * 1024 + 1024],
                                WFB[:, ws * 1024:ws * 1024 + 1024],
                                NH2[:], OP.is_le)
                h.tensor_tensor(tdst, tdst,
                                WFAi[:, ws * 1024:ws * 1024 + 1024],
                                OP.subtract).then_inc(sV.h, 1)
            kb.v(vtn2)
            sV.n += 1
            pb_tn[ws] = (sV, sV.n)
        ck_tpr = sV.n
        kb.barrier()

    with contextlib.ExitStack() as qcx:
        qsb = lambda n, s, d: qcx.enter_context(nc.sbuf_tensor(n, s, d))
        RR = qsb("RR", [128, 32 * 512], BF16)
        WFE = qsb("WFE", [128, 2 * 2048], BF16)
        WPE = qsb("WPE", [128, 2 * 1024], BF16)
        T1S = qsb("T1S", [16, 512], BF16)
        T2G = qsb("T2G", [16, 512], BF16)
        G1T = qsb("G1T", [128, 512], BF16)
        X2O = qsb("X2O", [128, 512], F32)
        X1L = qsb("X1L", [128, 512], BF16)

        wfe_cons = [(sT, 0), (sT, 0)]
        wpe_mm = [(sT, 0), (sT, 0)]
        x2_gate = (dS, 0)
        x1_gate = (sV, 0)
        for q in range(4):
            qt0 = q * 512
            def tt1(h, qt0=qt0, pv=sV.n, ps=sS.n):
                h.wait_ge(sV.h, pv)
                h.wait_ge(sS.h, ps)
                for k in range(8):
                    mm = h.matmul(Q1[0:16, 0:512], VTB[:, k * 16:(k + 1) * 16],
                                  XN[:, k * L + qt0:k * L + qt0 + 512],
                                  start=(k == 0), stop=(k == 7))
                mm.then_inc(sT.h, 1)
            kb.t(tt1)
            sT.n += 1
            ck_t1 = sT.n

            def st1(h, c=ck_t1):
                h.wait_ge(sT.h, c)
                h.copy(T1S[:], Q1[0:16, 0:512]).then_inc(sS.h, 1)
            kb.s(st1)
            sS.n += 1
            ck_t1s = sS.n

            def tt2(h, c=ck_t1s):
                h.wait_ge(sS.h, c)
                h.matmul(Q1[0:16, 512:1024], FBT[:], T1S[:], start=True,
                         stop=True).then_inc(sT.h, 1)
            kb.t(tt2)
            sT.n += 1
            ck_t2 = sT.n

            def vt2(h, c=ck_t2):
                h.wait_ge(sT.h, c)
                h.tensor_scalar(T2G[:], Q1[0:16, 512:1024], GCL[:], None,
                                OP.mult).then_inc(sV.h, 1)
            kb.v(vt2)
            sV.n += 1
            ck_t2g = sV.n

            q0f = [(sS, 0), (sS, 0)]
            for hh in range(8):
                for hf in range(2):
                    wslot = (hh * 2 + hf) % 2

                    def vcf(h, hh=hh, hf=hf, wslot=wslot, g=wfe_cons[wslot]):
                        if g[1]:
                            h.wait_ge(g[0].h, g[1])
                        src = bass.AP(TFC, hh * 512 + hf * 256,
                                      [[8 * FH, 128], [FH, 8], [1, 256]])
                        h.tensor_copy(
                            WFE[:, wslot * 2048:(wslot + 1) * 2048].rearrange(
                                "p (r f) -> p r f", r=8), src).then_inc(sV.h, 1)
                    kb.v(vcf)
                    sV.n += 1
                    ck_cast = sV.n

                    for hs2 in range(2):
                        hs = hf * 2 + hs2
                        slot = hs % 2
                        gate = q0f[slot]

                        def tf(h, hh=hh, hs=hs, hs2=hs2, slot=slot, qt0=qt0,
                               wslot=wslot, gate=gate, c=ck_cast, ct=ck_t2g):
                            h.wait_ge(sV.h, c)
                            h.wait_ge(sV.h, ct)
                            if gate[1]:
                                h.wait_ge(gate[0].h, gate[1])
                            h.matmul(Q0[:, slot * 512:(slot + 1) * 512],
                                     UTS[:, hh * 512 + hs * 128:hh * 512 + (hs + 1) * 128],
                                     T2G[:], start=True, stop=False)
                            for k in range(8):
                                mm = h.matmul(Q0[:, slot * 512:(slot + 1) * 512],
                                              WFE[:, wslot * 2048 + k * 256 + hs2 * 128:wslot * 2048 + k * 256 + hs2 * 128 + 128],
                                              XN[:, k * L + qt0:k * L + qt0 + 512],
                                              start=False, stop=(k == 7))
                            mm.then_inc(sT.h, 1)
                        kb.t(tf)
                        sT.n += 1
                        ck_f = sT.n

                        def sf(h, hh=hh, hs=hs, slot=slot, c=ck_f):
                            h.wait_ge(sT.h, c)
                            j = hh * 4 + hs
                            h.activation(RR[:, j * 512:(j + 1) * 512],
                                         Q0[:, slot * 512:(slot + 1) * 512],
                                         AF.Relu,
                                         scale=SCF[:, j:j + 1]).then_inc(sS.h, 1)
                        kb.s(sf)
                        sS.n += 1
                        q0f[slot] = (sS, sS.n)
                    wfe_cons[wslot] = (sT, sT.n)
            ck_fc = sS.n

            def vsq2(h, c=ck_fc):
                h.wait_ge(sS.h, c)
                for j in range(8):
                    mm = h.tensor_tensor(RR[:, j * 2048:(j + 1) * 2048],
                                         RR[:, j * 2048:(j + 1) * 2048],
                                         RR[:, j * 2048:(j + 1) * 2048], OP.mult)
                mm.then_inc(sV.h, 1)
            kb.v(vsq2)
            sV.n += 1
            ck_rr = sV.n

            for k in range(32):
                wslot = k % 2

                def vcp(h, k=k, wslot=wslot, g=wpe_mm[wslot]):
                    if g[1]:
                        h.wait_ge(g[0].h, g[1])
                    h.tensor_copy(WPE[:, wslot * 1024:(wslot + 1) * 1024],
                                  TPR[:, k * 1024:(k + 1) * 1024]).then_inc(sV.h, 1)
                kb.v(vcp)
                sV.n += 1
                ck_c = sV.n

                def tpj2(h, k=k, wslot=wslot, c=ck_c, cr=ck_rr,
                         cq0=q0f[0], cq1=q0f[1]):
                    h.wait_ge(sV.h, c)
                    h.wait_ge(sV.h, cr)
                    if k == 0:
                        h.wait_ge(cq0[0].h, cq0[1])
                        h.wait_ge(cq1[0].h, cq1[1])
                    for dd in range(8):
                        mm = h.matmul(kb.Q[dd // 2][:, (dd % 2) * 512:(dd % 2) * 512 + 512],
                                      WPE[:, wslot * 1024 + dd * 128:wslot * 1024 + (dd + 1) * 128],
                                      RR[:, k * 512:(k + 1) * 512],
                                      start=(k == 0), stop=(k == 31))
                    mm.then_inc(sT.h, 1)
                kb.t(tpj2)
                sT.n += 1
                wpe_mm[wslot] = (sT, sT.n)
            ck_pj = sT.n

            g1_gate = (sV, 0)
            for dd in range(8):
                def sg1(h, dd=dd, c=ck_pj, g=g1_gate):
                    h.wait_ge(sT.h, c)
                    if g[1]:
                        h.wait_ge(g[0].h, g[1])
                    h.activation(G1T[:], kb.Q[dd // 2][:, (dd % 2) * 512:(dd % 2) * 512 + 512],
                                 AF.Silu, scale=SCP[:, dd:dd + 1]).then_inc(sS.h, 1)
                kb.s(sg1)
                sS.n += 1
                ck_g1 = sS.n

                def gx1(h, dd=dd, q=q, g=x1_gate):
                    if g[1]:
                        h.wait_ge(g[0].h, g[1])
                    h.dma_start(out=X1L[:],
                                in_=kb.dout[dd * 128:(dd + 1) * 128,
                                            q * 512:(q + 1) * 512]).then_inc(dG.h, 16)
                kb.g(gx1)
                dG.n += 16
                ck_x1 = dG.n

                def vx2(h, dd=dd, q=q, c=ck_g1, cx=ck_x1, g=x2_gate):
                    h.wait_ge(sS.h, c)
                    h.wait_ge(dG.h, cx)
                    if g[1]:
                        h.wait_ge(g[0].h, g[1])
                    h.tensor_tensor(G1T[:], kb.SM[:, dd * L + q * 512:dd * L + q * 512 + 512],
                                    G1T[:], OP.mult)
                    h.tensor_tensor(X2O[:], X1L[:], G1T[:],
                                    OP.add).then_inc(sV.h, 1)
                kb.v(vx2)
                sV.n += 1
                ck_x2 = sV.n
                x1_gate = (sV, ck_x2)
                g1_gate = (sV, ck_x2)

                def yx2(h, dd=dd, q=q, c=ck_x2):
                    h.wait_ge(sV.h, c)
                    h.dma_start(out=kb.dout[dd * 128:(dd + 1) * 128,
                                            q * 512:(q + 1) * 512],
                                in_=X2O[:]).then_inc(dS.h, 16)
                kb.y(yx2)
                dS.n += 16
                x2_gate = (dS, dS.n)



def build_core_program(taps=()):
    nc = bass.Bass("TRN2", target_bir_lowering=False, debug=False,
                   detect_race_conditions=False)
    di = {}

    def dinp(name, shape, dt=F32):
        di[name] = nc.dram_tensor(name, list(shape), dt, kind="ExternalInput").ap()

    dinp("x0", (D, L))
    dinp("win", (D, DIP))
    dinp("wout", (DI, D))
    dinp("convw", (128, NCT * 4))
    dinp("convb", (128, NCT))
    dinp("dtb", (1, NH))
    dinp("alog", (NH, 1))
    dinp("dvec", (1, NH))
    dinp("normw", (128, 16))
    dinp("lnw0", (128, 8))
    dinp("lnb0", (128, 8))
    dinp("routw", (D, NEXP))
    dinp("w1", (NEXP * 16 * 128, 8 * 128))
    dinp("w2", (NEXP * 8 * 128, 16 * 128))
    dinp("wqkv", (D, 3 * D))
    dinp("wprj", (D, D))
    dinp("bprj", (128, 8))
    dinp("wfc", (D, FH))
    dinp("wpr", (FH, D))
    dinp("ut", (16, FH))
    dinp("vt", (D, 16))
    dinp("fbt", (16, 16))
    dinp("gsc", (1, 1))
    dinp("id16", (128, 128), BF16)
    dinp("idf", (128, 128), F32)
    dinp("l1tri", (128, 128), BF16)
    dinp("sub4", (128, 512), BF16)
    dinp("repsel", (16, 1024), BF16)
    dinp("ones1f", (1, 128), F32)
    dinp("ones1b", (1, 128), BF16)
    dinp("onesr512", (1, 512), F32)
    dinp("onescol", (128, 1), BF16)

    dout = nc.dram_tensor("out", [D, L], F32, kind="ExternalOutput").ap()
    tap_d = {}
    for tname, tshape, tdt in taps:
        tap_d[tname] = nc.dram_tensor("tap_" + tname, list(tshape), tdt,
                                      kind="ExternalOutput").ap()

    P = Prog()
    with contextlib.ExitStack() as ctx:
        kb = KB(nc, ctx, P, di, dout, tap_d)
        emit_consts(kb)

        with contextlib.ExitStack() as lnc:
            emit_ln_fm(kb, lnc, kb.XN, src_dram=di["x0"])
            kb.barrier()
        kb.tap_fm("xn", kb.XN)

        with contextlib.ExitStack() as mc:
            emit_mamba(kb, mc)
            kb.barrier()
        kb.tap_fm("sm", kb.SM)

        with contextlib.ExitStack() as mcx:
            emit_moe(kb, mcx)
            kb.barrier()
        kb.tap_fm("x1", kb.SM)

        with contextlib.ExitStack() as lnc2:
            emit_ln_fm(kb, lnc2, kb.XN, src_sbuf=kb.SM)
            kb.barrier()
        kb.tap_fm("xn1", kb.XN)

        with contextlib.ExitStack() as acx:
            emit_attn(kb, acx, kb.SM)
            kb.barrier()
        kb.tap_fm("s1", kb.SM)

        with contextlib.ExitStack() as fcx:
            emit_ffn(kb, fcx, kb.SM)

        fin = kb.dS.n

        def wfin(h, f=fin, dS=kb.dS):
            h.wait_ge(dS.h, f)
        kb.y(wfin)

        block = ctx.enter_context(nc.Block())
        P.flush(block)
    return nc


def _consts_host():
    i = np.arange(128)
    l1 = (i[:, None] <= i[None, :]).astype(BF)
    sub = np.zeros((128, 512), np.float32)
    for rep in range(4):
        for t in range(128):
            if t + 1 < 128:
                sub[t + 1, rep * 128 + t] = NEGBIG
    repsel = np.zeros((16, 1024), np.float32)
    for qq in range(8):
        for j in range(128):
            repsel[2 * qq + j // 64, qq * 128 + j] = 1.0
    return {
        "id16": np.eye(128, dtype=BF),
        "idf": np.eye(128, dtype=np.float32),
        "l1tri": l1,
        "sub4": sub.astype(BF),
        "repsel": repsel.astype(BF),
        "ones1f": np.ones((1, 128), np.float32),
        "ones1b": np.ones((1, 128), BF),
        "onesr512": np.ones((1, 512), np.float32),
        "onescol": np.ones((128, 1), BF),
    }


def _prep_weights(kw):
    f32 = np.float32
    c = lambda a: np.ascontiguousarray(a, dtype=f32)
    d = {}
    d["win"] = c(np.asarray(kw["L0_attn_in_proj_w"]).T)
    d["wout"] = c(np.asarray(kw["L0_attn_out_proj_w"]).T)
    cw = np.asarray(kw["L0_attn_conv_w"], f32)
    d["convw"] = c(cw.reshape(NCT, 128, 4).transpose(1, 0, 2).reshape(128, NCT * 4))
    cb = np.asarray(kw["L0_attn_conv_b"], f32)
    d["convb"] = c(cb.reshape(NCT, 128).T)
    d["dtb"] = c(np.asarray(kw["L0_attn_dt_bias"]).reshape(1, NH))
    d["alog"] = c(np.asarray(kw["L0_attn_A_log"]).reshape(NH, 1))
    d["dvec"] = c(np.asarray(kw["L0_attn_D"]).reshape(1, NH))
    d["normw"] = c(np.asarray(kw["L0_attn_norm_w"]).reshape(16, 128).T)
    d["lnw0"] = c(np.asarray(kw["L0_attn_ln_w"]).reshape(8, 128).T)
    d["lnb0"] = c(np.asarray(kw["L0_attn_ln_b"]).reshape(8, 128).T)
    d["routw"] = c(np.asarray(kw["L0_ffn_router_w"]).T)
    w1 = np.asarray(kw["L0_ffn_w1"], f32)
    d["w1"] = c(w1.reshape(8, 8, 128, 16, 128).transpose(0, 3, 2, 1, 4)
                .reshape(8 * 16 * 128, 8 * 128))
    w2 = np.asarray(kw["L0_ffn_w2"], f32)
    d["w2"] = c(w2.reshape(8, 16, 128, 8, 128).transpose(0, 3, 2, 1, 4)
                .reshape(8 * 8 * 128, 16 * 128))
    d["wqkv"] = c(np.asarray(kw["L1_attn_Wqkv"]).T)
    d["wprj"] = c(np.asarray(kw["L1_attn_Wproj"]).T)
    d["bprj"] = c(np.asarray(kw["L1_attn_bproj"]).reshape(8, 128).T)
    d["wfc"] = c(np.asarray(kw["L1_ffn_w_fc"]).T)
    d["wpr"] = c(np.asarray(kw["L1_ffn_w_proj"]).T)
    d["ut"] = c(np.asarray(kw["L1_ffn_u"]).T)
    d["vt"] = c(np.asarray(kw["L1_ffn_v"]).T)
    d["fbt"] = c(np.asarray(kw["L1_ffn_fast_b"]).T)
    d["gsc"] = c(np.asarray(kw["L1_ffn_gate"]).reshape(1, 1))
    d.update(_consts_host())
    return d


def kernel(**inputs):
    x = np.asarray(inputs["L0_attn_x"], np.float32)
    shared = _prep_weights(inputs)
    nc = build_core_program()
    in_maps = []
    for b in range(8):
        m = dict(shared)
        m["x0"] = np.ascontiguousarray(x[b].T)
        in_maps.append(m)
    res = run_bass_kernel_spmd(nc, in_maps, core_ids=list(range(8)))
    out = np.empty((8, L, D), np.float32)
    for b in range(8):
        out[b] = res.results[b]["out"].T
    return out
